# version 05
# baseline (speedup 1.0000x reference)
import numpy as np
import concourse.bass as bass
import concourse.mybir as mybir
from concourse.bass_utils import run_bass_kernel_spmd

F32 = mybir.dt.float32
F16 = mybir.dt.float16
BF16 = mybir.dt.bfloat16
I32 = mybir.dt.int32
AF = mybir.ActivationFunctionType
ALU = mybir.AluOpType
AX = mybir.AxisListType

L = 8192
D = 1024
DI = 2048
NS = 16
CL = 1024
CB = 8
T = 512
K1 = L // T
TT4 = T // 128
TOK2 = 4096
K2 = TOK2 // T
MAGIC = 777000
EPS_LN = 1e-5
EPS_Q = 1e-8
INV2048 = 1.0 / 2048.0
INVMEAN = 1.0 / (1024.0 * 1024.0)

R_PROJ = lambda k: k
R_STAT = lambda k: K1 + k
R_YG = lambda k: 2 * K1 + k


def build_nc(dbg=False, race_detect=False):
    nc = bass.Bass("TRN2", target_bir_lowering=False, detect_race_conditions=race_detect)

    x_in = nc.dram_tensor("x_in", [L, D], F32, kind="ExternalInput").ap()
    wipT_in = nc.dram_tensor("wipT_in", [D, 2 * CL], F32, kind="ExternalInput").ap()
    xpwT_in = nc.dram_tensor("xpwT_in", [128, CB * 33], F32, kind="ExternalInput").ap()
    alog_in = nc.dram_tensor("alog_in", [128, CB * NS], F32, kind="ExternalInput").ap()
    cw_in = nc.dram_tensor("cw_in", [128, CB * 4], F32, kind="ExternalInput").ap()
    cb_in = nc.dram_tensor("cb_in", [128, CB], F32, kind="ExternalInput").ap()
    dp_in = nc.dram_tensor("dp_in", [128, CB], F32, kind="ExternalInput").ap()
    dtw_in = nc.dram_tensor("dtw_in", [128, CB], F32, kind="ExternalInput").ap()
    dtb_in = nc.dram_tensor("dtb_in", [128, CB], F32, kind="ExternalInput").ap()
    lng_in = nc.dram_tensor("lng_in", [128, CB], F32, kind="ExternalInput").ap()
    lnb_in = nc.dram_tensor("lnb_in", [128, CB], F32, kind="ExternalInput").ap()
    owT_in = nc.dram_tensor("owT_in", [DI, D], F32, kind="ExternalInput").ap()
    ffnw_in = nc.dram_tensor("ffnw_in", [D, D], F32, kind="ExternalInput").ap()
    o_out = nc.dram_tensor("o_out", [TOK2, D], F32, kind="ExternalOutput").ap()

    dbg_o = {}
    if dbg:
        for nm, shp, dt_ in [("dbg_y", [128, CB * T], F32), ("dbg_proj", [33, T], F32),
                             ("dbg_dt", [128, CB * T], F32), ("dbg_xb", [128, CB * T], F16),
                             ("dbg_u", [128, CB * T], F16), ("dbg_sz", [128, CB * T], F16),
                             ("dbg_yg", [128, CB * T], F16), ("dbg_g", [128, CB * T], F32),
                             ("dbg_sg", [128, CB * T], F32), ("dbg_aq", [128, CB * T], F16),
                             ("dbg_h", [128, CB * T], F32), ("dbg_st", [2, T], F32)]:
            dbg_o[nm] = nc.dram_tensor(nm, shp, dt_, kind="ExternalOutput").ap()

    sh_proj = nc.dram_tensor("sh_proj", [2, K1 * 33, T], F32, addr_space="Shared")
    sh_st = nc.dram_tensor("sh_st", [2, K1 * 2, T], F32, addr_space="Shared")
    sh_yg = nc.dram_tensor("sh_yg", [2, 128, CB * L], F16, addr_space="Shared")
    sh_flag = nc.dram_tensor("sh_flag", [2, 64], I32, addr_space="Shared")
    d_bc = nc.dram_tensor("d_bc", [32, T], F16)
    d_dtin = nc.dram_tensor("d_dtin", [1, T], F32)
    d_rows = nc.dram_tensor("d_rows", [2, T], F32)
    d_asc = nc.dram_tensor("d_asc", [128, 4], F32)
    d_sc = nc.dram_tensor("d_sc", [1, T], F32)

    import contextlib
    est = contextlib.ExitStack()
    with est:
        sb = lambda name, shape, dt_: est.enter_context(nc.sbuf_tensor(name, shape, dt_))
        sem = lambda name: est.enter_context(nc.semaphore(name))

        state = sb("state", [128, 128], F32)
        flagv = sb("flagv", [1, 64], I32)
        Av = sb("Av", [128, CB * NS], F32)
        cwv = sb("cwv", [128, CB * 4], F32)
        cbv = sb("cbv", [128, CB], F32)
        Dv = sb("Dv", [128, CB], F32)
        dtwv = sb("dtwv", [128, CB], F32)
        dtbv = sb("dtbv", [128, CB], F32)
        lngv = sb("lngv", [128, CB], F32)
        lnbv = sb("lnbv", [128, CB], F32)
        idbf = sb("idbf", [128, 128], BF16)
        id16 = sb("id16", [128, 128], F16)
        idf32 = sb("idf32", [128, 128], F32)
        ones16c = sb("ones16c", [128, 1], F16)
        onesbfc = sb("onesbfc", [128, 1], BF16)
        onesf32c = sb("onesf32c", [128, 1], F32)
        onesf32r = sb("onesf32r", [1, 128], F32)
        iotai = sb("iotai", [128, 128], I32)
        cbias = sb("cbias", [128, 4], F32)
        wsc = sb("wsc", [1, 1], F32)
        wscpre = sb("wscpre", [1, 1], F32)
        wrecv = sb("wrecv", [128, 1], F32)
        wscv = sb("wscv", [128, 1], F32)

        names = ("s_x0 s_x1 s_w0 s_w1 s_wp s_bcd s_bcb s_bcc s_bcr s_sg s_pw s_pr s_sw s_sr s_yw0 s_yw1 s_yr s_fl s_o0 s_o1 s_x2 s_w20 s_w21 s_a2 s_b2 "
                 "v_st a_rs a_rs1 v_xn p_tp v_te p_ip v_cv a_xb a_z p_xp v_pv v_pj v_bc "
                 "v_t1 a_sp v_dt v_u a_dA o_xs v_ts v_ym p_ya v_ye v_yf a_y2 p_st v_se "
                 "v_mu a_rw a_rw1 v_nm v_yg su su2 "
                 "v2x v2a v2q p2t v2e p2f v2g a2g p2o v2s p2g v2o a2r wq_v wq_p wq_a c_v c_a c_v2").split()
        S = {n: sem(n) for n in names}
        g = lambda n: S[n]

        IDX = lambda k, cb, n: 128 * k + 16 * cb + n
        BC_CNT = lambda k, j: 16 * (7 * k + j)
        SG_CNT = lambda k, j: 16 * (4 * k + j)

        st1 = contextlib.ExitStack()
        with st1:
            wipT16 = st1.enter_context(nc.sbuf_tensor("wipT16", [128, CB * 2 * CL], F16))
            xpwT16 = st1.enter_context(nc.sbuf_tensor("xpwT16", [128, CB * 33], F16))
            x_tm = st1.enter_context(nc.sbuf_tensor("x_tm", [128, 2 * 1024], F32))
            bn6 = st1.enter_context(nc.sbuf_tensor("bn6", [128, 12], F32))
            mv = st1.enter_context(nc.sbuf_tensor("mv", [128, 4], F32))
            rstd1 = st1.enter_context(nc.sbuf_tensor("rstd1", [128, 2], F32))
            lnt = st1.enter_context(nc.sbuf_tensor("lnt", [128, 1], F32))
            xn16 = st1.enter_context(nc.sbuf_tensor("xn16", [128, 2 * 1024], F16))
            xn_fm = st1.enter_context(nc.sbuf_tensor("xn_fm", [128, CB * T], F16))
            cacc = st1.enter_context(nc.sbuf_tensor("cacc", [128, 2 * T], F32))
            tails = st1.enter_context(nc.sbuf_tensor("tails", [128, CB * 3], F32))
            xb16 = st1.enter_context(nc.sbuf_tensor("xb16", [128, CB * T], F16))
            sz16 = st1.enter_context(nc.sbuf_tensor("sz16", [128, CB * T], F16))
            dt_sb = st1.enter_context(nc.sbuf_tensor("dt_sb", [128, CB * T], F32))
            spt = st1.enter_context(nc.sbuf_tensor("spt", [128, 2 * T], F32))
            spe = st1.enter_context(nc.sbuf_tensor("spe", [128, 2 * T], F32))
            spl = st1.enter_context(nc.sbuf_tensor("spl", [128, 2 * T], F32))
            u16 = st1.enter_context(nc.sbuf_tensor("u16", [128, CB * T], F16))
            proj = st1.enter_context(nc.sbuf_tensor("proj", [33, T], F32))
            projp = st1.enter_context(nc.sbuf_tensor("projp", [33, T], F32))
            bc16 = st1.enter_context(nc.sbuf_tensor("bc16", [32, T], F16))
            Bp_g = st1.enter_context(nc.sbuf_tensor("Bp_g", [128, 8 * T], F16))
            Cp_g = st1.enter_context(nc.sbuf_tensor("Cp_g", [128, 8 * T], F16))
            dtin_bc = st1.enter_context(nc.sbuf_tensor("dtin_bc", [128, T], F32))
            rstd_bc = st1.enter_context(nc.sbuf_tensor("rstd_bc", [128, T], F32))
            nm_bc = st1.enter_context(nc.sbuf_tensor("nm_bc", [128, T], F32))
            dA2 = st1.enter_context(nc.sbuf_tensor("dA2", [128, 2 * T], F32))
            xs3 = st1.enter_context(nc.sbuf_tensor("xs3", [128, 3 * T], BF16))
            tmp2 = st1.enter_context(nc.sbuf_tensor("tmp2", [128, 2 * T], BF16))
            h_g8 = st1.enter_context(nc.sbuf_tensor("h_g8", [128, 8 * T], F32))
            y_sb = st1.enter_context(nc.sbuf_tensor("y_sb", [128, CB * T], F32))
            yst16 = st1.enter_context(nc.sbuf_tensor("yst16", [128, T], BF16))
            y2st16 = st1.enter_context(nc.sbuf_tensor("y2st16", [128, T], BF16))
            lnB1 = st1.enter_context(nc.sbuf_tensor("lnB1", [128, 2 * T], F32))
            ygc = st1.enter_context(nc.sbuf_tensor("ygc", [128, 2 * T], F16))
            ssum = st1.enter_context(nc.sbuf_tensor("ssum", [1, T], F32))
            s2sum = st1.enter_context(nc.sbuf_tensor("s2sum", [1, T], F32))
            sp0 = st1.enter_context(nc.sbuf_tensor("sp0", [1, T], F32))
            sp1 = st1.enter_context(nc.sbuf_tensor("sp1", [1, T], F32))
            murow = st1.enter_context(nc.sbuf_tensor("murow", [1, T], F32))
            varow = st1.enter_context(nc.sbuf_tensor("varow", [1, T], F32))
            rstdrow = st1.enter_context(nc.sbuf_tensor("rstdrow", [1, T], F32))
            nmrow = st1.enter_context(nc.sbuf_tensor("nmrow", [1, T], F32))
            pb0 = st1.enter_context(nc.psum_tensor("pb0", [128, T], F32))
            pb1 = st1.enter_context(nc.psum_tensor("pb1", [128, T], F32))
            pb2 = st1.enter_context(nc.psum_tensor("pb2", [128, T], F32))
            pb3 = st1.enter_context(nc.psum_tensor("pb3", [128, T], F32))
            pb4 = st1.enter_context(nc.psum_tensor("pb4", [128, T], F32))
            pb5 = st1.enter_context(nc.psum_tensor("pb5", [128, T], F32))
            pb6 = st1.enter_context(nc.psum_tensor("pb6", [128, T], F32))
            pbt = st1.enter_context(nc.psum_tensor("pbt", [128, 1024], F16))
            blk1 = st1.enter_context(nc.Block())
            pxb = [pb0, pb1]
            pz = [pb2, pb3]
            pby = [pb5, pb6]

            @blk1.sync
            def _(sy):
                pid = sy.partition_id()
                parv = sy.scalar_reg_alu(ALU.bitwise_and, pid, 1)
                with sy.register("c") as rc, sy.register("rr") as rreg:
                    def poll(par, rd):
                        sy.reg_mov(rc, 1)
                        with sy.While(rc):
                            sy.load(rreg, sh_flag.ap()[1 - par:2 - par, rd:rd + 1])
                            sy.reg_alu(rc, rreg, MAGIC + rd, ALU.not_equal)

                    for par in sy.Switch(parv, 2):
                        nw = 0
                        for kk in range(CB):
                            if kk >= 2:
                                sy.wait_ge(g("su"), kk - 1)
                            sy.dma_start(out=y_sb[:, (kk % 2) * 2048:(kk % 2) * 2048 + 2048],
                                         in_=wipT_in[kk * 128:(kk + 1) * 128, :]).then_inc(g(f"s_w{kk % 2}"), 16)
                        sy.wait_ge(g("su"), 7)
                        sy.dma_start(out=y_sb[:, 0:CB * 33],
                                     in_=xpwT_in).then_inc(g("s_wp"), 16)
                        for dst, src in [(Av, alog_in), (cwv, cw_in), (cbv, cb_in), (Dv, dp_in),
                                         (dtwv, dtw_in), (dtbv, dtb_in), (lngv, lng_in), (lnbv, lnb_in)]:
                            sy.dma_start(out=dst[:, :], in_=src).then_inc(g("s_wp"), 16)
                        for tg in range(TT4):
                            if tg >= 2:
                                sy.wait_ge(g("v_xn"), tg - 1)
                            sy.dma_start(out=x_tm[:, (tg % 2) * 1024:(tg % 2) * 1024 + 1024],
                                         in_=x_in[tg * 128:(tg + 1) * 128, :]).then_inc(g(f"s_x{tg % 2}"), 16)

                        sy.wait_ge(g("o_xs"), 1)
                        for k in range(K1):
                            sy.wait_ge(g("v_pv"), k + 1)
                            sy.dma_start(out=sh_proj.ap()[par, k * 33:(k + 1) * 33, :],
                                         in_=proj[:, :]).then_inc(g("s_pw"), 16)
                            sy.wait_ge(g("s_pw"), 16 * (k + 1))
                            rd = R_PROJ(k)
                            sy.dma_start(out=sh_flag.ap()[par:par + 1, rd:rd + 1],
                                         in_=flagv[0:1, rd:rd + 1]).then_inc(g("s_fl"), 16)
                            poll(par, rd)
                            sy.wait_ge(g("v_pj"), k)
                            sy.dma_start(out=projp[:, :],
                                         in_=sh_proj.ap()[1 - par, k * 33:(k + 1) * 33, :]).then_inc(g("s_pr"), 16)
                            if k > 0:
                                sy.wait_ge(g("s_bcb"), 16 * 2 * k)
                                sy.wait_ge(g("s_bcc"), 16 * 2 * k)
                                sy.wait_ge(g("s_bcd"), 16 * k)
                            sy.wait_ge(g("v_bc"), k + 1)
                            sy.dma_start(out=d_bc.ap(), in_=bc16[:, :]).then_inc(g("s_sg"), 16)
                            sy.dma_start(out=d_dtin.ap(), in_=proj[32:33, :]).then_inc(g("s_sg"), 16)
                            sy.wait_ge(g("s_sg"), SG_CNT(k, 2))
                            sy.wait_ge(g("v_t1"), 8 * k)
                            sy.dma_start(out=dtin_bc[:, :],
                                         in_=d_dtin.ap()[0:1, :].partition_broadcast(128)).then_inc(g("s_bcd"), 16)
                            for gg in range(2):
                                sy.wait_ge(g("v_ym"), max(0, 128 * k + 64 * gg) if (k > 0 or gg > 0) else 0)
                                sy.dma_start(out=Bp_g[:, :],
                                             in_=d_bc.ap()[8 * gg:8 * gg + 8, :]
                                             .partition_broadcast(128)).then_inc(g("s_bcb"), 16)
                                sy.wait_ge(g("o_xs"), max(0, 128 * k + 64 * gg + 1) if (k > 0 or gg > 0) else 0)
                                sy.dma_start(out=Cp_g[:, :],
                                             in_=d_bc.ap()[16 + 8 * gg:16 + 8 * gg + 8, :]
                                             .partition_broadcast(128)).then_inc(g("s_bcc"), 16)
                            if k + 1 < K1:
                                for t in range(TT4):
                                    tg = (k + 1) * TT4 + t
                                    sy.wait_ge(g("v_xn"), max(0, tg - 1))
                                    sy.dma_start(out=x_tm[:, (tg % 2) * 1024:(tg % 2) * 1024 + 1024],
                                                 in_=x_in[tg * 128:(tg + 1) * 128, :]).then_inc(g(f"s_x{tg % 2}"), 16)
                            sy.wait_ge(g("v_se"), k + 1)
                            sy.dma_start(out=sh_st.ap()[par, 2 * k:2 * k + 1, :],
                                         in_=ssum[:, :]).then_inc(g("s_sw"), 16)
                            sy.dma_start(out=sh_st.ap()[par, 2 * k + 1:2 * k + 2, :],
                                         in_=s2sum[:, :]).then_inc(g("s_sw"), 16)
                            sy.wait_ge(g("s_sw"), 32 * (k + 1))
                            rd = R_STAT(k)
                            sy.dma_start(out=sh_flag.ap()[par:par + 1, rd:rd + 1],
                                         in_=flagv[0:1, rd:rd + 1]).then_inc(g("s_fl"), 16)
                            poll(par, rd)
                            sy.wait_ge(g("v_mu"), k)
                            sy.wait_ge(g("a_rw"), k)
                            sy.dma_start(out=sp0[:, :],
                                         in_=sh_st.ap()[1 - par, 2 * k:2 * k + 1, :]).then_inc(g("s_sr"), 16)
                            sy.dma_start(out=sp1[:, :],
                                         in_=sh_st.ap()[1 - par, 2 * k + 1:2 * k + 2, :]).then_inc(g("s_sr"), 16)
                            sy.wait_ge(g("v_nm"), k + 1)
                            if k > 0:
                                sy.wait_ge(g("s_bcr"), 32 * k)
                            sy.dma_start(out=d_rows.ap()[0:1, :], in_=rstdrow[:, :]).then_inc(g("s_sg"), 16)
                            sy.dma_start(out=d_rows.ap()[1:2, :], in_=nmrow[:, :]).then_inc(g("s_sg"), 16)
                            sy.wait_ge(g("s_sg"), SG_CNT(k, 4))
                            sy.wait_ge(g("v_yg"), 8 * k)
                            sy.dma_start(out=rstd_bc[:, :],
                                         in_=d_rows.ap()[0:1, :].partition_broadcast(128)).then_inc(g("s_bcr"), 16)
                            sy.dma_start(out=nm_bc[:, :],
                                         in_=d_rows.ap()[1:2, :].partition_broadcast(128)).then_inc(g("s_bcr"), 16)
                            for cb in range(CB):
                                sy.wait_ge(g("v_yg"), 8 * k + cb + 1)
                                sy.dma_start(out=sh_yg.ap()[par, :, cb * L + k * T: cb * L + (k + 1) * T],
                                             in_=ygc[:, (cb % 2) * T:(cb % 2) * T + T]).then_inc(g(f"s_yw{cb % 2}"), 16)
                            sy.wait_ge(g("s_yw0"), 16 * 4 * (k + 1))
                            sy.wait_ge(g("s_yw1"), 16 * 4 * (k + 1))
                            rd = R_YG(k)
                            sy.dma_start(out=sh_flag.ap()[par:par + 1, rd:rd + 1],
                                         in_=flagv[0:1, rd:rd + 1]).then_inc(g("s_fl"), 16)

                            if dbg and k == 0:
                                sy.dma_start(out=dbg_o["dbg_y"], in_=y_sb[:, :]).then_inc(g("s_fl"), 16)
                                sy.dma_start(out=dbg_o["dbg_proj"], in_=proj[:, :]).then_inc(g("s_fl"), 16)
                                sy.dma_start(out=dbg_o["dbg_dt"], in_=dt_sb[:, :]).then_inc(g("s_fl"), 16)
                                sy.dma_start(out=dbg_o["dbg_xb"], in_=xb16[:, :]).then_inc(g("s_fl"), 16)
                                sy.dma_start(out=dbg_o["dbg_u"], in_=u16[:, :]).then_inc(g("s_fl"), 16)
                                sy.dma_start(out=dbg_o["dbg_sz"], in_=sz16[:, :]).then_inc(g("s_fl"), 16)
                                sy.dma_start(out=dbg_o["dbg_st"][0:1, :], in_=ssum[:, :]).then_inc(g("s_fl"), 16)
                                sy.dma_start(out=dbg_o["dbg_st"][1:2, :], in_=s2sum[:, :]).then_inc(g("s_fl"), 16)
                            if dbg and k == 1:
                                for cbq in range(CB):
                                    sy.dma_start(out=dbg_o["dbg_yg"][:, cbq * T:(cbq + 1) * T],
                                                 in_=sh_yg.ap()[par, :, cbq * L:cbq * L + T]).then_inc(g("s_fl"), 16)

            @blk1.vector
            def _(v):
                CV = [0]

                def vm(inst):
                    inst.then_inc(g("c_v"), 1)
                    CV[0] += 1
                    return CV[0]

                def vflush():
                    v.wait_ge(g("c_v"), CV[0])

                for kk in range(CB):
                    v.wait_ge(g(f"s_w{kk % 2}"), 16 * (kk // 2 + 1))
                    if kk:
                        v.wait_ge(g("su"), kk)
                    v.tensor_copy(out=wipT16[:, kk * 2048:(kk + 1) * 2048],
                                  in_=y_sb[:, (kk % 2) * 2048:(kk % 2) * 2048 + 2048]).then_inc(g("su"), 1)
                v.wait_ge(g("s_wp"), 16 * 9)
                v.wait_ge(g("su"), 8)
                v.tensor_copy(out=xpwT16[:, :], in_=y_sb[:, 0:CB * 33]).then_inc(g("su"), 1)
                v.wait_ge(g("a_rs"), 1)
                v.wait_ge(g("su"), 9)
                v.tensor_scalar(out=Av[:, :], in0=spt[:, 0:CB * NS], scalar1=-1.0,
                                scalar2=None, op0=ALU.mult).then_inc(g("su"), 1)
                v.wait_ge(g("o_xs"), 1)
                vm(v.tensor_scalar(out=idbf[:, :], in0=iotai[:, :], scalar1=0, scalar2=None,
                                   op0=ALU.is_equal))
                vm(v.tensor_scalar(out=id16[:, :], in0=iotai[:, :], scalar1=0, scalar2=None,
                                   op0=ALU.is_equal))
                vm(v.tensor_scalar(out=idf32[:, :], in0=iotai[:, :], scalar1=0, scalar2=None,
                                   op0=ALU.is_equal))
                vm(v.memset(state[:, :], 0.0))
                vm(v.memset(tails[:, :], 0.0))
                v.wait_ge(g("su"), 10)
                v.nop().then_inc(g("su"), 1)

                def te_evac(tg):
                    k_of = tg // TT4
                    t_of = tg % TT4
                    v.wait_ge(g("p_tp"), tg + 1)
                    v.wait_ge(g("p_ip"), 16 * k_of)
                    v.tensor_copy(
                        out=xn_fm.rearrange("p (db l) -> p db l", db=CB)[:, :, t_of * 128:(t_of + 1) * 128],
                        in_=pbt.rearrange("p (db l) -> p db l", db=CB)[:, :, :]).then_inc(g("v_te"), 1)

                def phaseA(k):
                    for t in range(TT4):
                        tg = k * TT4 + t
                        sl = (tg % 2) * 1024
                        ms = (tg % 2) * 2
                        v.wait_ge(g(f"s_x{tg % 2}"), 16 * (tg // 2 + 1))
                        vm(v.bn_stats(out=bn6[:, 0:6], in_=x_tm[:, sl:sl + 512]))
                        v.bn_stats(out=bn6[:, 6:12], in_=x_tm[:, sl + 512:sl + 1024]).then_inc(g("v_st"), 1)
                        v.wait_ge(g("v_st"), 2 * tg + 1)
                        vflush()
                        if tg >= 2:
                            v.wait_ge(g("v_xn"), tg - 1)
                            v.wait_ge(g("a_rs"), tg)
                        v.bn_aggr(out=mv[:, ms:ms + 2], in_=bn6[:, :]).then_inc(g("v_st"), 1)
                        v.wait_ge(g("a_rs"), tg + 2)
                        v.wait_ge(g("p_tp"), max(0, tg - 1))
                        v.tensor_scalar(out=xn16[:, sl:sl + 1024], in0=x_tm[:, sl:sl + 1024],
                                        scalar1=mv[:, ms:ms + 1], scalar2=rstd1[:, (tg % 2):(tg % 2) + 1],
                                        op0=ALU.subtract, op1=ALU.mult).then_inc(g("v_xn"), 1)
                        if t >= 1:
                            te_evac(tg - 1)
                    te_evac(k * TT4 + TT4 - 1)

                phaseA(0)
                tail_mark = {}
                u_marks = {}
                for k in range(K1):
                    if k == 0:
                        v.wait_ge(g("su"), 11)
                        vflush()
                    for q in range(4):
                        ca, cbn = 2 * q, 2 * q + 1
                        v.wait_ge(g("p_ip"), 16 * k + 2 * q + 2)
                        v.wait_ge(g("a_xb"), max(0, 8 * k + 2 * q))
                        for j in [3, 2, 1, 0]:
                            off = 3 - j
                            if j < 3:
                                vflush()
                            for cc, pp in [(ca, pxb[0]), (cbn, pxb[1])]:
                                accs = (cc % 2) * T
                                wcol = cwv[:, cc * 4 + j:cc * 4 + j + 1]
                                if j == 3:
                                    vm(v.tensor_scalar(out=cacc[:, accs:accs + T], in0=pp[:, :],
                                                       scalar1=wcol, scalar2=None, op0=ALU.mult))
                                else:
                                    vm(v.scalar_tensor_tensor(out=cacc[:, accs + off:accs + T],
                                                              in0=pp[:, 0:T - off], scalar=wcol,
                                                              in1=cacc[:, accs + off:accs + T],
                                                              op0=ALU.mult, op1=ALU.add))
                                    vm(v.scalar_tensor_tensor(out=cacc[:, accs:accs + off],
                                                              in0=tails[:, cc * 3 + j:cc * 3 + j + off],
                                                              scalar=wcol,
                                                              in1=cacc[:, accs:accs + off],
                                                              op0=ALU.mult, op1=ALU.add))
                        vflush()
                        for cc, pp in [(ca, pxb[0]), (cbn, pxb[1])]:
                            v.tensor_copy(out=tails[:, cc * 3:cc * 3 + 3],
                                          in_=pp[:, T - 3:T]).then_inc(g("v_cv"), 1)
                    v.wait_ge(g("p_xp"), k + 1)
                    v.wait_ge(g("s_sg"), SG_CNT(k - 1, 2) if k > 0 else 0)
                    v.tensor_copy(out=proj[:, :], in_=pb4[0:33, :]).then_inc(g("v_pv"), 1)
                    v.wait_ge(g("s_pr"), 16 * (k + 1))
                    v.wait_ge(g("v_pv"), k + 1)
                    v.tensor_tensor(out=proj[:, :], in0=proj[:, :], in1=projp[:, :],
                                    op=ALU.add).then_inc(g("v_pj"), 1)
                    v.wait_ge(g("v_pj"), k + 1)
                    v.tensor_copy(out=bc16[:, :], in_=proj[0:32, :]).then_inc(g("v_bc"), 1)
                    for q in range(4):
                        for cc in (2 * q, 2 * q + 1):
                            ss = (cc % 2) * T
                            v.wait_ge(g("s_bcd"), 16 * (k + 1))
                            v.wait_ge(g("a_sp"), max(0, 8 * k + cc - 1))
                            vm(v.tensor_scalar(out=spt[:, ss:ss + T], in0=dtin_bc[:, :],
                                               scalar1=dtwv[:, cc:cc + 1], scalar2=dtbv[:, cc:cc + 1],
                                               op0=ALU.mult, op1=ALU.add))
                        vflush()
                        for cc in (2 * q, 2 * q + 1):
                            ss = (cc % 2) * T
                            v.scalar_tensor_tensor(out=spe[:, ss:ss + T], in0=spt[:, ss:ss + T],
                                                   scalar=-1.0, in1=spt[:, ss:ss + T],
                                                   op0=ALU.mult, op1=ALU.max).then_inc(g("v_t1"), 1)
                        for cc in (2 * q, 2 * q + 1):
                            ss = (cc % 2) * T
                            v.wait_ge(g("a_sp"), 8 * k + cc + 1)
                            if k > 0:
                                v.wait_ge(g("a_dA"), 128 * (k - 1) + 64 + 8 * cc + 8)
                            v.scalar_tensor_tensor(out=dt_sb[:, cc * T:(cc + 1) * T],
                                                   in0=spt[:, ss:ss + T], scalar=0.0,
                                                   in1=spe[:, ss:ss + T],
                                                   op0=ALU.max, op1=ALU.add).then_inc(g("v_dt"), 1)
                        for cc in (2 * q, 2 * q + 1):
                            v.wait_ge(g("a_xb"), 8 * k + cc + 1)
                            v.wait_ge(g("v_dt"), 8 * k + cc + 1)
                            if k > 0:
                                v.wait_ge(g("v_ym"), 128 * (k - 1) + 64 + 8 * cc + 8)
                            u_marks[cc] = vm(v.tensor_tensor(
                                out=u16[:, cc * T:(cc + 1) * T],
                                in0=xb16[:, cc * T:(cc + 1) * T],
                                in1=dt_sb[:, cc * T:(cc + 1) * T],
                                op=ALU.mult))
                            v.nop().then_inc(g("v_u"), 1)
                    vflush()
                    prev = None
                    xs_marks = {}

                    def gather(gcb, gg):
                        v.wait_ge(g("v_ts"), 128 * k + 64 * gg + 8 * gcb + 8)
                        vm(v.tensor_copy(
                            out=state[:, gcb * 16 + 8 * gg:gcb * 16 + 8 * gg + 8],
                            in_=h_g8.rearrange("p (n t) -> p n t", t=T)[:, :, T - 1:T]))

                    def emit_xs(gg, cb, nn, i):
                        v.wait_ge(g("s_bcb"), 16 * (2 * k + gg + 1))
                        v.wait_ge(g("v_ts"), max(0, i - 2))
                        if nn == 0:
                            v.wait_ge(g("c_v"), u_marks[cb])
                        xs_marks[i] = vm(v.tensor_tensor(
                            out=xs3[:, (i % 3) * T:(i % 3) * T + T],
                            in0=u16[:, cb * T:(cb + 1) * T],
                            in1=Bp_g[:, nn * T:nn * T + T],
                            op=ALU.mult))
                        v.nop().then_inc(g("v_ym"), 1)

                    def emit_tts(gg, cb, nn, i):
                        v.wait_ge(g("a_dA"), i + 1)
                        v.wait_ge(g("c_v"), xs_marks.pop(i))
                        v.wait_ge(g("o_xs"), max(0, i - 6))
                        v.tensor_tensor_scan(
                            out=h_g8[:, nn * T:nn * T + T],
                            data0=dA2[:, (i % 2) * T:(i % 2) * T + T],
                            data1=xs3[:, (i % 3) * T:(i % 3) * T + T],
                            initial=state[:, cb * 16 + 8 * gg + nn:cb * 16 + 8 * gg + nn + 1],
                            op0=ALU.mult, op1=ALU.add).then_inc(g("v_ts"), 1)

                    def emit_yev(gg, cb):
                        v.wait_ge(g("p_ya"), 128 * k + 64 * gg + 8 * cb + 8)
                        if gg == 0:
                            if k > 0:
                                v.wait_ge(g("a_y2"), 8 * (k - 1) + cb + 1)
                            v.scalar_tensor_tensor(out=y_sb[:, cb * T:(cb + 1) * T],
                                                   in0=xb16[:, cb * T:(cb + 1) * T],
                                                   scalar=Dv[:, cb:cb + 1],
                                                   in1=pby[cb % 2][:, :],
                                                   op0=ALU.mult, op1=ALU.add).then_inc(g("v_ye"), 1)
                        else:
                            v.wait_ge(g("v_ye"), 16 * k + cb + 1)
                            v.tensor_tensor(out=y_sb[:, cb * T:(cb + 1) * T],
                                            in0=y_sb[:, cb * T:(cb + 1) * T],
                                            in1=pby[cb % 2][:, :],
                                            op=ALU.add).then_inc(g("v_ye"), 1)

                    seq = [(gg, cb, nn, 128 * k + 64 * gg + 8 * cb + nn)
                           for gg in range(2) for cb in range(CB) for nn in range(8)]
                    for item in seq:
                        gg, cb, nn, i = item
                        if nn == 0 and prev is not None:
                            pgg, pcb = prev[0], prev[1]
                            emit_xs(*item)
                            emit_tts(*prev)
                            prev = None
                            gather(pcb, pgg)
                            emit_yev(pgg, pcb)
                            vflush()
                            if gg == 1 and cb == 0 and k + 1 < K1:
                                phaseA(k + 1)
                        else:
                            emit_xs(*item)
                            if prev is not None:
                                emit_tts(*prev)
                        prev = item
                    emit_tts(*prev)
                    gather(CB - 1, 1)
                    emit_yev(1, CB - 1)
                    prev = None
                    for cb in range(CB):
                        v.wait_ge(g("v_ye"), 16 * k + 8 + cb + 1)
                        v.wait_ge(g("p_st"), max(0, 8 * k + cb))
                        v.tensor_copy(out=yst16[:, :],
                                      in_=y_sb[:, cb * T:(cb + 1) * T]).then_inc(g("v_yf"), 1)
                    v.wait_ge(g("p_st"), 8 * (k + 1))
                    v.wait_ge(g("s_sw"), 32 * k)
                    vm(v.tensor_copy(out=ssum[:, :], in_=pb4[0:1, :]))
                    v.tensor_copy(out=s2sum[:, :], in_=pb2[0:1, :]).then_inc(g("v_se"), 1)
                    v.wait_ge(g("s_sr"), 32 * (k + 1))
                    v.wait_ge(g("v_se"), k + 1)
                    vflush()
                    vm(v.tensor_tensor(out=ssum[:, :], in0=ssum[:, :], in1=sp0[:, :], op=ALU.add))
                    vm(v.tensor_tensor(out=s2sum[:, :], in0=s2sum[:, :], in1=sp1[:, :], op=ALU.add))
                    vflush()
                    vm(v.tensor_scalar(out=murow[:, :], in0=ssum[:, :], scalar1=INV2048,
                                       scalar2=None, op0=ALU.mult))
                    vm(v.tensor_scalar(out=varow[:, :], in0=s2sum[:, :], scalar1=INV2048,
                                       scalar2=None, op0=ALU.mult))
                    vflush()
                    vm(v.tensor_tensor(out=sp0[:, :], in0=murow[:, :], in1=murow[:, :], op=ALU.mult))
                    vflush()
                    v.tensor_tensor(out=varow[:, :], in0=varow[:, :], in1=sp0[:, :],
                                    op=ALU.subtract).then_inc(g("v_mu"), 1)
                    v.wait_ge(g("a_rw"), k + 1)
                    v.tensor_tensor(out=nmrow[:, :], in0=murow[:, :], in1=rstdrow[:, :],
                                    op=ALU.mult).then_inc(g("v_nm"), 1)
                    for q in range(4):
                        pair = (2 * q, 2 * q + 1)
                        for cc in pair:
                            v.wait_ge(g("s_bcr"), 32 * (k + 1))
                            v.wait_ge(g(f"s_yw{cc % 2}"), 16 * max(0, 4 * k + cc // 2))
                            vm(v.tensor_tensor(out=lnB1[:, (cc % 2) * T:(cc % 2) * T + T],
                                               in0=y_sb[:, cc * T:(cc + 1) * T],
                                               in1=rstd_bc[:, :], op=ALU.mult))
                        vflush()
                        for cc in pair:
                            vm(v.tensor_tensor(out=cacc[:, (cc % 2) * T:(cc % 2) * T + T],
                                               in0=lnB1[:, (cc % 2) * T:(cc % 2) * T + T],
                                               in1=nm_bc[:, :], op=ALU.subtract))
                        vflush()
                        for cc in pair:
                            vm(v.tensor_scalar(out=lnB1[:, (cc % 2) * T:(cc % 2) * T + T],
                                               in0=cacc[:, (cc % 2) * T:(cc % 2) * T + T],
                                               scalar1=lngv[:, cc:cc + 1], scalar2=lnbv[:, cc:cc + 1],
                                               op0=ALU.mult, op1=ALU.add))
                        vflush()
                        for cc in pair:
                            v.wait_ge(g("a_z"), 8 * k + cc + 1)
                            v.tensor_tensor(out=ygc[:, (cc % 2) * T:(cc % 2) * T + T],
                                            in0=lnB1[:, (cc % 2) * T:(cc % 2) * T + T],
                                            in1=sz16[:, cc * T:(cc + 1) * T],
                                            op=ALU.mult).then_inc(g("v_yg"), 1)

            @blk1.scalar
            def _(a):
                CA = [0]

                def am(inst):
                    inst.then_inc(g("c_a"), 1)
                    CA[0] += 1

                def aflush():
                    a.wait_ge(g("c_a"), CA[0])

                a.wait_ge(g("o_xs"), 1)
                a.wait_ge(g("s_wp"), 16 * 9)
                a.activation(out=spt[:, 0:CB * NS], in_=Av[:, :], func=AF.Exp).then_inc(g("a_rs"), 1)

                def rstds(k):
                    for t in range(TT4):
                        tg = k * TT4 + t
                        a.wait_ge(g("v_st"), 2 * tg + 2)
                        a.wait_ge(g("v_xn"), max(0, tg - 1))
                        a.wait_ge(g("a_rs"), tg + 1)
                        a.activation(out=lnt[:, :], in_=mv[:, 2 * (tg % 2) + 1:2 * (tg % 2) + 2],
                                     func=AF.Ln, bias=cbias[:, 0:1]).then_inc(g("a_rs1"), 1)
                        a.wait_ge(g("a_rs1"), tg + 1)
                        a.activation(out=rstd1[:, (tg % 2):(tg % 2) + 1], in_=lnt[:, :], func=AF.Exp,
                                     scale=-0.5).then_inc(g("a_rs"), 1)

                rstds(0)
                a.wait_ge(g("su"), 10)
                for k in range(K1):
                    for cb in range(CB):
                        a.wait_ge(g("v_cv"), 8 * k + cb + 1)
                        if k > 0:
                            a.wait_ge(g("v_u"), 8 * (k - 1) + cb + 1)
                            a.wait_ge(g("p_xp"), k)
                            a.wait_ge(g("v_ye"), 16 * (k - 1) + cb + 1)
                        a.activation(out=xb16[:, cb * T:(cb + 1) * T],
                                     in_=cacc[:, (cb % 2) * T:(cb % 2) * T + T],
                                     func=AF.Silu, bias=cbv[:, cb:cb + 1]).then_inc(g("a_xb"), 1)
                    for zi in range(CB):
                        a.wait_ge(g("p_ip"), 16 * k + 8 + zi + 1)
                        if k > 0:
                            a.wait_ge(g("v_yg"), 8 * (k - 1) + zi + 1)
                        a.activation(out=sz16[:, zi * T:(zi + 1) * T], in_=pz[zi % 2][:, :],
                                     func=AF.Silu).then_inc(g("a_z"), 1)
                    for q in range(4):
                        for cc in (2 * q, 2 * q + 1):
                            ss = (cc % 2) * T
                            a.wait_ge(g("v_t1"), 8 * k + cc + 1)
                            a.wait_ge(g("a_sp"), max(0, 8 * k + cc - 1))
                            am(a.activation(out=spl[:, ss:ss + T], in_=spe[:, ss:ss + T],
                                            func=AF.Exp, scale=-1.0))
                        aflush()
                        for cc in (2 * q, 2 * q + 1):
                            ss = (cc % 2) * T
                            a.wait_ge(g("v_dt"), max(0, 8 * k + cc - 1))
                            a.activation(out=spe[:, ss:ss + T], in_=spl[:, ss:ss + T],
                                         func=AF.Ln, bias=cbias[:, 1:2]).then_inc(g("a_sp"), 1)
                    for gg in range(2):
                        if gg == 1 and k + 1 < K1:
                            rstds(k + 1)
                        for cb in range(CB):
                            for nn in range(8):
                                i = 128 * k + 64 * gg + 8 * cb + nn
                                n = 8 * gg + nn
                                a.wait_ge(g("v_dt"), 8 * k + cb + 1)
                                a.wait_ge(g("v_ts"), max(0, i - 1))
                                a.activation(out=dA2[:, (i % 2) * T:(i % 2) * T + T],
                                             in_=dt_sb[:, cb * T:(cb + 1) * T], func=AF.Exp,
                                             scale=Av[:, cb * NS + n:cb * NS + n + 1]).then_inc(g("a_dA"), 1)
                    for cb in range(CB):
                        a.wait_ge(g("v_ye"), 16 * k + 8 + cb + 1)
                        a.wait_ge(g("p_st"), max(0, 8 * k + cb))
                        a.activation(out=y2st16[:, :],
                                     in_=y_sb[:, cb * T:(cb + 1) * T],
                                     func=AF.Square).then_inc(g("a_y2"), 1)
                    a.wait_ge(g("v_mu"), k + 1)
                    if k > 0:
                        a.wait_ge(g("s_sg"), SG_CNT(k - 1, 4))
                    a.activation(out=sp0[:, :], in_=varow[:, :], func=AF.Ln,
                                 bias=cbias[0:1, 0:1]).then_inc(g("a_rw1"), 1)
                    a.wait_ge(g("a_rw1"), k + 1)
                    a.activation(out=rstdrow[:, :], in_=sp0[:, :], func=AF.Exp,
                                 scale=-0.5).then_inc(g("a_rw"), 1)

            @blk1.gpsimd
            def _(p):
                p.iota(out=iotai[:, :], pattern=[[1, 128]], base=0, channel_multiplier=-1)
                p.memset(ones16c[:, :], 1.0)
                p.memset(onesbfc[:, :], 1.0)
                p.memset(onesf32c[:, :], 1.0)
                p.memset(onesf32r[:, :], 1.0)
                p.memset(cbias[:, 0:1], EPS_LN)
                p.memset(cbias[:, 1:2], 1.0)
                p.memset(cbias[:, 2:3], EPS_Q)
                p.iota(out=flagv[:, :], pattern=[[1, 64]], base=MAGIC,
                       channel_multiplier=0).then_inc(g("o_xs"), 1)
                for k in range(K1):
                    for gg in range(2):
                        for cb in range(CB):
                            for nn in range(8):
                                i = 128 * k + 64 * gg + 8 * cb + nn
                                p.wait_ge(g("v_ts"), i + 1)
                                p.wait_ge(g("s_bcc"), 16 * (2 * k + gg + 1))
                                p.wait_ge(g("p_ya"), max(0, i - 1))
                                p.tensor_tensor(out=tmp2[:, (i % 2) * T:(i % 2) * T + T],
                                                in0=h_g8[:, nn * T:nn * T + T],
                                                in1=Cp_g[:, nn * T:nn * T + T],
                                                op=ALU.mult).then_inc(g("o_xs"), 1)

            @blk1.tensor
            def _(te):
                te.wait_ge(g("c_v"), 5)
                te.wait_ge(g("o_xs"), 1)

                def transposes(k):
                    for t in range(TT4):
                        tg = k * TT4 + t
                        te.wait_ge(g("v_xn"), tg + 1)
                        te.wait_ge(g("v_te"), tg)
                        for db in range(CB):
                            mm = te.transpose(out=pbt[:, db * 128:(db + 1) * 128],
                                              in_=xn16[:, (tg % 2) * 1024 + db * 128:(tg % 2) * 1024 + (db + 1) * 128],
                                              identity=id16[:, :])
                        mm.then_inc(g("p_tp"), 1)

                def in_proj_part(k, mis):
                    te.wait_ge(g("su"), 11)
                    te.wait_ge(g("v_te"), 4 * (k + 1))
                    for mi in mis:
                        if mi < 8:
                            te.wait_ge(g("v_cv"), max(0, 8 * k + mi - 1))
                            bank = pxb[mi % 2]
                        else:
                            te.wait_ge(g("a_z"), max(0, 8 * k + (mi - 8) - 1))
                            te.wait_ge(g("v_se"), k)
                            bank = pz[mi % 2]
                        for kk in range(CB):
                            mm = te.matmul(out=bank[:, :],
                                           lhsT=wipT16[:, kk * 2048 + mi * 128:kk * 2048 + (mi + 1) * 128],
                                           rhs=xn_fm[:, kk * T:(kk + 1) * T],
                                           start=(kk == 0), stop=(kk == CB - 1))
                        mm.then_inc(g("p_ip"), 1)

                transposes(0)
                for k in range(K1):
                    in_proj_part(k, range(16) if k == 0 else range(2, 16))
                    te.wait_ge(g("v_pv"), k)
                    te.wait_ge(g("v_se"), k)
                    for cb in range(CB):
                        te.wait_ge(g("a_xb"), 8 * k + cb + 1)
                        mm = te.matmul(out=pb4[0:33, :],
                                       lhsT=xpwT16[:, cb * 33:(cb + 1) * 33],
                                       rhs=xb16[:, cb * T:(cb + 1) * T],
                                       start=(cb == 0), stop=(cb == CB - 1))
                    mm.then_inc(g("p_xp"), 1)
                    for gg in range(2):
                        if gg == 1 and k + 1 < K1:
                            transposes(k + 1)
                        for cb in range(CB):
                            if gg == 1 and cb == 4 and k + 1 < K1:
                                in_proj_part(k + 1, range(2))
                            for nn in range(8):
                                i = 128 * k + 64 * gg + 8 * cb + nn
                                te.wait_ge(g("o_xs"), i + 2)
                                if nn == 0:
                                    te.wait_ge(g("v_ye"), max(0, 16 * k + 8 * gg + cb - 1))
                                te.matmul(out=pby[cb % 2][:, :], lhsT=idbf[:, :],
                                          rhs=tmp2[:, (i % 2) * T:(i % 2) * T + T],
                                          start=(nn == 0), stop=(nn == 7)).then_inc(g("p_ya"), 1)
                    for cb in range(CB):
                        te.wait_ge(g("v_yf"), 8 * k + cb + 1)
                        te.wait_ge(g("a_y2"), 8 * k + cb + 1)
                        if cb == 0:
                            te.wait_ge(g("v_pv"), k + 1)
                            te.wait_ge(g("a_z"), 8 * (k + 1))
                            te.wait_ge(g("v_se"), k)
                        te.matmul(out=pb4[0:1, :], lhsT=onesbfc[:, :],
                                  rhs=yst16[:, :],
                                  start=(cb == 0), stop=(cb == CB - 1))
                        te.matmul(out=pb2[0:1, :], lhsT=onesbfc[:, :],
                                  rhs=y2st16[:, :],
                                  start=(cb == 0), stop=(cb == CB - 1)).then_inc(g("p_st"), 1)

        st2 = contextlib.ExitStack()
        with st2:
            owT16 = st2.enter_context(nc.sbuf_tensor("owT16", [128, 16 * D], F16))
            wq16 = st2.enter_context(nc.sbuf_tensor("wq16", [128, CB * D], F16))
            wqT16 = st2.enter_context(nc.sbuf_tensor("wqT16", [128, CB * D], F16))
            wld2 = st2.enter_context(nc.sbuf_tensor("wld2", [128, 2 * D], F32))
            abscols = st2.enter_context(nc.sbuf_tensor("abscols", [128, 9], F32))
            x_tm2 = st2.enter_context(nc.sbuf_tensor("x_tm2", [128, 4 * D], F32))
            bn62 = st2.enter_context(nc.sbuf_tensor("bn62", [128, 12], F32))
            mv2 = st2.enter_context(nc.sbuf_tensor("mv2", [128, 2 * 2], F32))
            rstd2 = st2.enter_context(nc.sbuf_tensor("rstd2", [128, 2], F32))
            lnt2 = st2.enter_context(nc.sbuf_tensor("lnt2", [128, 2], F32))
            xn2 = st2.enter_context(nc.sbuf_tensor("xn2", [128, 2 * D], F32))
            asc4 = st2.enter_context(nc.sbuf_tensor("asc4", [128, 4], F32))
            arec = st2.enter_context(nc.sbuf_tensor("arec", [128, 2], F32))
            zq2 = st2.enter_context(nc.sbuf_tensor("zq2", [128, 2 * D], F32))
            aqi2 = st2.enter_context(nc.sbuf_tensor("aqi2", [128, 2 * D], I32))
            aq_tm = st2.enter_context(nc.sbuf_tensor("aq_tm", [128, 4 * D], F16))
            aq_fm = st2.enter_context(nc.sbuf_tensor("aq_fm", [128, CB * T], F16))
            ascrow = st2.enter_context(nc.sbuf_tensor("ascrow", [1, T], F32))
            scrow = st2.enter_context(nc.sbuf_tensor("scrow", [1, T], F32))
            sc_bc = st2.enter_context(nc.sbuf_tensor("sc_bc", [128, T], F32))
            gbuf = st2.enter_context(nc.sbuf_tensor("gbuf", [128, 2 * T], F32))
            g_ffn = st2.enter_context(nc.sbuf_tensor("g_ffn", [128, CB * T], F32))
            yg_in = st2.enter_context(nc.sbuf_tensor("yg_in", [128, 16 * T], F16))
            sg_sb = st2.enter_context(nc.sbuf_tensor("sg_sb", [128, CB * T], F32))
            outt = st2.enter_context(nc.sbuf_tensor("outt", [128, 2 * D], F32))
            pc0 = st2.enter_context(nc.psum_tensor("pc0", [128, T], F32))
            pc1 = st2.enter_context(nc.psum_tensor("pc1", [128, T], F32))
            pc2 = st2.enter_context(nc.psum_tensor("pc2", [128, T], F32))
            pc3 = st2.enter_context(nc.psum_tensor("pc3", [128, T], F32))
            pct = st2.enter_context(nc.psum_tensor("pct", [128, 1024], F16))
            pcg = st2.enter_context(nc.psum_tensor("pcg", [128, 1024], F32))
            pcs = st2.enter_context(nc.psum_tensor("pcs", [128, 4], F32))
            blk2 = st2.enter_context(nc.Block())
            pf = [pc0, pc1]
            po = [pc2, pc3]

            @blk2.sync
            def _(sy):
                pid = sy.partition_id()
                parv = sy.scalar_reg_alu(ALU.bitwise_and, pid, 1)
                with sy.register("c2") as rc, sy.register("rr2") as rreg:
                    def poll(par, rd):
                        sy.reg_mov(rc, 1)
                        with sy.While(rc):
                            sy.load(rreg, sh_flag.ap()[1 - par:2 - par, rd:rd + 1])
                            sy.reg_alu(rc, rreg, MAGIC + rd, ALU.not_equal)

                    for par in sy.Switch(parv, 2):
                        nn = 0
                        for kk in range(CB):
                            if kk >= 2:
                                sy.wait_ge(g("wq_v"), kk - 1)
                            sy.dma_start(out=wld2[:, (kk % 2) * D:(kk % 2) * D + D],
                                         in_=ffnw_in[kk * 128:(kk + 1) * 128, :]).then_inc(g(f"s_w2{kk % 2}"), 16)
                        for kk in range(CB):
                            sy.wait_ge(g("wq_v"), 12 + max(0, kk - 1))
                            sy.dma_start(out=wld2[:, (kk % 2) * D:(kk % 2) * D + D],
                                         in_=ffnw_in[kk * 128:(kk + 1) * 128, :]).then_inc(g(f"s_w2{kk % 2}"), 16)
                        for kk in range(16):
                            sy.wait_ge(g("wq_v"), 20 if kk < 2 else 27 + kk)
                            sy.dma_start(out=wld2[:, (kk % 2) * D:(kk % 2) * D + D],
                                         in_=owT_in[kk * 128:(kk + 1) * 128, :]).then_inc(g(f"s_w2{kk % 2}"), 16)
                        sy.wait_ge(g("s_yw0"), 16 * 4 * K1)
                        sy.wait_ge(g("s_yw1"), 16 * 4 * K1)
                        for j in range(K2):
                            kk_g = 8 * par + j
                            sy.wait_ge(g("v2o"), 4 * j)
                            sy.dma_start(
                                out=x_tm2[:, :],
                                in_=x_in[par * TOK2 + j * T: par * TOK2 + (j + 1) * T, :]
                                .rearrange("(a p) d -> p a d", p=128)).then_inc(g("s_x2"), 16)
                            poll(par, R_YG(kk_g))
                            sy.wait_ge(g("p2o"), 8 * j)
                            sy.dma_start(
                                out=yg_in[:, 0:CB * T],
                                in_=sh_yg.ap().rearrange("a p (cb l) -> a p cb l", cb=CB)
                                [par, :, :, kk_g * T:(kk_g + 1) * T]).then_inc(g("s_yr"), 16)
                            sy.dma_start(
                                out=yg_in[:, CB * T:16 * T],
                                in_=sh_yg.ap().rearrange("a p (cb l) -> a p cb l", cb=CB)
                                [1 - par, :, :, kk_g * T:(kk_g + 1) * T]).then_inc(g("s_yr"), 16)
                            sy.wait_ge(g("v2x"), 4 * (j + 1))
                            if j > 0:
                                sy.wait_ge(g("s_a2"), 16 * 3 * j)
                                sy.wait_ge(g("v2q"), j)
                            sy.dma_start(out=d_asc.ap(), in_=asc4[:, :]).then_inc(g("s_a2"), 16)
                            sy.wait_ge(g("s_a2"), 16 * (3 * j + 1))
                            sy.dma_start(out=ascrow[:, :],
                                         in_=d_asc.ap().rearrange("p t -> t p").unsqueeze(0),
                                         allow_slow_non_contiguous=True).then_inc(g("s_a2"), 16)
                            sy.wait_ge(g("v2q"), j + 1)
                            if j > 0:
                                sy.wait_ge(g("s_b2"), 16 * j)
                            sy.dma_start(out=d_sc.ap(), in_=scrow[:, :]).then_inc(g("s_a2"), 16)
                            sy.wait_ge(g("s_a2"), 16 * (3 * j + 3))
                            sy.wait_ge(g("v2g"), 8 * j)
                            sy.dma_start(out=sc_bc[:, :],
                                         in_=d_sc.ap()[0:1, :].partition_broadcast(128)).then_inc(g("s_b2"), 16)
                            for t in range(TT4):
                                sy.wait_ge(g("v2o"), 4 * j + t + 1)
                                sy.dma_start(out=o_out[j * T + t * 128: j * T + (t + 1) * 128, :],
                                             in_=outt[:, (t % 2) * D:(t % 2) * D + D]).then_inc(g(f"s_o{t % 2}"), 16)
                            if dbg and j == 0:
                                sy.wait_ge(g("a2g"), 8)
                                sy.dma_start(out=dbg_o["dbg_g"], in_=g_ffn[:, :]).then_inc(g("s_fl"), 16)
                                sy.wait_ge(g("v2s"), 8)
                                sy.dma_start(out=dbg_o["dbg_sg"], in_=sg_sb[:, :]).then_inc(g("s_fl"), 16)
                                sy.wait_ge(g("v2e"), 4)
                                sy.dma_start(out=dbg_o["dbg_aq"], in_=aq_fm[:, :]).then_inc(g("s_fl"), 16)
                        sy.wait_ge(g("s_o0"), 16 * 2 * K2)
                        sy.wait_ge(g("s_o1"), 16 * 2 * K2)

            @blk2.vector
            def _(v):
                C2 = [0]

                def vm(inst):
                    inst.then_inc(g("c_v2"), 1)
                    C2[0] += 1

                def vflush():
                    v.wait_ge(g("c_v2"), C2[0])

                for kk in range(CB):
                    v.wait_ge(g(f"s_w2{kk % 2}"), 16 * (kk // 2 + 1))
                    v.tensor_reduce(out=abscols[:, kk:kk + 1],
                                    in_=wld2[:, (kk % 2) * D:(kk % 2) * D + D],
                                    axis=AX.X, op=ALU.add, apply_absolute_value=True).then_inc(g("wq_v"), 1)
                v.wait_ge(g("wq_v"), 8)
                v.tensor_reduce(out=abscols[:, 8:9], in_=abscols[:, 0:8], axis=AX.X,
                                op=ALU.add).then_inc(g("wq_v"), 1)
                v.wait_ge(g("wq_p"), 1)
                v.tensor_copy(out=wscpre[:, :], in_=pcs[0:1, 0:1]).then_inc(g("wq_v"), 1)
                v.wait_ge(g("wq_p"), 2)
                v.tensor_copy(out=wscv[:, :], in_=pcs[:, 1:2]).then_inc(g("wq_v"), 1)
                v.wait_ge(g("wq_v"), 11)
                v.reciprocal(out=wrecv[:, :], in_=wscv[:, :]).then_inc(g("wq_v"), 1)
                v.wait_ge(g("wq_v"), 12)
                for qq in range(4):
                    ka, kb = 2 * qq, 2 * qq + 1
                    for kk in (ka, kb):
                        v.wait_ge(g(f"s_w2{kk % 2}"), 16 * (4 + kk // 2 + 1))
                        vm(v.tensor_scalar(out=zq2[:, (kk % 2) * D:(kk % 2) * D + D],
                                           in0=wld2[:, (kk % 2) * D:(kk % 2) * D + D],
                                           scalar1=wrecv[:, 0:1], scalar2=None, op0=ALU.mult))
                    vflush()
                    for kk in (ka, kb):
                        sl = slice((kk % 2) * D, (kk % 2) * D + D)
                        wsl = slice(kk * D, (kk + 1) * D)
                        vm(v.tensor_copy(out=aqi2[:, sl], in_=zq2[:, sl]))
                        vflush()
                        vm(v.tensor_copy(out=wq16[:, wsl], in_=aqi2[:, sl]))
                        vflush()
                        vm(v.tensor_tensor(out=xn2[:, sl], in0=zq2[:, sl],
                                           in1=wq16[:, wsl], op=ALU.subtract))
                        vflush()
                        vm(v.tensor_scalar(out=zq2[:, sl], in0=xn2[:, sl],
                                           scalar1=0.5, scalar2=None, op0=ALU.is_gt))
                        vm(v.tensor_scalar(out=xn2[:, sl], in0=xn2[:, sl],
                                           scalar1=-0.5, scalar2=None, op0=ALU.is_lt))
                        vflush()
                        vm(v.tensor_tensor(out=wq16[:, wsl], in0=wq16[:, wsl],
                                           in1=zq2[:, sl], op=ALU.add))
                        vflush()
                        vm(v.tensor_tensor(out=wq16[:, wsl], in0=wq16[:, wsl],
                                           in1=xn2[:, sl], op=ALU.subtract))
                        vflush()
                        v.tensor_scalar(out=wq16[:, wsl], in0=wq16[:, wsl],
                                        scalar1=1.0, scalar2=-1.0,
                                        op0=ALU.min, op1=ALU.max).then_inc(g("wq_v"), 1)
                for kk in range(CB):
                    v.wait_ge(g("wq_p"), 3 + kk)
                    v.tensor_copy(
                        out=wqT16.rearrange("p (bb d) -> p bb d", bb=CB)[:, :, kk * 128:(kk + 1) * 128],
                        in_=pct.rearrange("p (bb d) -> p bb d", bb=CB)[:, :, :]).then_inc(g("wq_v"), 1)
                for kk in range(16):
                    v.wait_ge(g(f"s_w2{kk % 2}"), 16 * (8 + kk // 2 + 1))
                    v.tensor_copy(out=owT16[:, kk * D:(kk + 1) * D],
                                  in_=wld2[:, (kk % 2) * D:(kk % 2) * D + D]).then_inc(g("wq_v"), 1)

                v.wait_ge(g("wq_v"), 20)
                for j in range(K2):
                    for qq in range(2):
                        ta, tb = 2 * qq, 2 * qq + 1
                        for t in (ta, tb):
                            v.wait_ge(g("s_x2"), 16 * (j + 1))
                            vm(v.bn_stats(out=bn62[:, 0:6], in_=x_tm2[:, t * D:t * D + 512]))
                            v.bn_stats(out=bn62[:, 6:12],
                                       in_=x_tm2[:, t * D + 512:(t + 1) * D]).then_inc(g("v2a"), 1)
                            v.wait_ge(g("v2a"), 2 * (4 * j + t) + 1)
                            vflush()
                            v.wait_ge(g("v2x"), max(0, 4 * j + t - 1))
                            v.wait_ge(g("a2r"), max(0, 4 * j + t - 1))
                            v.bn_aggr(out=mv2[:, 2 * (t % 2):2 * (t % 2) + 2],
                                      in_=bn62[:, :]).then_inc(g("v2a"), 1)
                        for t in (ta, tb):
                            v.wait_ge(g("a2r"), 4 * j + t + 1)
                            v.wait_ge(g("p2t"), max(0, 4 * j + t - 1))
                            vm(v.tensor_scalar(out=xn2[:, (t % 2) * D:(t % 2) * D + D],
                                               in0=x_tm2[:, t * D:(t + 1) * D],
                                               scalar1=mv2[:, 2 * (t % 2):2 * (t % 2) + 1],
                                               scalar2=rstd2[:, (t % 2):(t % 2) + 1],
                                               op0=ALU.subtract, op1=ALU.mult))
                        vflush()
                        for t in (ta, tb):
                            vm(v.reduce_max(out=asc4[:, t:t + 1], in_=xn2[:, (t % 2) * D:(t % 2) * D + D],
                                            axis=AX.X, apply_absolute_value=True))
                        vflush()
                        for t in (ta, tb):
                            vm(v.tensor_scalar(out=asc4[:, t:t + 1], in0=asc4[:, t:t + 1],
                                               scalar1=EPS_Q, scalar2=None, op0=ALU.add))
                        vflush()
                        for t in (ta, tb):
                            vm(v.reciprocal(out=arec[:, (t % 2):(t % 2) + 1], in_=asc4[:, t:t + 1]))
                        vflush()
                        for t in (ta, tb):
                            vm(v.tensor_scalar(out=zq2[:, (t % 2) * D:(t % 2) * D + D],
                                               in0=xn2[:, (t % 2) * D:(t % 2) * D + D],
                                               scalar1=arec[:, (t % 2):(t % 2) + 1], scalar2=127.0,
                                               op0=ALU.mult, op1=ALU.mult))
                        vflush()
                        for t in (ta, tb):
                            sl = slice((t % 2) * D, (t % 2) * D + D)
                            asl = slice(t * D, (t + 1) * D)
                            vm(v.tensor_copy(out=aqi2[:, sl], in_=zq2[:, sl]))
                            vflush()
                            vm(v.tensor_copy(out=aq_tm[:, asl], in_=aqi2[:, sl]))
                            vflush()
                            vm(v.tensor_tensor(out=xn2[:, sl], in0=zq2[:, sl],
                                               in1=aq_tm[:, asl], op=ALU.subtract))
                            vflush()
                            vm(v.tensor_scalar(out=zq2[:, sl], in0=xn2[:, sl],
                                               scalar1=0.5, scalar2=None, op0=ALU.is_gt))
                            vm(v.tensor_scalar(out=xn2[:, sl], in0=xn2[:, sl],
                                               scalar1=-0.5, scalar2=None, op0=ALU.is_lt))
                            vflush()
                            vm(v.tensor_tensor(out=aq_tm[:, asl], in0=aq_tm[:, asl],
                                               in1=zq2[:, sl], op=ALU.add))
                            vflush()
                            v.tensor_tensor(out=aq_tm[:, asl], in0=aq_tm[:, asl],
                                            in1=xn2[:, sl], op=ALU.subtract).then_inc(g("v2x"), 1)
                        for t in (ta, tb):
                            v.wait_ge(g("p2t"), 4 * j + t + 1)
                            v.wait_ge(g("p2f"), 8 * j)
                            v.tensor_copy(
                                out=aq_fm.rearrange("p (db l) -> p db l", db=CB)[:, :, t * 128:(t + 1) * 128],
                                in_=pct.rearrange("p (db l) -> p db l", db=CB)[:, :, :]).then_inc(g("v2e"), 1)
                    v.wait_ge(g("s_a2"), 16 * (3 * j + 2))
                    v.wait_ge(g("wq_a"), 1)
                    v.tensor_scalar(out=scrow[:, :], in0=ascrow[:, :], scalar1=wsc[0:1, 0:1],
                                    scalar2=1.0 / 127.0, op0=ALU.mult, op1=ALU.mult).then_inc(g("v2q"), 1)
                    for m in range(CB):
                        v.wait_ge(g("p2f"), 8 * j + m + 1)
                        v.wait_ge(g("s_b2"), 16 * (j + 1))
                        v.wait_ge(g("a2g"), max(0, 8 * j + m - 1))
                        v.tensor_tensor(out=gbuf[:, (m % 2) * T:(m % 2) * T + T],
                                        in0=pf[m % 2][:, :], in1=sc_bc[:, :],
                                        op=ALU.mult).then_inc(g("v2g"), 1)
                    for m in range(CB):
                        v.wait_ge(g("p2o"), 8 * j + m + 1)
                        v.wait_ge(g("a2g"), 8 * j + m + 1)
                        v.tensor_tensor(out=sg_sb[:, m * T:(m + 1) * T],
                                        in0=po[m % 2][:, :], in1=g_ffn[:, m * T:(m + 1) * T],
                                        op=ALU.mult).then_inc(g("v2s"), 1)
                    for t in range(TT4):
                        v.wait_ge(g("p2g"), 4 * j + t + 1)
                        v.wait_ge(g(f"s_o{t % 2}"), 16 * max(0, 2 * j + t // 2))
                        v.tensor_tensor(out=outt[:, (t % 2) * D:(t % 2) * D + D],
                                        in0=x_tm2[:, t * D:(t + 1) * D], in1=pcg[:, :],
                                        op=ALU.add).then_inc(g("v2o"), 1)

            @blk2.scalar
            def _(a):
                a.wait_ge(g("wq_v"), 10)
                a.activation(out=wsc[:, :], in_=wscpre[:, :], func=AF.Copy,
                             scale=INVMEAN, bias=EPS_Q).then_inc(g("wq_a"), 1)
                for j in range(K2):
                    for t in range(TT4):
                        a.wait_ge(g("v2a"), 2 * (4 * j + t) + 2)
                        a.wait_ge(g("v2x"), max(0, 4 * j + t - 1))
                        a.activation(out=lnt2[:, (t % 2):(t % 2) + 1],
                                     in_=mv2[:, 2 * (t % 2) + 1:2 * (t % 2) + 2], func=AF.Ln,
                                     bias=cbias[:, 0:1]).then_inc(g("a_rs1"), 1)
                        a.wait_ge(g("a_rs1"), 4 * K1 + 4 * j + t + 1)
                        a.activation(out=rstd2[:, (t % 2):(t % 2) + 1],
                                     in_=lnt2[:, (t % 2):(t % 2) + 1], func=AF.Exp,
                                     scale=-0.5).then_inc(g("a2r"), 1)
                    for m in range(CB):
                        a.wait_ge(g("v2g"), 8 * j + m + 1)
                        a.activation(out=g_ffn[:, m * T:(m + 1) * T],
                                     in_=gbuf[:, (m % 2) * T:(m % 2) * T + T],
                                     func=AF.Silu).then_inc(g("a2g"), 1)

            @blk2.tensor
            def _(te):
                te.wait_ge(g("wq_v"), 9)
                te.matmul(out=pcs[0:1, 0:1], lhsT=onesf32c[:, :], rhs=abscols[:, 8:9],
                          start=True, stop=True).then_inc(g("wq_p"), 1)
                te.wait_ge(g("wq_a"), 1)
                te.matmul(out=pcs[:, 1:2], lhsT=onesf32r[:, :], rhs=wsc[:, :],
                          start=True, stop=True).then_inc(g("wq_p"), 1)
                for kk in range(CB):
                    te.wait_ge(g("wq_v"), 13 + kk)
                    te.wait_ge(g("wq_v"), 20 + max(0, kk))
                    for bb in range(CB):
                        mm = te.transpose(out=pct[:, bb * 128:(bb + 1) * 128],
                                          in_=wq16[:, kk * D + bb * 128:kk * D + (bb + 1) * 128],
                                          identity=id16[:, :])
                    mm.then_inc(g("wq_p"), 1)
                for j in range(K2):
                    for t in range(TT4):
                        te.wait_ge(g("v2x"), 4 * j + t + 1)
                        te.wait_ge(g("v2e"), 4 * j + t)
                        for db in range(CB):
                            mm = te.transpose(out=pct[:, db * 128:(db + 1) * 128],
                                              in_=aq_tm[:, t * D + db * 128:t * D + (db + 1) * 128],
                                              identity=id16[:, :])
                        mm.then_inc(g("p2t"), 1)
                    te.wait_ge(g("wq_v"), 28)
                    te.wait_ge(g("v2e"), 4 * (j + 1))
                    for m in range(CB):
                        te.wait_ge(g("v2g"), max(0, 8 * j + m - 1))
                        for kk in range(CB):
                            mm = te.matmul(out=pf[m % 2][:, :],
                                           lhsT=wqT16[:, kk * D + m * 128:kk * D + (m + 1) * 128],
                                           rhs=aq_fm[:, kk * T:(kk + 1) * T],
                                           start=(kk == 0), stop=(kk == CB - 1))
                        mm.then_inc(g("p2f"), 1)
                    te.wait_ge(g("wq_v"), 44)
                    te.wait_ge(g("s_yr"), 32 * (j + 1))
                    for m in range(CB):
                        te.wait_ge(g("v2s"), max(0, 8 * j + m - 1))
                        for kk in range(16):
                            mm = te.matmul(out=po[m % 2][:, :],
                                           lhsT=owT16[:, kk * D + m * 128:kk * D + (m + 1) * 128],
                                           rhs=yg_in[:, kk * T:(kk + 1) * T],
                                           start=(kk == 0), stop=(kk == 15))
                        mm.then_inc(g("p2o"), 1)
                    for t in range(TT4):
                        te.wait_ge(g("v2s"), 8 * (j + 1))
                        te.wait_ge(g("v2o"), 4 * j + t)
                        for m in range(CB):
                            mm = te.transpose(out=pcg[:, m * 128:(m + 1) * 128],
                                              in_=sg_sb[:, m * T + t * 128:m * T + (t + 1) * 128],
                                              identity=idf32[:, :])
                        mm.then_inc(g("p2g"), 1)

    return nc


_NC_CACHE = {}


def kernel(L0_attn_x, L0_attn_in_proj_w, L0_attn_conv_w, L0_attn_conv_b,
           L0_attn_x_proj_w, L0_attn_A_log, L0_attn_D_param, L0_attn_dt_w,
           L0_attn_dt_b, L0_attn_out_w, L0_attn_ln_g, L0_attn_ln_b, L0_ffn_weight):
    f32 = np.float32
    x = np.asarray(L0_attn_x, f32)
    ipT = np.ascontiguousarray(np.asarray(L0_attn_in_proj_w, f32).T)
    xp = np.asarray(L0_attn_x_proj_w, f32)
    xp_re = np.concatenate([xp[1:17], xp[17:33], xp[0:1]], 0)
    alog = np.asarray(L0_attn_A_log, f32)
    cw = np.asarray(L0_attn_conv_w, f32)
    cbb = np.asarray(L0_attn_conv_b, f32)
    dp = np.asarray(L0_attn_D_param, f32)
    dtw = np.asarray(L0_attn_dt_w, f32)[:, 0]
    dtb = np.asarray(L0_attn_dt_b, f32)
    lng = np.asarray(L0_attn_ln_g, f32)
    lnb = np.asarray(L0_attn_ln_b, f32)
    owT = np.ascontiguousarray(np.asarray(L0_attn_out_w, f32).T)
    ffnw = np.ascontiguousarray(np.asarray(L0_ffn_weight, f32))

    def vec_layout(vv, half):
        return np.ascontiguousarray(vv[half * CL:(half + 1) * CL].reshape(CB, 128).T)

    in_maps = []
    for r in range(8):
        b, par = r // 2, r & 1
        h0 = par * CL
        wipT = np.ascontiguousarray(
            np.concatenate([ipT[:, h0:h0 + CL], ipT[:, DI + h0:DI + h0 + CL]], 1))
        xpwT = np.ascontiguousarray(
            xp_re[:, h0:h0 + CL].T.reshape(CB, 128, 33).transpose(1, 0, 2).reshape(128, CB * 33))
        al = np.ascontiguousarray(
            alog[h0:h0 + CL].reshape(CB, 128, NS).transpose(1, 0, 2).reshape(128, CB * NS))
        cwc = np.ascontiguousarray(
            cw[h0:h0 + CL].reshape(CB, 128, 4).transpose(1, 0, 2).reshape(128, CB * 4))
        owc = np.ascontiguousarray(
            np.concatenate([owT[h0:h0 + CL], owT[CL - h0:CL - h0 + CL]], 0)) if False else \
            np.ascontiguousarray(
            np.concatenate([owT[h0:h0 + CL], owT[(1 - par) * CL:(1 - par) * CL + CL]], 0))
        in_maps.append(dict(
            x_in=np.ascontiguousarray(x[b]),
            wipT_in=wipT, xpwT_in=xpwT, alog_in=al, cw_in=cwc,
            cb_in=vec_layout(cbb, par), dp_in=vec_layout(dp, par),
            dtw_in=vec_layout(dtw, par), dtb_in=vec_layout(dtb, par),
            lng_in=vec_layout(lng, par), lnb_in=vec_layout(lnb, par),
            owT_in=owc, ffnw_in=ffnw,
        ))

    nc = build_nc(dbg=False)
    res = run_bass_kernel_spmd(nc, in_maps, core_ids=list(range(8)))
    out = np.empty((4, L, D), f32)
    for r in range(8):
        b, par = r // 2, r & 1
        out[b, par * TOK2:(par + 1) * TOK2] = res.results[r]["o_out"]
    return out


# version 06
# speedup vs baseline: 1.3233x; 1.3233x over previous
import numpy as np
import concourse.bass as bass
import concourse.mybir as mybir
from concourse.bass_utils import run_bass_kernel_spmd

F32 = mybir.dt.float32
F16 = mybir.dt.float16
BF16 = mybir.dt.bfloat16
I32 = mybir.dt.int32
AF = mybir.ActivationFunctionType
ALU = mybir.AluOpType
AX = mybir.AxisListType

L = 8192
D = 1024
DI = 2048
NS = 16
CL = 1024
CB = 8
T = 512
K1 = L // T
TT4 = T // 128
TOK2 = 4096
K2 = TOK2 // T
MAGIC = 777000
EPS_LN = 1e-5
EPS_Q = 1e-8
INV2048 = 1.0 / 2048.0
INVMEAN = 1.0 / (1024.0 * 1024.0)

R_PROJ = lambda k: k
R_STAT = lambda k: K1 + k
R_YG = lambda k: 2 * K1 + k


def build_nc(dbg=False, race_detect=False):
    nc = bass.Bass("TRN2", target_bir_lowering=False, detect_race_conditions=race_detect)

    x_in = nc.dram_tensor("x_in", [L, D], F32, kind="ExternalInput").ap()
    wipT_in = nc.dram_tensor("wipT_in", [D, 2 * CL], F32, kind="ExternalInput").ap()
    xpwT_in = nc.dram_tensor("xpwT_in", [128, CB * 33], F32, kind="ExternalInput").ap()
    alog_in = nc.dram_tensor("alog_in", [128, CB * NS], F32, kind="ExternalInput").ap()
    cw_in = nc.dram_tensor("cw_in", [128, CB * 4], F32, kind="ExternalInput").ap()
    cb_in = nc.dram_tensor("cb_in", [128, CB], F32, kind="ExternalInput").ap()
    dp_in = nc.dram_tensor("dp_in", [128, CB], F32, kind="ExternalInput").ap()
    dtw_in = nc.dram_tensor("dtw_in", [128, CB], F32, kind="ExternalInput").ap()
    dtb_in = nc.dram_tensor("dtb_in", [128, CB], F32, kind="ExternalInput").ap()
    lng_in = nc.dram_tensor("lng_in", [128, CB], F32, kind="ExternalInput").ap()
    lnb_in = nc.dram_tensor("lnb_in", [128, CB], F32, kind="ExternalInput").ap()
    owT_in = nc.dram_tensor("owT_in", [DI, D], F32, kind="ExternalInput").ap()
    ffnw_in = nc.dram_tensor("ffnw_in", [D, D], F32, kind="ExternalInput").ap()
    o_out = nc.dram_tensor("o_out", [TOK2, D], F32, kind="ExternalOutput").ap()

    dbg_o = {}
    if dbg:
        for nm, shp, dt_ in [("dbg_y", [128, CB * T], F32), ("dbg_proj", [33, T], F32),
                             ("dbg_dt", [128, CB * T], F32), ("dbg_xb", [128, CB * T], F16),
                             ("dbg_u", [128, CB * T], F16), ("dbg_sz", [128, CB * T], F16),
                             ("dbg_yg", [128, CB * T], F16), ("dbg_g", [128, CB * T], F32),
                             ("dbg_sg", [128, CB * T], F32), ("dbg_aq", [128, CB * T], F16),
                             ("dbg_h", [128, CB * T], F32), ("dbg_st", [2, T], F32)]:
            dbg_o[nm] = nc.dram_tensor(nm, shp, dt_, kind="ExternalOutput").ap()

    sh_proj = nc.dram_tensor("sh_proj", [2, K1 * 33, T], F32, addr_space="Shared")
    sh_st = nc.dram_tensor("sh_st", [2, K1 * 2, T], F32, addr_space="Shared")
    sh_yg = nc.dram_tensor("sh_yg", [2, 128, CB * L], F16, addr_space="Shared")
    sh_flag = nc.dram_tensor("sh_flag", [2, 64], I32, addr_space="Shared")
    d_bc = nc.dram_tensor("d_bc", [32, T], F16)
    d_dtin = nc.dram_tensor("d_dtin", [1, T], F32)
    d_rows = nc.dram_tensor("d_rows", [2, T], F32)
    d_asc = nc.dram_tensor("d_asc", [128, 4], F32)
    d_sc = nc.dram_tensor("d_sc", [1, T], F32)

    import contextlib
    est = contextlib.ExitStack()
    with est:
        sb = lambda name, shape, dt_: est.enter_context(nc.sbuf_tensor(name, shape, dt_))
        sem = lambda name: est.enter_context(nc.semaphore(name))

        state = sb("state", [128, 128], F32)
        flagv = sb("flagv", [1, 64], I32)
        Av = sb("Av", [128, CB * NS], F32)
        cwv = sb("cwv", [128, CB * 4], F32)
        cbv = sb("cbv", [128, CB], F32)
        Dv = sb("Dv", [128, CB], F32)
        dtwv = sb("dtwv", [128, CB], F32)
        dtbv = sb("dtbv", [128, CB], F32)
        lngv = sb("lngv", [128, CB], F32)
        lnbv = sb("lnbv", [128, CB], F32)
        idbf = sb("idbf", [128, 128], BF16)
        id16 = sb("id16", [128, 128], F16)
        idf32 = sb("idf32", [128, 128], F32)
        ones16c = sb("ones16c", [128, 1], F16)
        onesbfc = sb("onesbfc", [128, 1], BF16)
        onesf32c = sb("onesf32c", [128, 1], F32)
        onesf32r = sb("onesf32r", [1, 128], F32)
        iotai = sb("iotai", [128, 128], I32)
        cbias = sb("cbias", [128, 4], F32)
        wsc = sb("wsc", [1, 1], F32)
        wscpre = sb("wscpre", [1, 1], F32)
        wrecv = sb("wrecv", [128, 1], F32)
        wscv = sb("wscv", [128, 1], F32)

        names = ("s_x0 s_x1 s_w0 s_w1 s_wp s_bcd s_bcb s_bcc s_bcr s_sg s_pw s_pr s_sw s_sr s_yw0 s_yw1 s_yr s_fl s_o0 s_o1 s_x2 s_w20 s_w21 s_a2 s_b2 "
                 "v_st a_rs a_rs1 v_xn p_tp v_te p_ip v_cv a_xb a_z p_xp v_pv v_pj v_bc "
                 "v_t1 a_sp v_dt v_u a_dA o_xs v_ts v_ym p_ya v_ye v_yf a_y2 p_st v_se "
                 "v_mu a_rw a_rw1 v_nm v_yg su su2 "
                 "v2x v2a v2q p2t v2e p2f v2g a2g p2o v2s p2g v2o a2r wq_v wq_p wq_a c_v c_a c_v2").split()
        S = {n: sem(n) for n in names}
        g = lambda n: S[n]

        IDX = lambda k, cb, n: 128 * k + 16 * cb + n
        BC_CNT = lambda k, j: 16 * (7 * k + j)
        SG_CNT = lambda k, j: 16 * (4 * k + j)

        st1 = contextlib.ExitStack()
        with st1:
            wipT16 = st1.enter_context(nc.sbuf_tensor("wipT16", [128, CB * 2 * CL], F16))
            xpwT16 = st1.enter_context(nc.sbuf_tensor("xpwT16", [128, CB * 33], F16))
            x_tm = st1.enter_context(nc.sbuf_tensor("x_tm", [128, 2 * 1024], F32))
            bn6 = st1.enter_context(nc.sbuf_tensor("bn6", [128, 12], F32))
            mv = st1.enter_context(nc.sbuf_tensor("mv", [128, 4], F32))
            rstd1 = st1.enter_context(nc.sbuf_tensor("rstd1", [128, 2], F32))
            lnt = st1.enter_context(nc.sbuf_tensor("lnt", [128, 1], F32))
            xn16 = st1.enter_context(nc.sbuf_tensor("xn16", [128, 2 * 1024], F16))
            xn_fm = st1.enter_context(nc.sbuf_tensor("xn_fm", [128, CB * T], F16))
            cacc = st1.enter_context(nc.sbuf_tensor("cacc", [128, 2 * T], F32))
            tails = st1.enter_context(nc.sbuf_tensor("tails", [128, CB * 3], F32))
            xb16 = st1.enter_context(nc.sbuf_tensor("xb16", [128, CB * T], F16))
            sz16 = st1.enter_context(nc.sbuf_tensor("sz16", [128, CB * T], F16))
            dt_sb = st1.enter_context(nc.sbuf_tensor("dt_sb", [128, CB * T], F32))
            spt = st1.enter_context(nc.sbuf_tensor("spt", [128, 2 * T], F32))
            spe = st1.enter_context(nc.sbuf_tensor("spe", [128, 2 * T], F32))
            spl = st1.enter_context(nc.sbuf_tensor("spl", [128, 2 * T], F32))
            u16 = st1.enter_context(nc.sbuf_tensor("u16", [128, CB * T], F16))
            proj = st1.enter_context(nc.sbuf_tensor("proj", [33, T], F32))
            projp = st1.enter_context(nc.sbuf_tensor("projp", [33, T], F32))
            bc16 = st1.enter_context(nc.sbuf_tensor("bc16", [32, T], F16))
            Bp_g = st1.enter_context(nc.sbuf_tensor("Bp_g", [128, 8 * T], F16))
            Cp_g = st1.enter_context(nc.sbuf_tensor("Cp_g", [128, 8 * T], F16))
            dtin_bc = st1.enter_context(nc.sbuf_tensor("dtin_bc", [128, T], F32))
            rstd_bc = st1.enter_context(nc.sbuf_tensor("rstd_bc", [128, T], F32))
            nm_bc = st1.enter_context(nc.sbuf_tensor("nm_bc", [128, T], F32))
            dA2 = st1.enter_context(nc.sbuf_tensor("dA2", [128, 2 * T], F32))
            xs3 = st1.enter_context(nc.sbuf_tensor("xs3", [128, 3 * T], BF16))
            tmp2 = st1.enter_context(nc.sbuf_tensor("tmp2", [128, 2 * T], BF16))
            h_g8 = st1.enter_context(nc.sbuf_tensor("h_g8", [128, 8 * T], F32))
            y_sb = st1.enter_context(nc.sbuf_tensor("y_sb", [128, CB * T], F32))
            yst16 = st1.enter_context(nc.sbuf_tensor("yst16", [128, T], BF16))
            y2st16 = st1.enter_context(nc.sbuf_tensor("y2st16", [128, T], BF16))
            lnB1 = st1.enter_context(nc.sbuf_tensor("lnB1", [128, 2 * T], F32))
            ygc = st1.enter_context(nc.sbuf_tensor("ygc", [128, 2 * T], F16))
            ssum = st1.enter_context(nc.sbuf_tensor("ssum", [1, T], F32))
            s2sum = st1.enter_context(nc.sbuf_tensor("s2sum", [1, T], F32))
            sp0 = st1.enter_context(nc.sbuf_tensor("sp0", [1, T], F32))
            sp1 = st1.enter_context(nc.sbuf_tensor("sp1", [1, T], F32))
            murow = st1.enter_context(nc.sbuf_tensor("murow", [1, T], F32))
            varow = st1.enter_context(nc.sbuf_tensor("varow", [1, T], F32))
            rstdrow = st1.enter_context(nc.sbuf_tensor("rstdrow", [1, T], F32))
            nmrow = st1.enter_context(nc.sbuf_tensor("nmrow", [1, T], F32))
            pb0 = st1.enter_context(nc.psum_tensor("pb0", [128, T], F32))
            pb1 = st1.enter_context(nc.psum_tensor("pb1", [128, T], F32))
            pb2 = st1.enter_context(nc.psum_tensor("pb2", [128, T], F32))
            pb3 = st1.enter_context(nc.psum_tensor("pb3", [128, T], F32))
            pb4 = st1.enter_context(nc.psum_tensor("pb4", [128, T], F32))
            pb5 = st1.enter_context(nc.psum_tensor("pb5", [128, T], F32))
            pb6 = st1.enter_context(nc.psum_tensor("pb6", [128, T], F32))
            pbt = st1.enter_context(nc.psum_tensor("pbt", [128, 1024], F16))
            blk1 = st1.enter_context(nc.Block())
            pxb = [pb0, pb1]
            pz = [pb2, pb3]
            pby = [pb5, pb6]

            @blk1.sync
            def _(sy):
                pid = sy.partition_id()
                parv = sy.scalar_reg_alu(ALU.bitwise_and, pid, 1)
                with sy.register("c") as rc, sy.register("rr") as rreg:
                    def poll(par, rd):
                        sy.reg_mov(rc, 1)
                        with sy.While(rc):
                            sy.load(rreg, sh_flag.ap()[1 - par:2 - par, rd:rd + 1])
                            sy.reg_alu(rc, rreg, MAGIC + rd, ALU.not_equal)

                    for par in sy.Switch(parv, 2):
                        nw = 0
                        for kk in range(CB):
                            if kk >= 2:
                                sy.wait_ge(g("su"), kk - 1)
                            sy.dma_start(out=y_sb[:, (kk % 2) * 2048:(kk % 2) * 2048 + 2048],
                                         in_=wipT_in[kk * 128:(kk + 1) * 128, :]).then_inc(g(f"s_w{kk % 2}"), 16)
                        sy.wait_ge(g("su"), 7)
                        sy.dma_start(out=y_sb[:, 0:CB * 33],
                                     in_=xpwT_in).then_inc(g("s_wp"), 16)
                        for dst, src in [(Av, alog_in), (cwv, cw_in), (cbv, cb_in), (Dv, dp_in),
                                         (dtwv, dtw_in), (dtbv, dtb_in), (lngv, lng_in), (lnbv, lnb_in)]:
                            sy.dma_start(out=dst[:, :], in_=src).then_inc(g("s_wp"), 16)
                        for tg in range(TT4):
                            if tg >= 2:
                                sy.wait_ge(g("v_xn"), tg - 1)
                            sy.dma_start(out=x_tm[:, (tg % 2) * 1024:(tg % 2) * 1024 + 1024],
                                         in_=x_in[tg * 128:(tg + 1) * 128, :]).then_inc(g(f"s_x{tg % 2}"), 16)

                        sy.wait_ge(g("o_xs"), 1)
                        for k in range(K1):
                            sy.wait_ge(g("v_pv"), k + 1)
                            sy.dma_start(out=sh_proj.ap()[par, k * 33:(k + 1) * 33, :],
                                         in_=proj[:, :]).then_inc(g("s_pw"), 16)
                            sy.wait_ge(g("s_pw"), 16 * (k + 1))
                            rd = R_PROJ(k)
                            sy.dma_start(out=sh_flag.ap()[par:par + 1, rd:rd + 1],
                                         in_=flagv[0:1, rd:rd + 1]).then_inc(g("s_fl"), 16)
                            poll(par, rd)
                            sy.wait_ge(g("v_pj"), k)
                            sy.dma_start(out=projp[:, :],
                                         in_=sh_proj.ap()[1 - par, k * 33:(k + 1) * 33, :]).then_inc(g("s_pr"), 16)
                            if k > 0:
                                sy.wait_ge(g("s_bcb"), 16 * 2 * k)
                                sy.wait_ge(g("s_bcc"), 16 * 2 * k)
                                sy.wait_ge(g("s_bcd"), 16 * k)
                            sy.wait_ge(g("v_bc"), k + 1)
                            sy.dma_start(out=d_bc.ap(), in_=bc16[:, :]).then_inc(g("s_sg"), 16)
                            sy.dma_start(out=d_dtin.ap(), in_=proj[32:33, :]).then_inc(g("s_sg"), 16)
                            sy.wait_ge(g("s_sg"), SG_CNT(k, 2))
                            sy.wait_ge(g("v_t1"), 8 * k)
                            sy.dma_start(out=dtin_bc[:, :],
                                         in_=d_dtin.ap()[0:1, :].partition_broadcast(128)).then_inc(g("s_bcd"), 16)
                            for gg in range(2):
                                sy.wait_ge(g("v_ym"), max(0, 128 * k + 64 * gg) if (k > 0 or gg > 0) else 0)
                                sy.dma_start(out=Bp_g[:, :],
                                             in_=d_bc.ap()[8 * gg:8 * gg + 8, :]
                                             .partition_broadcast(128)).then_inc(g("s_bcb"), 16)
                                sy.wait_ge(g("o_xs"), max(0, 128 * k + 64 * gg + 1) if (k > 0 or gg > 0) else 0)
                                sy.dma_start(out=Cp_g[:, :],
                                             in_=d_bc.ap()[16 + 8 * gg:16 + 8 * gg + 8, :]
                                             .partition_broadcast(128)).then_inc(g("s_bcc"), 16)
                            if k + 1 < K1:
                                for t in range(2):
                                    tg = (k + 1) * TT4 + t
                                    sy.wait_ge(g("v_xn"), max(0, tg - 1))
                                    sy.dma_start(out=x_tm[:, (tg % 2) * 1024:(tg % 2) * 1024 + 1024],
                                                 in_=x_in[tg * 128:(tg + 1) * 128, :]).then_inc(g(f"s_x{tg % 2}"), 16)
                            sy.wait_ge(g("v_se"), k + 1)
                            sy.dma_start(out=sh_st.ap()[par, 2 * k:2 * k + 1, :],
                                         in_=ssum[:, :]).then_inc(g("s_sw"), 16)
                            sy.dma_start(out=sh_st.ap()[par, 2 * k + 1:2 * k + 2, :],
                                         in_=s2sum[:, :]).then_inc(g("s_sw"), 16)
                            sy.wait_ge(g("s_sw"), 32 * (k + 1))
                            rd = R_STAT(k)
                            sy.dma_start(out=sh_flag.ap()[par:par + 1, rd:rd + 1],
                                         in_=flagv[0:1, rd:rd + 1]).then_inc(g("s_fl"), 16)
                            poll(par, rd)
                            sy.wait_ge(g("v_mu"), k)
                            sy.wait_ge(g("a_rw"), k)
                            sy.dma_start(out=sp0[:, :],
                                         in_=sh_st.ap()[1 - par, 2 * k:2 * k + 1, :]).then_inc(g("s_sr"), 16)
                            sy.dma_start(out=sp1[:, :],
                                         in_=sh_st.ap()[1 - par, 2 * k + 1:2 * k + 2, :]).then_inc(g("s_sr"), 16)
                            if k + 1 < K1:
                                for t in range(2, TT4):
                                    tg = (k + 1) * TT4 + t
                                    sy.wait_ge(g("v_xn"), max(0, tg - 1))
                                    sy.dma_start(out=x_tm[:, (tg % 2) * 1024:(tg % 2) * 1024 + 1024],
                                                 in_=x_in[tg * 128:(tg + 1) * 128, :]).then_inc(g(f"s_x{tg % 2}"), 16)
                            sy.wait_ge(g("v_nm"), k + 1)
                            if k > 0:
                                sy.wait_ge(g("s_bcr"), 32 * k)
                            sy.dma_start(out=d_rows.ap()[0:1, :], in_=rstdrow[:, :]).then_inc(g("s_sg"), 16)
                            sy.dma_start(out=d_rows.ap()[1:2, :], in_=nmrow[:, :]).then_inc(g("s_sg"), 16)
                            sy.wait_ge(g("s_sg"), SG_CNT(k, 4))
                            sy.wait_ge(g("v_yg"), 8 * k)
                            sy.dma_start(out=rstd_bc[:, :],
                                         in_=d_rows.ap()[0:1, :].partition_broadcast(128)).then_inc(g("s_bcr"), 16)
                            sy.dma_start(out=nm_bc[:, :],
                                         in_=d_rows.ap()[1:2, :].partition_broadcast(128)).then_inc(g("s_bcr"), 16)
                            for cb in range(CB):
                                sy.wait_ge(g("v_yg"), 8 * k + cb + 1)
                                sy.dma_start(out=sh_yg.ap()[par, :, cb * L + k * T: cb * L + (k + 1) * T],
                                             in_=ygc[:, (cb % 2) * T:(cb % 2) * T + T]).then_inc(g(f"s_yw{cb % 2}"), 16)
                            sy.wait_ge(g("s_yw0"), 16 * 4 * (k + 1))
                            sy.wait_ge(g("s_yw1"), 16 * 4 * (k + 1))
                            rd = R_YG(k)
                            sy.dma_start(out=sh_flag.ap()[par:par + 1, rd:rd + 1],
                                         in_=flagv[0:1, rd:rd + 1]).then_inc(g("s_fl"), 16)

                            if dbg and k == 0:
                                sy.dma_start(out=dbg_o["dbg_y"], in_=y_sb[:, :]).then_inc(g("s_fl"), 16)
                                sy.dma_start(out=dbg_o["dbg_proj"], in_=proj[:, :]).then_inc(g("s_fl"), 16)
                                sy.dma_start(out=dbg_o["dbg_dt"], in_=dt_sb[:, :]).then_inc(g("s_fl"), 16)
                                sy.dma_start(out=dbg_o["dbg_xb"], in_=xb16[:, :]).then_inc(g("s_fl"), 16)
                                sy.dma_start(out=dbg_o["dbg_u"], in_=u16[:, :]).then_inc(g("s_fl"), 16)
                                sy.dma_start(out=dbg_o["dbg_sz"], in_=sz16[:, :]).then_inc(g("s_fl"), 16)
                                sy.dma_start(out=dbg_o["dbg_st"][0:1, :], in_=ssum[:, :]).then_inc(g("s_fl"), 16)
                                sy.dma_start(out=dbg_o["dbg_st"][1:2, :], in_=s2sum[:, :]).then_inc(g("s_fl"), 16)
                            if dbg and k == 1:
                                for cbq in range(CB):
                                    sy.dma_start(out=dbg_o["dbg_yg"][:, cbq * T:(cbq + 1) * T],
                                                 in_=sh_yg.ap()[par, :, cbq * L:cbq * L + T]).then_inc(g("s_fl"), 16)

            @blk1.vector
            def _(v):
                CV = [0]

                def vm(inst):
                    inst.then_inc(g("c_v"), 1)
                    CV[0] += 1
                    return CV[0]

                def vflush():
                    v.wait_ge(g("c_v"), CV[0])

                for kk in range(CB):
                    v.wait_ge(g(f"s_w{kk % 2}"), 16 * (kk // 2 + 1))
                    if kk:
                        v.wait_ge(g("su"), kk)
                    v.tensor_copy(out=wipT16[:, kk * 2048:(kk + 1) * 2048],
                                  in_=y_sb[:, (kk % 2) * 2048:(kk % 2) * 2048 + 2048]).then_inc(g("su"), 1)
                v.wait_ge(g("s_wp"), 16 * 9)
                v.wait_ge(g("su"), 8)
                v.tensor_copy(out=xpwT16[:, :], in_=y_sb[:, 0:CB * 33]).then_inc(g("su"), 1)
                v.wait_ge(g("a_rs"), 1)
                v.wait_ge(g("su"), 9)
                v.tensor_scalar(out=Av[:, :], in0=spt[:, 0:CB * NS], scalar1=-1.0,
                                scalar2=None, op0=ALU.mult).then_inc(g("su"), 1)
                v.wait_ge(g("o_xs"), 1)
                vm(v.tensor_scalar(out=idbf[:, :], in0=iotai[:, :], scalar1=0, scalar2=None,
                                   op0=ALU.is_equal))
                vm(v.tensor_scalar(out=id16[:, :], in0=iotai[:, :], scalar1=0, scalar2=None,
                                   op0=ALU.is_equal))
                vm(v.tensor_scalar(out=idf32[:, :], in0=iotai[:, :], scalar1=0, scalar2=None,
                                   op0=ALU.is_equal))
                vm(v.memset(state[:, :], 0.0))
                vm(v.memset(tails[:, :], 0.0))
                v.wait_ge(g("su"), 10)
                v.nop().then_inc(g("su"), 1)

                def te_evac(tg):
                    k_of = tg // TT4
                    t_of = tg % TT4
                    v.wait_ge(g("p_tp"), tg + 1)
                    v.wait_ge(g("p_ip"), 16 * k_of)
                    v.tensor_copy(
                        out=xn_fm.rearrange("p (db l) -> p db l", db=CB)[:, :, t_of * 128:(t_of + 1) * 128],
                        in_=pbt.rearrange("p (db l) -> p db l", db=CB)[:, :, :]).then_inc(g("v_te"), 1)

                def phaseA(k):
                    for t in range(TT4):
                        tg = k * TT4 + t
                        sl = (tg % 2) * 1024
                        ms = (tg % 2) * 2
                        v.wait_ge(g(f"s_x{tg % 2}"), 16 * (tg // 2 + 1))
                        vm(v.bn_stats(out=bn6[:, 0:6], in_=x_tm[:, sl:sl + 512]))
                        v.bn_stats(out=bn6[:, 6:12], in_=x_tm[:, sl + 512:sl + 1024]).then_inc(g("v_st"), 1)
                        v.wait_ge(g("v_st"), 2 * tg + 1)
                        vflush()
                        if tg >= 2:
                            v.wait_ge(g("v_xn"), tg - 1)
                            v.wait_ge(g("a_rs"), tg)
                        v.bn_aggr(out=mv[:, ms:ms + 2], in_=bn6[:, :]).then_inc(g("v_st"), 1)
                        v.wait_ge(g("a_rs"), tg + 2)
                        v.wait_ge(g("p_tp"), max(0, tg - 1))
                        v.tensor_scalar(out=xn16[:, sl:sl + 1024], in0=x_tm[:, sl:sl + 1024],
                                        scalar1=mv[:, ms:ms + 1], scalar2=rstd1[:, (tg % 2):(tg % 2) + 1],
                                        op0=ALU.subtract, op1=ALU.mult).then_inc(g("v_xn"), 1)
                        if t >= 1:
                            te_evac(tg - 1)
                    te_evac(k * TT4 + TT4 - 1)

                phaseA(0)
                tail_mark = {}
                u_marks = {}
                for k in range(K1):
                    if k == 0:
                        v.wait_ge(g("su"), 11)
                        vflush()
                    for q in range(4):
                        ca, cbn = 2 * q, 2 * q + 1
                        v.wait_ge(g("p_ip"), 16 * k + 2 * q + 2)
                        v.wait_ge(g("a_xb"), max(0, 8 * k + 2 * q))
                        for j in [3, 2, 1, 0]:
                            off = 3 - j
                            if j < 3:
                                vflush()
                            for cc, pp in [(ca, pxb[0]), (cbn, pxb[1])]:
                                accs = (cc % 2) * T
                                wcol = cwv[:, cc * 4 + j:cc * 4 + j + 1]
                                if j == 3:
                                    vm(v.tensor_scalar(out=cacc[:, accs:accs + T], in0=pp[:, :],
                                                       scalar1=wcol, scalar2=None, op0=ALU.mult))
                                else:
                                    vm(v.scalar_tensor_tensor(out=cacc[:, accs + off:accs + T],
                                                              in0=pp[:, 0:T - off], scalar=wcol,
                                                              in1=cacc[:, accs + off:accs + T],
                                                              op0=ALU.mult, op1=ALU.add))
                                    vm(v.scalar_tensor_tensor(out=cacc[:, accs:accs + off],
                                                              in0=tails[:, cc * 3 + j:cc * 3 + j + off],
                                                              scalar=wcol,
                                                              in1=cacc[:, accs:accs + off],
                                                              op0=ALU.mult, op1=ALU.add))
                        vflush()
                        for cc, pp in [(ca, pxb[0]), (cbn, pxb[1])]:
                            v.tensor_copy(out=tails[:, cc * 3:cc * 3 + 3],
                                          in_=pp[:, T - 3:T]).then_inc(g("v_cv"), 1)
                    v.wait_ge(g("p_xp"), k + 1)
                    v.wait_ge(g("s_sg"), SG_CNT(k - 1, 2) if k > 0 else 0)
                    v.tensor_copy(out=proj[:, :], in_=pb4[0:33, :]).then_inc(g("v_pv"), 1)
                    v.wait_ge(g("s_pr"), 16 * (k + 1))
                    v.wait_ge(g("v_pv"), k + 1)
                    v.tensor_tensor(out=proj[:, :], in0=proj[:, :], in1=projp[:, :],
                                    op=ALU.add).then_inc(g("v_pj"), 1)
                    v.wait_ge(g("v_pj"), k + 1)
                    v.tensor_copy(out=bc16[:, :], in_=proj[0:32, :]).then_inc(g("v_bc"), 1)
                    for q in range(4):
                        for cc in (2 * q, 2 * q + 1):
                            ss = (cc % 2) * T
                            v.wait_ge(g("s_bcd"), 16 * (k + 1))
                            v.wait_ge(g("a_sp"), max(0, 8 * k + cc - 1))
                            vm(v.tensor_scalar(out=spt[:, ss:ss + T], in0=dtin_bc[:, :],
                                               scalar1=dtwv[:, cc:cc + 1], scalar2=dtbv[:, cc:cc + 1],
                                               op0=ALU.mult, op1=ALU.add))
                        vflush()
                        for cc in (2 * q, 2 * q + 1):
                            ss = (cc % 2) * T
                            v.scalar_tensor_tensor(out=spe[:, ss:ss + T], in0=spt[:, ss:ss + T],
                                                   scalar=-1.0, in1=spt[:, ss:ss + T],
                                                   op0=ALU.mult, op1=ALU.max).then_inc(g("v_t1"), 1)
                        for cc in (2 * q, 2 * q + 1):
                            ss = (cc % 2) * T
                            v.wait_ge(g("a_sp"), 8 * k + cc + 1)
                            if k > 0:
                                v.wait_ge(g("a_dA"), 128 * (k - 1) + 64 + 8 * cc + 8)
                            v.scalar_tensor_tensor(out=dt_sb[:, cc * T:(cc + 1) * T],
                                                   in0=spt[:, ss:ss + T], scalar=0.0,
                                                   in1=spe[:, ss:ss + T],
                                                   op0=ALU.max, op1=ALU.add).then_inc(g("v_dt"), 1)
                        for cc in (2 * q, 2 * q + 1):
                            v.wait_ge(g("a_xb"), 8 * k + cc + 1)
                            v.wait_ge(g("v_dt"), 8 * k + cc + 1)
                            if k > 0:
                                v.wait_ge(g("v_ym"), 128 * (k - 1) + 64 + 8 * cc + 8)
                            u_marks[cc] = vm(v.tensor_tensor(
                                out=u16[:, cc * T:(cc + 1) * T],
                                in0=xb16[:, cc * T:(cc + 1) * T],
                                in1=dt_sb[:, cc * T:(cc + 1) * T],
                                op=ALU.mult))
                            v.nop().then_inc(g("v_u"), 1)
                    vflush()
                    prev = None
                    xs_marks = {}

                    def gather(gcb, gg):
                        v.wait_ge(g("v_ts"), 128 * k + 64 * gg + 8 * gcb + 8)
                        vm(v.tensor_copy(
                            out=state[:, gcb * 16 + 8 * gg:gcb * 16 + 8 * gg + 8],
                            in_=h_g8.rearrange("p (n t) -> p n t", t=T)[:, :, T - 1:T]))

                    def emit_xs(gg, cb, nn, i):
                        v.wait_ge(g("s_bcb"), 16 * (2 * k + gg + 1))
                        v.wait_ge(g("v_ts"), max(0, i - 2))
                        if nn == 0:
                            v.wait_ge(g("c_v"), u_marks[cb])
                        xs_marks[i] = vm(v.tensor_tensor(
                            out=xs3[:, (i % 3) * T:(i % 3) * T + T],
                            in0=u16[:, cb * T:(cb + 1) * T],
                            in1=Bp_g[:, nn * T:nn * T + T],
                            op=ALU.mult))
                        v.nop().then_inc(g("v_ym"), 1)

                    def emit_tts(gg, cb, nn, i):
                        v.wait_ge(g("a_dA"), i + 1)
                        v.wait_ge(g("c_v"), xs_marks.pop(i))
                        v.wait_ge(g("o_xs"), max(0, i - 6))
                        v.tensor_tensor_scan(
                            out=h_g8[:, nn * T:nn * T + T],
                            data0=dA2[:, (i % 2) * T:(i % 2) * T + T],
                            data1=xs3[:, (i % 3) * T:(i % 3) * T + T],
                            initial=state[:, cb * 16 + 8 * gg + nn:cb * 16 + 8 * gg + nn + 1],
                            op0=ALU.mult, op1=ALU.add).then_inc(g("v_ts"), 1)

                    def emit_yev(gg, cb):
                        v.wait_ge(g("p_ya"), 128 * k + 64 * gg + 8 * cb + 8)
                        if gg == 0:
                            if k > 0:
                                v.wait_ge(g("a_y2"), 8 * (k - 1) + cb + 1)
                            v.scalar_tensor_tensor(out=y_sb[:, cb * T:(cb + 1) * T],
                                                   in0=xb16[:, cb * T:(cb + 1) * T],
                                                   scalar=Dv[:, cb:cb + 1],
                                                   in1=pby[cb % 2][:, :],
                                                   op0=ALU.mult, op1=ALU.add).then_inc(g("v_ye"), 1)
                        else:
                            v.wait_ge(g("v_ye"), 16 * k + cb + 1)
                            v.tensor_tensor(out=y_sb[:, cb * T:(cb + 1) * T],
                                            in0=y_sb[:, cb * T:(cb + 1) * T],
                                            in1=pby[cb % 2][:, :],
                                            op=ALU.add).then_inc(g("v_ye"), 1)

                    seq = [(gg, cb, nn, 128 * k + 64 * gg + 8 * cb + nn)
                           for gg in range(2) for cb in range(CB) for nn in range(8)]
                    for item in seq:
                        gg, cb, nn, i = item
                        if nn == 0 and prev is not None:
                            pgg, pcb = prev[0], prev[1]
                            emit_xs(*item)
                            emit_tts(*prev)
                            prev = None
                            gather(pcb, pgg)
                            emit_yev(pgg, pcb)
                            vflush()
                        else:
                            emit_xs(*item)
                            if prev is not None:
                                emit_tts(*prev)
                        prev = item
                    emit_tts(*prev)
                    gather(CB - 1, 1)
                    emit_yev(1, CB - 1)
                    prev = None
                    for cb in range(CB):
                        v.wait_ge(g("v_ye"), 16 * k + 8 + cb + 1)
                        v.wait_ge(g("p_st"), max(0, 8 * k + cb))
                        v.tensor_copy(out=yst16[:, :],
                                      in_=y_sb[:, cb * T:(cb + 1) * T]).then_inc(g("v_yf"), 1)
                    v.wait_ge(g("p_st"), 8 * (k + 1))
                    v.wait_ge(g("s_sw"), 32 * k)
                    vm(v.tensor_copy(out=ssum[:, :], in_=pb4[0:1, :]))
                    v.tensor_copy(out=s2sum[:, :], in_=pb2[0:1, :]).then_inc(g("v_se"), 1)
                    if k + 1 < K1:
                        phaseA(k + 1)
                    v.wait_ge(g("s_sr"), 32 * (k + 1))
                    v.wait_ge(g("v_se"), k + 1)
                    vflush()
                    vm(v.tensor_tensor(out=ssum[:, :], in0=ssum[:, :], in1=sp0[:, :], op=ALU.add))
                    vm(v.tensor_tensor(out=s2sum[:, :], in0=s2sum[:, :], in1=sp1[:, :], op=ALU.add))
                    vflush()
                    vm(v.tensor_scalar(out=murow[:, :], in0=ssum[:, :], scalar1=INV2048,
                                       scalar2=None, op0=ALU.mult))
                    vm(v.tensor_scalar(out=varow[:, :], in0=s2sum[:, :], scalar1=INV2048,
                                       scalar2=None, op0=ALU.mult))
                    vflush()
                    vm(v.tensor_tensor(out=sp0[:, :], in0=murow[:, :], in1=murow[:, :], op=ALU.mult))
                    vflush()
                    v.tensor_tensor(out=varow[:, :], in0=varow[:, :], in1=sp0[:, :],
                                    op=ALU.subtract).then_inc(g("v_mu"), 1)
                    v.wait_ge(g("a_rw"), k + 1)
                    v.tensor_tensor(out=nmrow[:, :], in0=murow[:, :], in1=rstdrow[:, :],
                                    op=ALU.mult).then_inc(g("v_nm"), 1)
                    for q in range(4):
                        pair = (2 * q, 2 * q + 1)
                        for cc in pair:
                            v.wait_ge(g("s_bcr"), 32 * (k + 1))
                            v.wait_ge(g(f"s_yw{cc % 2}"), 16 * max(0, 4 * k + cc // 2))
                            vm(v.tensor_tensor(out=lnB1[:, (cc % 2) * T:(cc % 2) * T + T],
                                               in0=y_sb[:, cc * T:(cc + 1) * T],
                                               in1=rstd_bc[:, :], op=ALU.mult))
                        vflush()
                        for cc in pair:
                            vm(v.tensor_tensor(out=cacc[:, (cc % 2) * T:(cc % 2) * T + T],
                                               in0=lnB1[:, (cc % 2) * T:(cc % 2) * T + T],
                                               in1=nm_bc[:, :], op=ALU.subtract))
                        vflush()
                        for cc in pair:
                            vm(v.tensor_scalar(out=lnB1[:, (cc % 2) * T:(cc % 2) * T + T],
                                               in0=cacc[:, (cc % 2) * T:(cc % 2) * T + T],
                                               scalar1=lngv[:, cc:cc + 1], scalar2=lnbv[:, cc:cc + 1],
                                               op0=ALU.mult, op1=ALU.add))
                        vflush()
                        for cc in pair:
                            v.wait_ge(g("a_z"), 8 * k + cc + 1)
                            v.tensor_tensor(out=ygc[:, (cc % 2) * T:(cc % 2) * T + T],
                                            in0=lnB1[:, (cc % 2) * T:(cc % 2) * T + T],
                                            in1=sz16[:, cc * T:(cc + 1) * T],
                                            op=ALU.mult).then_inc(g("v_yg"), 1)

            @blk1.scalar
            def _(a):
                CA = [0]

                def am(inst):
                    inst.then_inc(g("c_a"), 1)
                    CA[0] += 1

                def aflush():
                    a.wait_ge(g("c_a"), CA[0])

                a.wait_ge(g("o_xs"), 1)
                a.wait_ge(g("s_wp"), 16 * 9)
                a.activation(out=spt[:, 0:CB * NS], in_=Av[:, :], func=AF.Exp).then_inc(g("a_rs"), 1)

                def rstds(k):
                    for t in range(TT4):
                        tg = k * TT4 + t
                        a.wait_ge(g("v_st"), 2 * tg + 2)
                        a.wait_ge(g("v_xn"), max(0, tg - 1))
                        a.wait_ge(g("a_rs"), tg + 1)
                        a.activation(out=lnt[:, :], in_=mv[:, 2 * (tg % 2) + 1:2 * (tg % 2) + 2],
                                     func=AF.Ln, bias=cbias[:, 0:1]).then_inc(g("a_rs1"), 1)
                        a.wait_ge(g("a_rs1"), tg + 1)
                        a.activation(out=rstd1[:, (tg % 2):(tg % 2) + 1], in_=lnt[:, :], func=AF.Exp,
                                     scale=-0.5).then_inc(g("a_rs"), 1)

                rstds(0)
                a.wait_ge(g("su"), 10)
                for k in range(K1):
                    for cb in range(CB):
                        a.wait_ge(g("v_cv"), 8 * k + cb + 1)
                        if k > 0:
                            a.wait_ge(g("v_u"), 8 * (k - 1) + cb + 1)
                            a.wait_ge(g("p_xp"), k)
                            a.wait_ge(g("v_ye"), 16 * (k - 1) + cb + 1)
                        a.activation(out=xb16[:, cb * T:(cb + 1) * T],
                                     in_=cacc[:, (cb % 2) * T:(cb % 2) * T + T],
                                     func=AF.Silu, bias=cbv[:, cb:cb + 1]).then_inc(g("a_xb"), 1)
                    for zi in range(CB):
                        a.wait_ge(g("p_ip"), 16 * k + 8 + zi + 1)
                        if k > 0:
                            a.wait_ge(g("v_yg"), 8 * (k - 1) + zi + 1)
                        a.activation(out=sz16[:, zi * T:(zi + 1) * T], in_=pz[zi % 2][:, :],
                                     func=AF.Silu).then_inc(g("a_z"), 1)
                    for q in range(4):
                        for cc in (2 * q, 2 * q + 1):
                            ss = (cc % 2) * T
                            a.wait_ge(g("v_t1"), 8 * k + cc + 1)
                            a.wait_ge(g("a_sp"), max(0, 8 * k + cc - 1))
                            am(a.activation(out=spl[:, ss:ss + T], in_=spe[:, ss:ss + T],
                                            func=AF.Exp, scale=-1.0))
                        aflush()
                        for cc in (2 * q, 2 * q + 1):
                            ss = (cc % 2) * T
                            a.wait_ge(g("v_dt"), max(0, 8 * k + cc - 1))
                            a.activation(out=spe[:, ss:ss + T], in_=spl[:, ss:ss + T],
                                         func=AF.Ln, bias=cbias[:, 1:2]).then_inc(g("a_sp"), 1)
                    for gg in range(2):
                        for cb in range(CB):
                            for nn in range(8):
                                i = 128 * k + 64 * gg + 8 * cb + nn
                                n = 8 * gg + nn
                                a.wait_ge(g("v_dt"), 8 * k + cb + 1)
                                a.wait_ge(g("v_ts"), max(0, i - 1))
                                a.activation(out=dA2[:, (i % 2) * T:(i % 2) * T + T],
                                             in_=dt_sb[:, cb * T:(cb + 1) * T], func=AF.Exp,
                                             scale=Av[:, cb * NS + n:cb * NS + n + 1]).then_inc(g("a_dA"), 1)
                    for cb in range(CB):
                        a.wait_ge(g("v_ye"), 16 * k + 8 + cb + 1)
                        a.wait_ge(g("p_st"), max(0, 8 * k + cb))
                        a.activation(out=y2st16[:, :],
                                     in_=y_sb[:, cb * T:(cb + 1) * T],
                                     func=AF.Square).then_inc(g("a_y2"), 1)
                    if k + 1 < K1:
                        rstds(k + 1)
                    a.wait_ge(g("v_mu"), k + 1)
                    if k > 0:
                        a.wait_ge(g("s_sg"), SG_CNT(k - 1, 4))
                    a.activation(out=sp0[:, :], in_=varow[:, :], func=AF.Ln,
                                 bias=cbias[0:1, 0:1]).then_inc(g("a_rw1"), 1)
                    a.wait_ge(g("a_rw1"), k + 1)
                    a.activation(out=rstdrow[:, :], in_=sp0[:, :], func=AF.Exp,
                                 scale=-0.5).then_inc(g("a_rw"), 1)

            @blk1.gpsimd
            def _(p):
                p.iota(out=iotai[:, :], pattern=[[1, 128]], base=0, channel_multiplier=-1)
                p.memset(ones16c[:, :], 1.0)
                p.memset(onesbfc[:, :], 1.0)
                p.memset(onesf32c[:, :], 1.0)
                p.memset(onesf32r[:, :], 1.0)
                p.memset(cbias[:, 0:1], EPS_LN)
                p.memset(cbias[:, 1:2], 1.0)
                p.memset(cbias[:, 2:3], EPS_Q)
                p.iota(out=flagv[:, :], pattern=[[1, 64]], base=MAGIC,
                       channel_multiplier=0).then_inc(g("o_xs"), 1)
                for k in range(K1):
                    for gg in range(2):
                        for cb in range(CB):
                            for nn in range(8):
                                i = 128 * k + 64 * gg + 8 * cb + nn
                                p.wait_ge(g("v_ts"), i + 1)
                                p.wait_ge(g("s_bcc"), 16 * (2 * k + gg + 1))
                                p.wait_ge(g("p_ya"), max(0, i - 1))
                                p.tensor_tensor(out=tmp2[:, (i % 2) * T:(i % 2) * T + T],
                                                in0=h_g8[:, nn * T:nn * T + T],
                                                in1=Cp_g[:, nn * T:nn * T + T],
                                                op=ALU.mult).then_inc(g("o_xs"), 1)

            @blk1.tensor
            def _(te):
                te.wait_ge(g("c_v"), 5)
                te.wait_ge(g("o_xs"), 1)

                def transposes(k):
                    for t in range(TT4):
                        tg = k * TT4 + t
                        te.wait_ge(g("v_xn"), tg + 1)
                        te.wait_ge(g("v_te"), tg)
                        for db in range(CB):
                            mm = te.transpose(out=pbt[:, db * 128:(db + 1) * 128],
                                              in_=xn16[:, (tg % 2) * 1024 + db * 128:(tg % 2) * 1024 + (db + 1) * 128],
                                              identity=id16[:, :])
                        mm.then_inc(g("p_tp"), 1)

                def in_proj_part(k, mis):
                    te.wait_ge(g("su"), 11)
                    te.wait_ge(g("v_te"), 4 * (k + 1))
                    for mi in mis:
                        if mi < 8:
                            te.wait_ge(g("v_cv"), max(0, 8 * k + mi - 1))
                            bank = pxb[mi % 2]
                        else:
                            te.wait_ge(g("a_z"), max(0, 8 * k + (mi - 8) - 1))
                            te.wait_ge(g("v_se"), k)
                            bank = pz[mi % 2]
                        for kk in range(CB):
                            mm = te.matmul(out=bank[:, :],
                                           lhsT=wipT16[:, kk * 2048 + mi * 128:kk * 2048 + (mi + 1) * 128],
                                           rhs=xn_fm[:, kk * T:(kk + 1) * T],
                                           start=(kk == 0), stop=(kk == CB - 1))
                        mm.then_inc(g("p_ip"), 1)

                transposes(0)
                for k in range(K1):
                    in_proj_part(k, range(16) if k == 0 else range(2, 16))
                    te.wait_ge(g("v_pv"), k)
                    te.wait_ge(g("v_se"), k)
                    for cb in range(CB):
                        te.wait_ge(g("a_xb"), 8 * k + cb + 1)
                        mm = te.matmul(out=pb4[0:33, :],
                                       lhsT=xpwT16[:, cb * 33:(cb + 1) * 33],
                                       rhs=xb16[:, cb * T:(cb + 1) * T],
                                       start=(cb == 0), stop=(cb == CB - 1))
                    mm.then_inc(g("p_xp"), 1)
                    for gg in range(2):
                        for cb in range(CB):
                            for nn in range(8):
                                i = 128 * k + 64 * gg + 8 * cb + nn
                                te.wait_ge(g("o_xs"), i + 2)
                                if nn == 0:
                                    te.wait_ge(g("v_ye"), max(0, 16 * k + 8 * gg + cb - 1))
                                te.matmul(out=pby[cb % 2][:, :], lhsT=idbf[:, :],
                                          rhs=tmp2[:, (i % 2) * T:(i % 2) * T + T],
                                          start=(nn == 0), stop=(nn == 7)).then_inc(g("p_ya"), 1)
                    for cb in range(CB):
                        te.wait_ge(g("v_yf"), 8 * k + cb + 1)
                        te.wait_ge(g("a_y2"), 8 * k + cb + 1)
                        if cb == 0:
                            te.wait_ge(g("v_pv"), k + 1)
                            te.wait_ge(g("a_z"), 8 * (k + 1))
                            te.wait_ge(g("v_se"), k)
                        te.matmul(out=pb4[0:1, :], lhsT=onesbfc[:, :],
                                  rhs=yst16[:, :],
                                  start=(cb == 0), stop=(cb == CB - 1))
                        te.matmul(out=pb2[0:1, :], lhsT=onesbfc[:, :],
                                  rhs=y2st16[:, :],
                                  start=(cb == 0), stop=(cb == CB - 1)).then_inc(g("p_st"), 1)
                    if k + 1 < K1:
                        transposes(k + 1)
                        in_proj_part(k + 1, range(2))

        st2 = contextlib.ExitStack()
        with st2:
            owT16 = st2.enter_context(nc.sbuf_tensor("owT16", [128, 16 * D], F16))
            wq16 = st2.enter_context(nc.sbuf_tensor("wq16", [128, CB * D], F16))
            wqT16 = st2.enter_context(nc.sbuf_tensor("wqT16", [128, CB * D], F16))
            wld2 = st2.enter_context(nc.sbuf_tensor("wld2", [128, 2 * D], F32))
            abscols = st2.enter_context(nc.sbuf_tensor("abscols", [128, 9], F32))
            x_tm2 = st2.enter_context(nc.sbuf_tensor("x_tm2", [128, 4 * D], F32))
            bn62 = st2.enter_context(nc.sbuf_tensor("bn62", [128, 12], F32))
            mv2 = st2.enter_context(nc.sbuf_tensor("mv2", [128, 2 * 2], F32))
            rstd2 = st2.enter_context(nc.sbuf_tensor("rstd2", [128, 2], F32))
            lnt2 = st2.enter_context(nc.sbuf_tensor("lnt2", [128, 2], F32))
            xn2 = st2.enter_context(nc.sbuf_tensor("xn2", [128, 2 * D], F32))
            asc4 = st2.enter_context(nc.sbuf_tensor("asc4", [128, 4], F32))
            arec = st2.enter_context(nc.sbuf_tensor("arec", [128, 2], F32))
            zq2 = st2.enter_context(nc.sbuf_tensor("zq2", [128, 2 * D], F32))
            aqi2 = st2.enter_context(nc.sbuf_tensor("aqi2", [128, 2 * D], I32))
            aq_tm = st2.enter_context(nc.sbuf_tensor("aq_tm", [128, 4 * D], F16))
            aq_fm = st2.enter_context(nc.sbuf_tensor("aq_fm", [128, CB * T], F16))
            ascrow = st2.enter_context(nc.sbuf_tensor("ascrow", [1, T], F32))
            scrow = st2.enter_context(nc.sbuf_tensor("scrow", [1, T], F32))
            sc_bc = st2.enter_context(nc.sbuf_tensor("sc_bc", [128, T], F32))
            gbuf = st2.enter_context(nc.sbuf_tensor("gbuf", [128, 2 * T], F32))
            g_ffn = st2.enter_context(nc.sbuf_tensor("g_ffn", [128, CB * T], F32))
            yg_in = st2.enter_context(nc.sbuf_tensor("yg_in", [128, 16 * T], F16))
            sg_sb = st2.enter_context(nc.sbuf_tensor("sg_sb", [128, CB * T], F32))
            outt = st2.enter_context(nc.sbuf_tensor("outt", [128, 2 * D], F32))
            pc0 = st2.enter_context(nc.psum_tensor("pc0", [128, T], F32))
            pc1 = st2.enter_context(nc.psum_tensor("pc1", [128, T], F32))
            pc2 = st2.enter_context(nc.psum_tensor("pc2", [128, T], F32))
            pc3 = st2.enter_context(nc.psum_tensor("pc3", [128, T], F32))
            pct = st2.enter_context(nc.psum_tensor("pct", [128, 1024], F16))
            pcg = st2.enter_context(nc.psum_tensor("pcg", [128, 1024], F32))
            pcs = st2.enter_context(nc.psum_tensor("pcs", [128, 4], F32))
            blk2 = st2.enter_context(nc.Block())
            pf = [pc0, pc1]
            po = [pc2, pc3]

            @blk2.sync
            def _(sy):
                pid = sy.partition_id()
                parv = sy.scalar_reg_alu(ALU.bitwise_and, pid, 1)
                with sy.register("c2") as rc, sy.register("rr2") as rreg:
                    def poll(par, rd):
                        sy.reg_mov(rc, 1)
                        with sy.While(rc):
                            sy.load(rreg, sh_flag.ap()[1 - par:2 - par, rd:rd + 1])
                            sy.reg_alu(rc, rreg, MAGIC + rd, ALU.not_equal)

                    for par in sy.Switch(parv, 2):
                        nn = 0
                        for kk in range(CB):
                            if kk >= 2:
                                sy.wait_ge(g("wq_v"), kk - 1)
                            sy.dma_start(out=wld2[:, (kk % 2) * D:(kk % 2) * D + D],
                                         in_=ffnw_in[kk * 128:(kk + 1) * 128, :]).then_inc(g(f"s_w2{kk % 2}"), 16)
                        for kk in range(CB):
                            sy.wait_ge(g("wq_v"), 12 + max(0, kk - 1))
                            sy.dma_start(out=wld2[:, (kk % 2) * D:(kk % 2) * D + D],
                                         in_=ffnw_in[kk * 128:(kk + 1) * 128, :]).then_inc(g(f"s_w2{kk % 2}"), 16)
                        for kk in range(16):
                            sy.wait_ge(g("wq_v"), 20 if kk < 2 else 27 + kk)
                            sy.dma_start(out=wld2[:, (kk % 2) * D:(kk % 2) * D + D],
                                         in_=owT_in[kk * 128:(kk + 1) * 128, :]).then_inc(g(f"s_w2{kk % 2}"), 16)
                        sy.wait_ge(g("s_yw0"), 16 * 4 * K1)
                        sy.wait_ge(g("s_yw1"), 16 * 4 * K1)
                        for j in range(K2):
                            kk_g = 8 * par + j
                            sy.wait_ge(g("v2o"), 4 * j)
                            sy.dma_start(
                                out=x_tm2[:, :],
                                in_=x_in[par * TOK2 + j * T: par * TOK2 + (j + 1) * T, :]
                                .rearrange("(a p) d -> p a d", p=128)).then_inc(g("s_x2"), 16)
                            poll(par, R_YG(kk_g))
                            sy.wait_ge(g("p2o"), 8 * j)
                            sy.dma_start(
                                out=yg_in[:, 0:CB * T],
                                in_=sh_yg.ap().rearrange("a p (cb l) -> a p cb l", cb=CB)
                                [par, :, :, kk_g * T:(kk_g + 1) * T]).then_inc(g("s_yr"), 16)
                            sy.dma_start(
                                out=yg_in[:, CB * T:16 * T],
                                in_=sh_yg.ap().rearrange("a p (cb l) -> a p cb l", cb=CB)
                                [1 - par, :, :, kk_g * T:(kk_g + 1) * T]).then_inc(g("s_yr"), 16)
                            sy.wait_ge(g("v2x"), 4 * (j + 1))
                            if j > 0:
                                sy.wait_ge(g("s_a2"), 16 * 3 * j)
                                sy.wait_ge(g("v2q"), j)
                            sy.dma_start(out=d_asc.ap(), in_=asc4[:, :]).then_inc(g("s_a2"), 16)
                            sy.wait_ge(g("s_a2"), 16 * (3 * j + 1))
                            sy.dma_start(out=ascrow[:, :],
                                         in_=d_asc.ap().rearrange("p t -> t p").unsqueeze(0),
                                         allow_slow_non_contiguous=True).then_inc(g("s_a2"), 16)
                            sy.wait_ge(g("v2q"), j + 1)
                            if j > 0:
                                sy.wait_ge(g("s_b2"), 16 * j)
                            sy.dma_start(out=d_sc.ap(), in_=scrow[:, :]).then_inc(g("s_a2"), 16)
                            sy.wait_ge(g("s_a2"), 16 * (3 * j + 3))
                            sy.wait_ge(g("v2g"), 8 * j)
                            sy.dma_start(out=sc_bc[:, :],
                                         in_=d_sc.ap()[0:1, :].partition_broadcast(128)).then_inc(g("s_b2"), 16)
                            for t in range(TT4):
                                sy.wait_ge(g("v2o"), 4 * j + t + 1)
                                sy.dma_start(out=o_out[j * T + t * 128: j * T + (t + 1) * 128, :],
                                             in_=outt[:, (t % 2) * D:(t % 2) * D + D]).then_inc(g(f"s_o{t % 2}"), 16)
                            if dbg and j == 0:
                                sy.wait_ge(g("a2g"), 8)
                                sy.dma_start(out=dbg_o["dbg_g"], in_=g_ffn[:, :]).then_inc(g("s_fl"), 16)
                                sy.wait_ge(g("v2s"), 8)
                                sy.dma_start(out=dbg_o["dbg_sg"], in_=sg_sb[:, :]).then_inc(g("s_fl"), 16)
                                sy.wait_ge(g("v2e"), 4)
                                sy.dma_start(out=dbg_o["dbg_aq"], in_=aq_fm[:, :]).then_inc(g("s_fl"), 16)
                        sy.wait_ge(g("s_o0"), 16 * 2 * K2)
                        sy.wait_ge(g("s_o1"), 16 * 2 * K2)

            @blk2.vector
            def _(v):
                C2 = [0]

                def vm(inst):
                    inst.then_inc(g("c_v2"), 1)
                    C2[0] += 1

                def vflush():
                    v.wait_ge(g("c_v2"), C2[0])

                for kk in range(CB):
                    v.wait_ge(g(f"s_w2{kk % 2}"), 16 * (kk // 2 + 1))
                    v.tensor_reduce(out=abscols[:, kk:kk + 1],
                                    in_=wld2[:, (kk % 2) * D:(kk % 2) * D + D],
                                    axis=AX.X, op=ALU.add, apply_absolute_value=True).then_inc(g("wq_v"), 1)
                v.wait_ge(g("wq_v"), 8)
                v.tensor_reduce(out=abscols[:, 8:9], in_=abscols[:, 0:8], axis=AX.X,
                                op=ALU.add).then_inc(g("wq_v"), 1)
                v.wait_ge(g("wq_p"), 1)
                v.tensor_copy(out=wscpre[:, :], in_=pcs[0:1, 0:1]).then_inc(g("wq_v"), 1)
                v.wait_ge(g("wq_p"), 2)
                v.tensor_copy(out=wscv[:, :], in_=pcs[:, 1:2]).then_inc(g("wq_v"), 1)
                v.wait_ge(g("wq_v"), 11)
                v.reciprocal(out=wrecv[:, :], in_=wscv[:, :]).then_inc(g("wq_v"), 1)
                v.wait_ge(g("wq_v"), 12)
                for qq in range(4):
                    ka, kb = 2 * qq, 2 * qq + 1
                    for kk in (ka, kb):
                        v.wait_ge(g(f"s_w2{kk % 2}"), 16 * (4 + kk // 2 + 1))
                        vm(v.tensor_scalar(out=zq2[:, (kk % 2) * D:(kk % 2) * D + D],
                                           in0=wld2[:, (kk % 2) * D:(kk % 2) * D + D],
                                           scalar1=wrecv[:, 0:1], scalar2=None, op0=ALU.mult))
                    vflush()
                    for kk in (ka, kb):
                        sl = slice((kk % 2) * D, (kk % 2) * D + D)
                        wsl = slice(kk * D, (kk + 1) * D)
                        vm(v.tensor_copy(out=aqi2[:, sl], in_=zq2[:, sl]))
                        vflush()
                        vm(v.tensor_copy(out=wq16[:, wsl], in_=aqi2[:, sl]))
                        vflush()
                        vm(v.tensor_tensor(out=xn2[:, sl], in0=zq2[:, sl],
                                           in1=wq16[:, wsl], op=ALU.subtract))
                        vflush()
                        vm(v.tensor_scalar(out=zq2[:, sl], in0=xn2[:, sl],
                                           scalar1=0.5, scalar2=None, op0=ALU.is_gt))
                        vm(v.tensor_scalar(out=xn2[:, sl], in0=xn2[:, sl],
                                           scalar1=-0.5, scalar2=None, op0=ALU.is_lt))
                        vflush()
                        vm(v.tensor_tensor(out=wq16[:, wsl], in0=wq16[:, wsl],
                                           in1=zq2[:, sl], op=ALU.add))
                        vflush()
                        vm(v.tensor_tensor(out=wq16[:, wsl], in0=wq16[:, wsl],
                                           in1=xn2[:, sl], op=ALU.subtract))
                        vflush()
                        v.tensor_scalar(out=wq16[:, wsl], in0=wq16[:, wsl],
                                        scalar1=1.0, scalar2=-1.0,
                                        op0=ALU.min, op1=ALU.max).then_inc(g("wq_v"), 1)
                for kk in range(CB):
                    v.wait_ge(g("wq_p"), 3 + kk)
                    v.tensor_copy(
                        out=wqT16.rearrange("p (bb d) -> p bb d", bb=CB)[:, :, kk * 128:(kk + 1) * 128],
                        in_=pct.rearrange("p (bb d) -> p bb d", bb=CB)[:, :, :]).then_inc(g("wq_v"), 1)
                for kk in range(16):
                    v.wait_ge(g(f"s_w2{kk % 2}"), 16 * (8 + kk // 2 + 1))
                    v.tensor_copy(out=owT16[:, kk * D:(kk + 1) * D],
                                  in_=wld2[:, (kk % 2) * D:(kk % 2) * D + D]).then_inc(g("wq_v"), 1)

                v.wait_ge(g("wq_v"), 20)
                for j in range(K2):
                    for qq in range(2):
                        ta, tb = 2 * qq, 2 * qq + 1
                        for t in (ta, tb):
                            v.wait_ge(g("s_x2"), 16 * (j + 1))
                            vm(v.bn_stats(out=bn62[:, 0:6], in_=x_tm2[:, t * D:t * D + 512]))
                            v.bn_stats(out=bn62[:, 6:12],
                                       in_=x_tm2[:, t * D + 512:(t + 1) * D]).then_inc(g("v2a"), 1)
                            v.wait_ge(g("v2a"), 2 * (4 * j + t) + 1)
                            vflush()
                            v.wait_ge(g("v2x"), max(0, 4 * j + t - 1))
                            v.wait_ge(g("a2r"), max(0, 4 * j + t - 1))
                            v.bn_aggr(out=mv2[:, 2 * (t % 2):2 * (t % 2) + 2],
                                      in_=bn62[:, :]).then_inc(g("v2a"), 1)
                        for t in (ta, tb):
                            v.wait_ge(g("a2r"), 4 * j + t + 1)
                            v.wait_ge(g("p2t"), max(0, 4 * j + t - 1))
                            vm(v.tensor_scalar(out=xn2[:, (t % 2) * D:(t % 2) * D + D],
                                               in0=x_tm2[:, t * D:(t + 1) * D],
                                               scalar1=mv2[:, 2 * (t % 2):2 * (t % 2) + 1],
                                               scalar2=rstd2[:, (t % 2):(t % 2) + 1],
                                               op0=ALU.subtract, op1=ALU.mult))
                        vflush()
                        for t in (ta, tb):
                            vm(v.reduce_max(out=asc4[:, t:t + 1], in_=xn2[:, (t % 2) * D:(t % 2) * D + D],
                                            axis=AX.X, apply_absolute_value=True))
                        vflush()
                        for t in (ta, tb):
                            vm(v.tensor_scalar(out=asc4[:, t:t + 1], in0=asc4[:, t:t + 1],
                                               scalar1=EPS_Q, scalar2=None, op0=ALU.add))
                        vflush()
                        for t in (ta, tb):
                            vm(v.reciprocal(out=arec[:, (t % 2):(t % 2) + 1], in_=asc4[:, t:t + 1]))
                        vflush()
                        for t in (ta, tb):
                            vm(v.tensor_scalar(out=zq2[:, (t % 2) * D:(t % 2) * D + D],
                                               in0=xn2[:, (t % 2) * D:(t % 2) * D + D],
                                               scalar1=arec[:, (t % 2):(t % 2) + 1], scalar2=127.0,
                                               op0=ALU.mult, op1=ALU.mult))
                        vflush()
                        for t in (ta, tb):
                            sl = slice((t % 2) * D, (t % 2) * D + D)
                            asl = slice(t * D, (t + 1) * D)
                            vm(v.tensor_copy(out=aqi2[:, sl], in_=zq2[:, sl]))
                            vflush()
                            vm(v.tensor_copy(out=aq_tm[:, asl], in_=aqi2[:, sl]))
                            vflush()
                            vm(v.tensor_tensor(out=xn2[:, sl], in0=zq2[:, sl],
                                               in1=aq_tm[:, asl], op=ALU.subtract))
                            vflush()
                            vm(v.tensor_scalar(out=zq2[:, sl], in0=xn2[:, sl],
                                               scalar1=0.5, scalar2=None, op0=ALU.is_gt))
                            vm(v.tensor_scalar(out=xn2[:, sl], in0=xn2[:, sl],
                                               scalar1=-0.5, scalar2=None, op0=ALU.is_lt))
                            vflush()
                            vm(v.tensor_tensor(out=aq_tm[:, asl], in0=aq_tm[:, asl],
                                               in1=zq2[:, sl], op=ALU.add))
                            vflush()
                            v.tensor_tensor(out=aq_tm[:, asl], in0=aq_tm[:, asl],
                                            in1=xn2[:, sl], op=ALU.subtract).then_inc(g("v2x"), 1)
                        for t in (ta, tb):
                            v.wait_ge(g("p2t"), 4 * j + t + 1)
                            v.wait_ge(g("p2f"), 8 * j)
                            v.tensor_copy(
                                out=aq_fm.rearrange("p (db l) -> p db l", db=CB)[:, :, t * 128:(t + 1) * 128],
                                in_=pct.rearrange("p (db l) -> p db l", db=CB)[:, :, :]).then_inc(g("v2e"), 1)
                    v.wait_ge(g("s_a2"), 16 * (3 * j + 2))
                    v.wait_ge(g("wq_a"), 1)
                    v.tensor_scalar(out=scrow[:, :], in0=ascrow[:, :], scalar1=wsc[0:1, 0:1],
                                    scalar2=1.0 / 127.0, op0=ALU.mult, op1=ALU.mult).then_inc(g("v2q"), 1)
                    for m in range(CB):
                        v.wait_ge(g("p2f"), 8 * j + m + 1)
                        v.wait_ge(g("s_b2"), 16 * (j + 1))
                        v.wait_ge(g("a2g"), max(0, 8 * j + m - 1))
                        v.tensor_tensor(out=gbuf[:, (m % 2) * T:(m % 2) * T + T],
                                        in0=pf[m % 2][:, :], in1=sc_bc[:, :],
                                        op=ALU.mult).then_inc(g("v2g"), 1)
                    for m in range(CB):
                        v.wait_ge(g("p2o"), 8 * j + m + 1)
                        v.wait_ge(g("a2g"), 8 * j + m + 1)
                        v.tensor_tensor(out=sg_sb[:, m * T:(m + 1) * T],
                                        in0=po[m % 2][:, :], in1=g_ffn[:, m * T:(m + 1) * T],
                                        op=ALU.mult).then_inc(g("v2s"), 1)
                    for t in range(TT4):
                        v.wait_ge(g("p2g"), 4 * j + t + 1)
                        v.wait_ge(g(f"s_o{t % 2}"), 16 * max(0, 2 * j + t // 2))
                        v.tensor_tensor(out=outt[:, (t % 2) * D:(t % 2) * D + D],
                                        in0=x_tm2[:, t * D:(t + 1) * D], in1=pcg[:, :],
                                        op=ALU.add).then_inc(g("v2o"), 1)

            @blk2.scalar
            def _(a):
                a.wait_ge(g("wq_v"), 10)
                a.activation(out=wsc[:, :], in_=wscpre[:, :], func=AF.Copy,
                             scale=INVMEAN, bias=EPS_Q).then_inc(g("wq_a"), 1)
                for j in range(K2):
                    for t in range(TT4):
                        a.wait_ge(g("v2a"), 2 * (4 * j + t) + 2)
                        a.wait_ge(g("v2x"), max(0, 4 * j + t - 1))
                        a.activation(out=lnt2[:, (t % 2):(t % 2) + 1],
                                     in_=mv2[:, 2 * (t % 2) + 1:2 * (t % 2) + 2], func=AF.Ln,
                                     bias=cbias[:, 0:1]).then_inc(g("a_rs1"), 1)
                        a.wait_ge(g("a_rs1"), 4 * K1 + 4 * j + t + 1)
                        a.activation(out=rstd2[:, (t % 2):(t % 2) + 1],
                                     in_=lnt2[:, (t % 2):(t % 2) + 1], func=AF.Exp,
                                     scale=-0.5).then_inc(g("a2r"), 1)
                    for m in range(CB):
                        a.wait_ge(g("v2g"), 8 * j + m + 1)
                        a.activation(out=g_ffn[:, m * T:(m + 1) * T],
                                     in_=gbuf[:, (m % 2) * T:(m % 2) * T + T],
                                     func=AF.Silu).then_inc(g("a2g"), 1)

            @blk2.tensor
            def _(te):
                te.wait_ge(g("wq_v"), 9)
                te.matmul(out=pcs[0:1, 0:1], lhsT=onesf32c[:, :], rhs=abscols[:, 8:9],
                          start=True, stop=True).then_inc(g("wq_p"), 1)
                te.wait_ge(g("wq_a"), 1)
                te.matmul(out=pcs[:, 1:2], lhsT=onesf32r[:, :], rhs=wsc[:, :],
                          start=True, stop=True).then_inc(g("wq_p"), 1)
                for kk in range(CB):
                    te.wait_ge(g("wq_v"), 13 + kk)
                    te.wait_ge(g("wq_v"), 20 + max(0, kk))
                    for bb in range(CB):
                        mm = te.transpose(out=pct[:, bb * 128:(bb + 1) * 128],
                                          in_=wq16[:, kk * D + bb * 128:kk * D + (bb + 1) * 128],
                                          identity=id16[:, :])
                    mm.then_inc(g("wq_p"), 1)
                for j in range(K2):
                    for t in range(TT4):
                        te.wait_ge(g("v2x"), 4 * j + t + 1)
                        te.wait_ge(g("v2e"), 4 * j + t)
                        for db in range(CB):
                            mm = te.transpose(out=pct[:, db * 128:(db + 1) * 128],
                                              in_=aq_tm[:, t * D + db * 128:t * D + (db + 1) * 128],
                                              identity=id16[:, :])
                        mm.then_inc(g("p2t"), 1)
                    te.wait_ge(g("wq_v"), 28)
                    te.wait_ge(g("v2e"), 4 * (j + 1))
                    for m in range(CB):
                        te.wait_ge(g("v2g"), max(0, 8 * j + m - 1))
                        for kk in range(CB):
                            mm = te.matmul(out=pf[m % 2][:, :],
                                           lhsT=wqT16[:, kk * D + m * 128:kk * D + (m + 1) * 128],
                                           rhs=aq_fm[:, kk * T:(kk + 1) * T],
                                           start=(kk == 0), stop=(kk == CB - 1))
                        mm.then_inc(g("p2f"), 1)
                    te.wait_ge(g("wq_v"), 44)
                    te.wait_ge(g("s_yr"), 32 * (j + 1))
                    for m in range(CB):
                        te.wait_ge(g("v2s"), max(0, 8 * j + m - 1))
                        for kk in range(16):
                            mm = te.matmul(out=po[m % 2][:, :],
                                           lhsT=owT16[:, kk * D + m * 128:kk * D + (m + 1) * 128],
                                           rhs=yg_in[:, kk * T:(kk + 1) * T],
                                           start=(kk == 0), stop=(kk == 15))
                        mm.then_inc(g("p2o"), 1)
                    for t in range(TT4):
                        te.wait_ge(g("v2s"), 8 * (j + 1))
                        te.wait_ge(g("v2o"), 4 * j + t)
                        for m in range(CB):
                            mm = te.transpose(out=pcg[:, m * 128:(m + 1) * 128],
                                              in_=sg_sb[:, m * T + t * 128:m * T + (t + 1) * 128],
                                              identity=idf32[:, :])
                        mm.then_inc(g("p2g"), 1)

    return nc


_NC_CACHE = {}


def kernel(L0_attn_x, L0_attn_in_proj_w, L0_attn_conv_w, L0_attn_conv_b,
           L0_attn_x_proj_w, L0_attn_A_log, L0_attn_D_param, L0_attn_dt_w,
           L0_attn_dt_b, L0_attn_out_w, L0_attn_ln_g, L0_attn_ln_b, L0_ffn_weight):
    f32 = np.float32
    x = np.asarray(L0_attn_x, f32)
    ipT = np.ascontiguousarray(np.asarray(L0_attn_in_proj_w, f32).T)
    xp = np.asarray(L0_attn_x_proj_w, f32)
    xp_re = np.concatenate([xp[1:17], xp[17:33], xp[0:1]], 0)
    alog = np.asarray(L0_attn_A_log, f32)
    cw = np.asarray(L0_attn_conv_w, f32)
    cbb = np.asarray(L0_attn_conv_b, f32)
    dp = np.asarray(L0_attn_D_param, f32)
    dtw = np.asarray(L0_attn_dt_w, f32)[:, 0]
    dtb = np.asarray(L0_attn_dt_b, f32)
    lng = np.asarray(L0_attn_ln_g, f32)
    lnb = np.asarray(L0_attn_ln_b, f32)
    owT = np.ascontiguousarray(np.asarray(L0_attn_out_w, f32).T)
    ffnw = np.ascontiguousarray(np.asarray(L0_ffn_weight, f32))

    def vec_layout(vv, half):
        return np.ascontiguousarray(vv[half * CL:(half + 1) * CL].reshape(CB, 128).T)

    in_maps = []
    for r in range(8):
        b, par = r // 2, r & 1
        h0 = par * CL
        wipT = np.ascontiguousarray(
            np.concatenate([ipT[:, h0:h0 + CL], ipT[:, DI + h0:DI + h0 + CL]], 1))
        xpwT = np.ascontiguousarray(
            xp_re[:, h0:h0 + CL].T.reshape(CB, 128, 33).transpose(1, 0, 2).reshape(128, CB * 33))
        al = np.ascontiguousarray(
            alog[h0:h0 + CL].reshape(CB, 128, NS).transpose(1, 0, 2).reshape(128, CB * NS))
        cwc = np.ascontiguousarray(
            cw[h0:h0 + CL].reshape(CB, 128, 4).transpose(1, 0, 2).reshape(128, CB * 4))
        owc = np.ascontiguousarray(
            np.concatenate([owT[h0:h0 + CL], owT[CL - h0:CL - h0 + CL]], 0)) if False else \
            np.ascontiguousarray(
            np.concatenate([owT[h0:h0 + CL], owT[(1 - par) * CL:(1 - par) * CL + CL]], 0))
        in_maps.append(dict(
            x_in=np.ascontiguousarray(x[b]),
            wipT_in=wipT, xpwT_in=xpwT, alog_in=al, cw_in=cwc,
            cb_in=vec_layout(cbb, par), dp_in=vec_layout(dp, par),
            dtw_in=vec_layout(dtw, par), dtb_in=vec_layout(dtb, par),
            lng_in=vec_layout(lng, par), lnb_in=vec_layout(lnb, par),
            owT_in=owc, ffnw_in=ffnw,
        ))

    nc = build_nc(dbg=False)
    res = run_bass_kernel_spmd(nc, in_maps, core_ids=list(range(8)))
    out = np.empty((4, L, D), f32)
    for r in range(8):
        b, par = r // 2, r & 1
        out[b, par * TOK2:(par + 1) * TOK2] = res.results[r]["o_out"]
    return out


# version 07
# speedup vs baseline: 1.3955x; 1.0545x over previous
import numpy as np
import concourse.bass as bass
import concourse.mybir as mybir
from concourse.bass_utils import run_bass_kernel_spmd

F32 = mybir.dt.float32
F16 = mybir.dt.float16
BF16 = mybir.dt.bfloat16
I32 = mybir.dt.int32
AF = mybir.ActivationFunctionType
ALU = mybir.AluOpType
AX = mybir.AxisListType

L = 8192
D = 1024
DI = 2048
NS = 16
CL = 1024
CB = 8
T = 512
K1 = L // T
TT4 = T // 128
TOK2 = 4096
K2 = TOK2 // T
MAGIC = 777000
EPS_LN = 1e-5
EPS_Q = 1e-8
INV2048 = 1.0 / 2048.0
INVMEAN = 1.0 / (1024.0 * 1024.0)

R_PROJ = lambda k: k
R_STAT = lambda k: K1 + k
R_YG = lambda k: 2 * K1 + k


def build_nc(dbg=False, race_detect=False):
    nc = bass.Bass("TRN2", target_bir_lowering=False, detect_race_conditions=race_detect)

    x_in = nc.dram_tensor("x_in", [L, D], F32, kind="ExternalInput").ap()
    wipT_in = nc.dram_tensor("wipT_in", [D, 2 * CL], F32, kind="ExternalInput").ap()
    xpwT_in = nc.dram_tensor("xpwT_in", [128, CB * 33], F32, kind="ExternalInput").ap()
    alog_in = nc.dram_tensor("alog_in", [128, CB * NS], F32, kind="ExternalInput").ap()
    cw_in = nc.dram_tensor("cw_in", [128, CB * 4], F32, kind="ExternalInput").ap()
    cb_in = nc.dram_tensor("cb_in", [128, CB], F32, kind="ExternalInput").ap()
    dp_in = nc.dram_tensor("dp_in", [128, CB], F32, kind="ExternalInput").ap()
    dtw_in = nc.dram_tensor("dtw_in", [128, CB], F32, kind="ExternalInput").ap()
    dtb_in = nc.dram_tensor("dtb_in", [128, CB], F32, kind="ExternalInput").ap()
    lng_in = nc.dram_tensor("lng_in", [128, CB], F32, kind="ExternalInput").ap()
    lnb_in = nc.dram_tensor("lnb_in", [128, CB], F32, kind="ExternalInput").ap()
    owT_in = nc.dram_tensor("owT_in", [DI, D], F32, kind="ExternalInput").ap()
    ffnw_in = nc.dram_tensor("ffnw_in", [D, D], F32, kind="ExternalInput").ap()
    o_out = nc.dram_tensor("o_out", [TOK2, D], F32, kind="ExternalOutput").ap()

    dbg_o = {}
    if dbg:
        for nm, shp, dt_ in [("dbg_y", [128, CB * T], F32), ("dbg_proj", [33, T], F32),
                             ("dbg_dt", [128, CB * T], F32), ("dbg_xb", [128, CB * T], F16),
                             ("dbg_u", [128, CB * T], BF16), ("dbg_sz", [128, CB * T], F16),
                             ("dbg_yg", [128, CB * T], F16), ("dbg_g", [128, CB * T], F32),
                             ("dbg_sg", [128, CB * T], F32), ("dbg_aq", [128, CB * T], F16),
                             ("dbg_h", [128, CB * T], F32), ("dbg_st", [2, T], F32)]:
            dbg_o[nm] = nc.dram_tensor(nm, shp, dt_, kind="ExternalOutput").ap()

    sh_proj = nc.dram_tensor("sh_proj", [2, K1 * 33, T], F32, addr_space="Shared")
    sh_st = nc.dram_tensor("sh_st", [2, K1 * 2, T], F32, addr_space="Shared")
    sh_yg = nc.dram_tensor("sh_yg", [2, 128, CB * L], F16, addr_space="Shared")
    sh_flag = nc.dram_tensor("sh_flag", [2, 64], I32, addr_space="Shared")
    d_bc = nc.dram_tensor("d_bc", [32, T], BF16)
    d_bcc = nc.dram_tensor("d_bcc", [32, T], F16)
    d_dtin = nc.dram_tensor("d_dtin", [1, T], F32)
    d_rows = nc.dram_tensor("d_rows", [2, T], F32)
    d_asc = nc.dram_tensor("d_asc", [128, 4], F32)
    d_sc = nc.dram_tensor("d_sc", [1, T], F32)

    import contextlib
    est = contextlib.ExitStack()
    with est:
        sb = lambda name, shape, dt_: est.enter_context(nc.sbuf_tensor(name, shape, dt_))
        sem = lambda name: est.enter_context(nc.semaphore(name))

        state = sb("state", [128, 128], F32)
        flagv = sb("flagv", [1, 64], I32)
        Av = sb("Av", [128, CB * NS], F32)
        cwv = sb("cwv", [128, CB * 4], F32)
        cbv = sb("cbv", [128, CB], F32)
        Dv = sb("Dv", [128, CB], F32)
        dtwv = sb("dtwv", [128, CB], F32)
        dtbv = sb("dtbv", [128, CB], F32)
        lngv = sb("lngv", [128, CB], F32)
        lnbv = sb("lnbv", [128, CB], F32)
        idbf = sb("idbf", [128, 128], BF16)
        id16 = sb("id16", [128, 128], F16)
        idf32 = sb("idf32", [128, 128], F32)
        ones16c = sb("ones16c", [128, 1], F16)
        onesbfc = sb("onesbfc", [128, 1], BF16)
        onesf32c = sb("onesf32c", [128, 1], F32)
        onesf32r = sb("onesf32r", [1, 128], F32)
        iotai = sb("iotai", [128, 128], I32)
        cbias = sb("cbias", [128, 4], F32)
        wsc = sb("wsc", [1, 1], F32)
        wscpre = sb("wscpre", [1, 1], F32)
        wrecv = sb("wrecv", [128, 1], F32)
        wscv = sb("wscv", [128, 1], F32)

        names = ("s_x0 s_x1 s_w0 s_w1 s_wp s_bcd s_bcb s_bcc s_bcr s_sg s_pw s_pr s_sw s_sr s_yw0 s_yw1 s_yr s_fl s_o0 s_o1 s_x2 s_w20 s_w21 s_a2 s_b2 "
                 "v_st a_rs a_rs1 v_xn p_tp v_te p_ip v_cv a_xb a_z p_xp v_pv v_pj v_bc "
                 "v_t1 a_sp v_dt v_u a_dA o_xs v_ts v_ym p_ya v_ye v_yf a_y2 p_st v_se "
                 "v_mu a_rw a_rw1 v_nm v_yg su su2 "
                 "v2x v2a v2q p2t v2e p2f v2g a2g p2o v2s p2g v2o a2r wq_v wq_p wq_a c_v c_a c_v2").split()
        S = {n: sem(n) for n in names}
        g = lambda n: S[n]

        IDX = lambda k, cb, n: 128 * k + 16 * cb + n
        BC_CNT = lambda k, j: 16 * (7 * k + j)
        SG_CNT = lambda k, j: 16 * (5 * k + j)

        st1 = contextlib.ExitStack()
        with st1:
            wipT16 = st1.enter_context(nc.sbuf_tensor("wipT16", [128, CB * 2 * CL], F16))
            xpwT16 = st1.enter_context(nc.sbuf_tensor("xpwT16", [128, CB * 33], F16))
            x_tm = st1.enter_context(nc.sbuf_tensor("x_tm", [128, 2 * 1024], F32))
            bn6 = st1.enter_context(nc.sbuf_tensor("bn6", [128, 12], F32))
            mv = st1.enter_context(nc.sbuf_tensor("mv", [128, 4], F32))
            rstd1 = st1.enter_context(nc.sbuf_tensor("rstd1", [128, 2], F32))
            lnt = st1.enter_context(nc.sbuf_tensor("lnt", [128, 1], F32))
            xn16 = st1.enter_context(nc.sbuf_tensor("xn16", [128, 2 * 1024], F16))
            xn_fm = st1.enter_context(nc.sbuf_tensor("xn_fm", [128, CB * T], F16))
            cacc = st1.enter_context(nc.sbuf_tensor("cacc", [128, 2 * T], F32))
            tails = st1.enter_context(nc.sbuf_tensor("tails", [128, CB * 3], F32))
            xb16 = st1.enter_context(nc.sbuf_tensor("xb16", [128, CB * T], F16))
            sz16 = st1.enter_context(nc.sbuf_tensor("sz16", [128, CB * T], F16))
            dt_sb = st1.enter_context(nc.sbuf_tensor("dt_sb", [128, CB * T], F32))
            spt = st1.enter_context(nc.sbuf_tensor("spt", [128, 2 * T], F32))
            spe = st1.enter_context(nc.sbuf_tensor("spe", [128, 2 * T], F32))
            spl = st1.enter_context(nc.sbuf_tensor("spl", [128, 2 * T], F32))
            u16 = st1.enter_context(nc.sbuf_tensor("u16", [128, CB * T], BF16))
            proj = st1.enter_context(nc.sbuf_tensor("proj", [33, T], F32))
            projp = st1.enter_context(nc.sbuf_tensor("projp", [33, T], F32))
            bc16 = st1.enter_context(nc.sbuf_tensor("bc16", [32, T], BF16))
            bc16c = st1.enter_context(nc.sbuf_tensor("bc16c", [32, T], F16))
            Bp_g = st1.enter_context(nc.sbuf_tensor("Bp_g", [128, 8 * T], BF16))
            Cp_g = st1.enter_context(nc.sbuf_tensor("Cp_g", [128, 8 * T], F16))
            dtin_bc = st1.enter_context(nc.sbuf_tensor("dtin_bc", [128, T], F32))
            rstd_bc = st1.enter_context(nc.sbuf_tensor("rstd_bc", [128, T], F32))
            nm_bc = st1.enter_context(nc.sbuf_tensor("nm_bc", [128, T], F32))
            dA2 = st1.enter_context(nc.sbuf_tensor("dA2", [128, 2 * T], F32))
            xs3 = st1.enter_context(nc.sbuf_tensor("xs3", [128, 3 * T], BF16))
            tmp2 = st1.enter_context(nc.sbuf_tensor("tmp2", [128, 2 * T], BF16))
            h_g8 = st1.enter_context(nc.sbuf_tensor("h_g8", [128, 8 * T], F32))
            y_sb = st1.enter_context(nc.sbuf_tensor("y_sb", [128, CB * T], F32))
            yst16 = st1.enter_context(nc.sbuf_tensor("yst16", [128, T], BF16))
            y2st16 = st1.enter_context(nc.sbuf_tensor("y2st16", [128, T], BF16))
            lnB1 = st1.enter_context(nc.sbuf_tensor("lnB1", [128, 2 * T], F32))
            ygc = st1.enter_context(nc.sbuf_tensor("ygc", [128, 2 * T], F16))
            ssum = st1.enter_context(nc.sbuf_tensor("ssum", [1, T], F32))
            s2sum = st1.enter_context(nc.sbuf_tensor("s2sum", [1, T], F32))
            sp0 = st1.enter_context(nc.sbuf_tensor("sp0", [1, T], F32))
            sp1 = st1.enter_context(nc.sbuf_tensor("sp1", [1, T], F32))
            murow = st1.enter_context(nc.sbuf_tensor("murow", [1, T], F32))
            varow = st1.enter_context(nc.sbuf_tensor("varow", [1, T], F32))
            rstdrow = st1.enter_context(nc.sbuf_tensor("rstdrow", [1, T], F32))
            nmrow = st1.enter_context(nc.sbuf_tensor("nmrow", [1, T], F32))
            pb0 = st1.enter_context(nc.psum_tensor("pb0", [128, T], F32))
            pb1 = st1.enter_context(nc.psum_tensor("pb1", [128, T], F32))
            pb2 = st1.enter_context(nc.psum_tensor("pb2", [128, T], F32))
            pb3 = st1.enter_context(nc.psum_tensor("pb3", [128, T], F32))
            pb4 = st1.enter_context(nc.psum_tensor("pb4", [128, T], F32))
            pb5 = st1.enter_context(nc.psum_tensor("pb5", [128, T], F32))
            pb6 = st1.enter_context(nc.psum_tensor("pb6", [128, T], F32))
            pbt = st1.enter_context(nc.psum_tensor("pbt", [128, 1024], F16))
            blk1 = st1.enter_context(nc.Block())
            pxb = [pb0, pb1]
            pz = [pb2, pb3]
            pby = [pb5, pb6]

            @blk1.sync
            def _(sy):
                pid = sy.partition_id()
                parv = sy.scalar_reg_alu(ALU.bitwise_and, pid, 1)
                with sy.register("c") as rc, sy.register("rr") as rreg:
                    def poll(par, rd):
                        sy.reg_mov(rc, 1)
                        with sy.While(rc):
                            sy.load(rreg, sh_flag.ap()[1 - par:2 - par, rd:rd + 1])
                            sy.reg_alu(rc, rreg, MAGIC + rd, ALU.not_equal)

                    for par in sy.Switch(parv, 2):
                        nw = 0
                        for kk in range(CB):
                            if kk >= 2:
                                sy.wait_ge(g("su"), kk - 1)
                            sy.dma_start(out=y_sb[:, (kk % 2) * 2048:(kk % 2) * 2048 + 2048],
                                         in_=wipT_in[kk * 128:(kk + 1) * 128, :]).then_inc(g(f"s_w{kk % 2}"), 16)
                        sy.wait_ge(g("su"), 7)
                        sy.dma_start(out=y_sb[:, 0:CB * 33],
                                     in_=xpwT_in).then_inc(g("s_wp"), 16)
                        for dst, src in [(Av, alog_in), (cwv, cw_in), (cbv, cb_in), (Dv, dp_in),
                                         (dtwv, dtw_in), (dtbv, dtb_in), (lngv, lng_in), (lnbv, lnb_in)]:
                            sy.dma_start(out=dst[:, :], in_=src).then_inc(g("s_wp"), 16)
                        for tg in range(TT4):
                            if tg >= 2:
                                sy.wait_ge(g("v_xn"), tg - 1)
                            sy.dma_start(out=x_tm[:, (tg % 2) * 1024:(tg % 2) * 1024 + 1024],
                                         in_=x_in[tg * 128:(tg + 1) * 128, :]).then_inc(g(f"s_x{tg % 2}"), 16)

                        sy.wait_ge(g("o_xs"), 1)
                        for k in range(K1):
                            sy.wait_ge(g("v_pv"), k + 1)
                            sy.dma_start(out=sh_proj.ap()[par, k * 33:(k + 1) * 33, :],
                                         in_=proj[:, :]).then_inc(g("s_pw"), 16)
                            sy.wait_ge(g("s_pw"), 16 * (k + 1))
                            rd = R_PROJ(k)
                            sy.dma_start(out=sh_flag.ap()[par:par + 1, rd:rd + 1],
                                         in_=flagv[0:1, rd:rd + 1]).then_inc(g("s_fl"), 16)
                            poll(par, rd)
                            sy.wait_ge(g("v_pj"), k)
                            sy.dma_start(out=projp[:, :],
                                         in_=sh_proj.ap()[1 - par, k * 33:(k + 1) * 33, :]).then_inc(g("s_pr"), 16)
                            if k > 0:
                                sy.wait_ge(g("s_bcb"), 16 * 2 * k)
                                sy.wait_ge(g("s_bcc"), 16 * 2 * k)
                                sy.wait_ge(g("s_bcd"), 16 * k)
                            sy.wait_ge(g("v_bc"), k + 1)
                            sy.dma_start(out=d_bc.ap(), in_=bc16[:, :]).then_inc(g("s_sg"), 16)
                            sy.dma_start(out=d_bcc.ap(), in_=bc16c[:, :]).then_inc(g("s_sg"), 16)
                            sy.dma_start(out=d_dtin.ap(), in_=proj[32:33, :]).then_inc(g("s_sg"), 16)
                            sy.wait_ge(g("s_sg"), SG_CNT(k, 3))
                            sy.wait_ge(g("v_t1"), 8 * k)
                            sy.dma_start(out=dtin_bc[:, :],
                                         in_=d_dtin.ap()[0:1, :].partition_broadcast(128)).then_inc(g("s_bcd"), 16)
                            for gg in range(2):
                                sy.wait_ge(g("v_ym"), max(0, 128 * k + 64 * gg) if (k > 0 or gg > 0) else 0)
                                sy.dma_start(out=Bp_g[:, :],
                                             in_=d_bc.ap()[8 * gg:8 * gg + 8, :]
                                             .partition_broadcast(128)).then_inc(g("s_bcb"), 16)
                                sy.wait_ge(g("o_xs"), max(0, 128 * k + 64 * gg + 1) if (k > 0 or gg > 0) else 0)
                                sy.dma_start(out=Cp_g[:, :],
                                             in_=d_bcc.ap()[16 + 8 * gg:16 + 8 * gg + 8, :]
                                             .partition_broadcast(128)).then_inc(g("s_bcc"), 16)
                            if k + 1 < K1:
                                for t in range(2):
                                    tg = (k + 1) * TT4 + t
                                    sy.wait_ge(g("v_xn"), max(0, tg - 1))
                                    sy.dma_start(out=x_tm[:, (tg % 2) * 1024:(tg % 2) * 1024 + 1024],
                                                 in_=x_in[tg * 128:(tg + 1) * 128, :]).then_inc(g(f"s_x{tg % 2}"), 16)
                            sy.wait_ge(g("v_se"), k + 1)
                            sy.dma_start(out=sh_st.ap()[par, 2 * k:2 * k + 1, :],
                                         in_=ssum[:, :]).then_inc(g("s_sw"), 16)
                            sy.dma_start(out=sh_st.ap()[par, 2 * k + 1:2 * k + 2, :],
                                         in_=s2sum[:, :]).then_inc(g("s_sw"), 16)
                            sy.wait_ge(g("s_sw"), 32 * (k + 1))
                            rd = R_STAT(k)
                            sy.dma_start(out=sh_flag.ap()[par:par + 1, rd:rd + 1],
                                         in_=flagv[0:1, rd:rd + 1]).then_inc(g("s_fl"), 16)
                            poll(par, rd)
                            sy.wait_ge(g("v_mu"), k)
                            sy.wait_ge(g("a_rw"), k)
                            sy.dma_start(out=sp0[:, :],
                                         in_=sh_st.ap()[1 - par, 2 * k:2 * k + 1, :]).then_inc(g("s_sr"), 16)
                            sy.dma_start(out=sp1[:, :],
                                         in_=sh_st.ap()[1 - par, 2 * k + 1:2 * k + 2, :]).then_inc(g("s_sr"), 16)
                            if k + 1 < K1:
                                for t in range(2, TT4):
                                    tg = (k + 1) * TT4 + t
                                    sy.wait_ge(g("v_xn"), max(0, tg - 1))
                                    sy.dma_start(out=x_tm[:, (tg % 2) * 1024:(tg % 2) * 1024 + 1024],
                                                 in_=x_in[tg * 128:(tg + 1) * 128, :]).then_inc(g(f"s_x{tg % 2}"), 16)
                            sy.wait_ge(g("v_nm"), k + 1)
                            if k > 0:
                                sy.wait_ge(g("s_bcr"), 32 * k)
                            sy.dma_start(out=d_rows.ap()[0:1, :], in_=rstdrow[:, :]).then_inc(g("s_sg"), 16)
                            sy.dma_start(out=d_rows.ap()[1:2, :], in_=nmrow[:, :]).then_inc(g("s_sg"), 16)
                            sy.wait_ge(g("s_sg"), SG_CNT(k, 5))
                            sy.wait_ge(g("v_yg"), 8 * k)
                            sy.dma_start(out=rstd_bc[:, :],
                                         in_=d_rows.ap()[0:1, :].partition_broadcast(128)).then_inc(g("s_bcr"), 16)
                            sy.dma_start(out=nm_bc[:, :],
                                         in_=d_rows.ap()[1:2, :].partition_broadcast(128)).then_inc(g("s_bcr"), 16)
                            for cb in range(CB):
                                sy.wait_ge(g("v_yg"), 8 * k + cb + 1)
                                sy.dma_start(out=sh_yg.ap()[par, :, cb * L + k * T: cb * L + (k + 1) * T],
                                             in_=ygc[:, (cb % 2) * T:(cb % 2) * T + T]).then_inc(g(f"s_yw{cb % 2}"), 16)
                            sy.wait_ge(g("s_yw0"), 16 * 4 * (k + 1))
                            sy.wait_ge(g("s_yw1"), 16 * 4 * (k + 1))
                            rd = R_YG(k)
                            sy.dma_start(out=sh_flag.ap()[par:par + 1, rd:rd + 1],
                                         in_=flagv[0:1, rd:rd + 1]).then_inc(g("s_fl"), 16)

                            if dbg and k == 0:
                                sy.dma_start(out=dbg_o["dbg_y"], in_=y_sb[:, :]).then_inc(g("s_fl"), 16)
                                sy.dma_start(out=dbg_o["dbg_proj"], in_=proj[:, :]).then_inc(g("s_fl"), 16)
                                sy.dma_start(out=dbg_o["dbg_dt"], in_=dt_sb[:, :]).then_inc(g("s_fl"), 16)
                                sy.dma_start(out=dbg_o["dbg_xb"], in_=xb16[:, :]).then_inc(g("s_fl"), 16)
                                sy.dma_start(out=dbg_o["dbg_u"], in_=u16[:, :]).then_inc(g("s_fl"), 16)
                                sy.dma_start(out=dbg_o["dbg_sz"], in_=sz16[:, :]).then_inc(g("s_fl"), 16)
                                sy.dma_start(out=dbg_o["dbg_st"][0:1, :], in_=ssum[:, :]).then_inc(g("s_fl"), 16)
                                sy.dma_start(out=dbg_o["dbg_st"][1:2, :], in_=s2sum[:, :]).then_inc(g("s_fl"), 16)
                            if dbg and k == 1:
                                for cbq in range(CB):
                                    sy.dma_start(out=dbg_o["dbg_yg"][:, cbq * T:(cbq + 1) * T],
                                                 in_=sh_yg.ap()[par, :, cbq * L:cbq * L + T]).then_inc(g("s_fl"), 16)

            @blk1.vector
            def _(v):
                CV = [0]

                def vm(inst):
                    inst.then_inc(g("c_v"), 1)
                    CV[0] += 1
                    return CV[0]

                def vflush():
                    v.wait_ge(g("c_v"), CV[0])

                for kk in range(CB):
                    v.wait_ge(g(f"s_w{kk % 2}"), 16 * (kk // 2 + 1))
                    if kk:
                        v.wait_ge(g("su"), kk)
                    v.tensor_copy(out=wipT16[:, kk * 2048:(kk + 1) * 2048],
                                  in_=y_sb[:, (kk % 2) * 2048:(kk % 2) * 2048 + 2048]).then_inc(g("su"), 1)
                v.wait_ge(g("s_wp"), 16 * 9)
                v.wait_ge(g("su"), 8)
                v.tensor_copy(out=xpwT16[:, :], in_=y_sb[:, 0:CB * 33]).then_inc(g("su"), 1)
                v.wait_ge(g("a_rs"), 1)
                v.wait_ge(g("su"), 9)
                v.tensor_scalar(out=Av[:, :], in0=spt[:, 0:CB * NS], scalar1=-1.0,
                                scalar2=None, op0=ALU.mult).then_inc(g("su"), 1)
                v.wait_ge(g("o_xs"), 1)
                vm(v.tensor_scalar(out=idbf[:, :], in0=iotai[:, :], scalar1=0, scalar2=None,
                                   op0=ALU.is_equal))
                vm(v.tensor_scalar(out=id16[:, :], in0=iotai[:, :], scalar1=0, scalar2=None,
                                   op0=ALU.is_equal))
                vm(v.tensor_scalar(out=idf32[:, :], in0=iotai[:, :], scalar1=0, scalar2=None,
                                   op0=ALU.is_equal))
                vm(v.memset(state[:, :], 0.0))
                vm(v.memset(tails[:, :], 0.0))
                v.wait_ge(g("su"), 10)
                v.nop().then_inc(g("su"), 1)

                def te_evac(tg):
                    k_of = tg // TT4
                    t_of = tg % TT4
                    v.wait_ge(g("p_tp"), tg + 1)
                    v.wait_ge(g("p_ip"), 16 * k_of)
                    v.tensor_copy(
                        out=xn_fm.rearrange("p (db l) -> p db l", db=CB)[:, :, t_of * 128:(t_of + 1) * 128],
                        in_=pbt.rearrange("p (db l) -> p db l", db=CB)[:, :, :]).then_inc(g("v_te"), 1)

                def phaseA(k):
                    for t in range(TT4):
                        tg = k * TT4 + t
                        sl = (tg % 2) * 1024
                        ms = (tg % 2) * 2
                        v.wait_ge(g(f"s_x{tg % 2}"), 16 * (tg // 2 + 1))
                        vm(v.bn_stats(out=bn6[:, 0:6], in_=x_tm[:, sl:sl + 512]))
                        v.bn_stats(out=bn6[:, 6:12], in_=x_tm[:, sl + 512:sl + 1024]).then_inc(g("v_st"), 1)
                        v.wait_ge(g("v_st"), 2 * tg + 1)
                        vflush()
                        if tg >= 2:
                            v.wait_ge(g("v_xn"), tg - 1)
                            v.wait_ge(g("a_rs"), tg)
                        v.bn_aggr(out=mv[:, ms:ms + 2], in_=bn6[:, :]).then_inc(g("v_st"), 1)
                        v.wait_ge(g("a_rs"), tg + 2)
                        v.wait_ge(g("p_tp"), max(0, tg - 1))
                        v.tensor_scalar(out=xn16[:, sl:sl + 1024], in0=x_tm[:, sl:sl + 1024],
                                        scalar1=mv[:, ms:ms + 1], scalar2=rstd1[:, (tg % 2):(tg % 2) + 1],
                                        op0=ALU.subtract, op1=ALU.mult).then_inc(g("v_xn"), 1)
                        if t >= 1:
                            te_evac(tg - 1)
                    te_evac(k * TT4 + TT4 - 1)

                phaseA(0)
                tail_mark = {}
                u_marks = {}
                for k in range(K1):
                    if k == 0:
                        v.wait_ge(g("su"), 11)
                        vflush()
                    for q in range(4):
                        ca, cbn = 2 * q, 2 * q + 1
                        v.wait_ge(g("p_ip"), 16 * k + 2 * q + 2)
                        v.wait_ge(g("a_xb"), max(0, 8 * k + 2 * q))
                        for j in [3, 2, 1, 0]:
                            off = 3 - j
                            if j < 3:
                                vflush()
                            for cc, pp in [(ca, pxb[0]), (cbn, pxb[1])]:
                                accs = (cc % 2) * T
                                wcol = cwv[:, cc * 4 + j:cc * 4 + j + 1]
                                if j == 3:
                                    vm(v.tensor_scalar(out=cacc[:, accs:accs + T], in0=pp[:, :],
                                                       scalar1=wcol, scalar2=None, op0=ALU.mult))
                                else:
                                    vm(v.scalar_tensor_tensor(out=cacc[:, accs + off:accs + T],
                                                              in0=pp[:, 0:T - off], scalar=wcol,
                                                              in1=cacc[:, accs + off:accs + T],
                                                              op0=ALU.mult, op1=ALU.add))
                                    vm(v.scalar_tensor_tensor(out=cacc[:, accs:accs + off],
                                                              in0=tails[:, cc * 3 + j:cc * 3 + j + off],
                                                              scalar=wcol,
                                                              in1=cacc[:, accs:accs + off],
                                                              op0=ALU.mult, op1=ALU.add))
                        vflush()
                        for cc, pp in [(ca, pxb[0]), (cbn, pxb[1])]:
                            v.tensor_copy(out=tails[:, cc * 3:cc * 3 + 3],
                                          in_=pp[:, T - 3:T]).then_inc(g("v_cv"), 1)
                    v.wait_ge(g("p_xp"), k + 1)
                    v.wait_ge(g("s_sg"), SG_CNT(k - 1, 3) if k > 0 else 0)
                    v.tensor_copy(out=proj[:, :], in_=pb4[0:33, :]).then_inc(g("v_pv"), 1)
                    v.wait_ge(g("s_pr"), 16 * (k + 1))
                    v.wait_ge(g("v_pv"), k + 1)
                    v.tensor_tensor(out=proj[:, :], in0=proj[:, :], in1=projp[:, :],
                                    op=ALU.add).then_inc(g("v_pj"), 1)
                    v.wait_ge(g("v_pj"), k + 1)
                    vm(v.tensor_copy(out=bc16[:, :], in_=proj[0:32, :]))
                    v.tensor_copy(out=bc16c[:, :], in_=proj[0:32, :]).then_inc(g("v_bc"), 1)
                    for q in range(4):
                        for cc in (2 * q, 2 * q + 1):
                            ss = (cc % 2) * T
                            v.wait_ge(g("s_bcd"), 16 * (k + 1))
                            v.wait_ge(g("a_sp"), max(0, 8 * k + cc - 1))
                            vm(v.tensor_scalar(out=spt[:, ss:ss + T], in0=dtin_bc[:, :],
                                               scalar1=dtwv[:, cc:cc + 1], scalar2=dtbv[:, cc:cc + 1],
                                               op0=ALU.mult, op1=ALU.add))
                        vflush()
                        for cc in (2 * q, 2 * q + 1):
                            ss = (cc % 2) * T
                            v.scalar_tensor_tensor(out=spe[:, ss:ss + T], in0=spt[:, ss:ss + T],
                                                   scalar=-1.0, in1=spt[:, ss:ss + T],
                                                   op0=ALU.mult, op1=ALU.max).then_inc(g("v_t1"), 1)
                        for cc in (2 * q, 2 * q + 1):
                            ss = (cc % 2) * T
                            v.wait_ge(g("a_sp"), 8 * k + cc + 1)
                            if k > 0:
                                v.wait_ge(g("a_dA"), 128 * (k - 1) + 64 + 8 * cc + 8)
                            v.scalar_tensor_tensor(out=dt_sb[:, cc * T:(cc + 1) * T],
                                                   in0=spt[:, ss:ss + T], scalar=0.0,
                                                   in1=spe[:, ss:ss + T],
                                                   op0=ALU.max, op1=ALU.add).then_inc(g("v_dt"), 1)
                        for cc in (2 * q, 2 * q + 1):
                            v.wait_ge(g("a_xb"), 8 * k + cc + 1)
                            v.wait_ge(g("v_dt"), 8 * k + cc + 1)
                            if k > 0:
                                v.wait_ge(g("v_ym"), 128 * (k - 1) + 64 + 8 * cc + 8)
                            u_marks[cc] = vm(v.tensor_tensor(
                                out=u16[:, cc * T:(cc + 1) * T],
                                in0=xb16[:, cc * T:(cc + 1) * T],
                                in1=dt_sb[:, cc * T:(cc + 1) * T],
                                op=ALU.mult))
                            v.nop().then_inc(g("v_u"), 1)
                    vflush()
                    prev = None
                    xs_marks = {}

                    def gather(gcb, gg):
                        v.wait_ge(g("v_ts"), 128 * k + 64 * gg + 8 * gcb + 8)
                        vm(v.tensor_copy(
                            out=state[:, gcb * 16 + 8 * gg:gcb * 16 + 8 * gg + 8],
                            in_=h_g8.rearrange("p (n t) -> p n t", t=T)[:, :, T - 1:T]))

                    def emit_xs(gg, cb, nn, i):
                        v.wait_ge(g("s_bcb"), 16 * (2 * k + gg + 1))
                        v.wait_ge(g("v_ts"), max(0, i - 2))
                        if nn == 0:
                            v.wait_ge(g("c_v"), u_marks[cb])
                        xs_marks[i] = vm(v.tensor_tensor(
                            out=xs3[:, (i % 3) * T:(i % 3) * T + T],
                            in0=u16[:, cb * T:(cb + 1) * T],
                            in1=Bp_g[:, nn * T:nn * T + T],
                            op=ALU.mult))
                        v.nop().then_inc(g("v_ym"), 1)

                    def emit_tts(gg, cb, nn, i):
                        v.wait_ge(g("a_dA"), i + 1)
                        v.wait_ge(g("c_v"), xs_marks.pop(i))
                        v.wait_ge(g("o_xs"), max(0, i - 6))
                        v.tensor_tensor_scan(
                            out=h_g8[:, nn * T:nn * T + T],
                            data0=dA2[:, (i % 2) * T:(i % 2) * T + T],
                            data1=xs3[:, (i % 3) * T:(i % 3) * T + T],
                            initial=state[:, cb * 16 + 8 * gg + nn:cb * 16 + 8 * gg + nn + 1],
                            op0=ALU.mult, op1=ALU.add).then_inc(g("v_ts"), 1)

                    def emit_yev(gg, cb):
                        v.wait_ge(g("p_ya"), 128 * k + 64 * gg + 8 * cb + 8)
                        if gg == 0:
                            if k > 0:
                                v.wait_ge(g("a_y2"), 8 * (k - 1) + cb + 1)
                            v.scalar_tensor_tensor(out=y_sb[:, cb * T:(cb + 1) * T],
                                                   in0=xb16[:, cb * T:(cb + 1) * T],
                                                   scalar=Dv[:, cb:cb + 1],
                                                   in1=pby[cb % 2][:, :],
                                                   op0=ALU.mult, op1=ALU.add).then_inc(g("v_ye"), 1)
                        else:
                            v.wait_ge(g("v_ye"), 16 * k + cb + 1)
                            v.tensor_tensor(out=y_sb[:, cb * T:(cb + 1) * T],
                                            in0=y_sb[:, cb * T:(cb + 1) * T],
                                            in1=pby[cb % 2][:, :],
                                            op=ALU.add).then_inc(g("v_ye"), 1)

                    seq = [(gg, cb, nn, 128 * k + 64 * gg + 8 * cb + nn)
                           for gg in range(2) for cb in range(CB) for nn in range(8)]
                    pend_yev = None
                    for item in seq:
                        gg, cb, nn, i = item
                        if nn == 0 and prev is not None:
                            pgg, pcb = prev[0], prev[1]
                            emit_xs(*item)
                            emit_tts(*prev)
                            prev = None
                            gather(pcb, pgg)
                            if pend_yev is not None:
                                emit_yev(*pend_yev)
                            pend_yev = (pgg, pcb)
                            vflush()
                        else:
                            emit_xs(*item)
                            if prev is not None:
                                emit_tts(*prev)
                        prev = item
                    emit_tts(*prev)
                    gather(CB - 1, 1)
                    if pend_yev is not None:
                        emit_yev(*pend_yev)
                    emit_yev(1, CB - 1)
                    prev = None
                    for cb in range(CB):
                        v.wait_ge(g("v_ye"), 16 * k + 8 + cb + 1)
                        v.wait_ge(g("p_st"), max(0, 8 * k + cb))
                        v.tensor_copy(out=yst16[:, :],
                                      in_=y_sb[:, cb * T:(cb + 1) * T]).then_inc(g("v_yf"), 1)
                    v.wait_ge(g("p_st"), 8 * (k + 1))
                    v.wait_ge(g("s_sw"), 32 * k)
                    vm(v.tensor_copy(out=ssum[:, :], in_=pb4[0:1, :]))
                    v.tensor_copy(out=s2sum[:, :], in_=pb2[0:1, :]).then_inc(g("v_se"), 1)
                    if k + 1 < K1:
                        phaseA(k + 1)
                    v.wait_ge(g("s_sr"), 32 * (k + 1))
                    v.wait_ge(g("v_se"), k + 1)
                    vflush()
                    vm(v.tensor_tensor(out=ssum[:, :], in0=ssum[:, :], in1=sp0[:, :], op=ALU.add))
                    vm(v.tensor_tensor(out=s2sum[:, :], in0=s2sum[:, :], in1=sp1[:, :], op=ALU.add))
                    vflush()
                    vm(v.tensor_scalar(out=murow[:, :], in0=ssum[:, :], scalar1=INV2048,
                                       scalar2=None, op0=ALU.mult))
                    vm(v.tensor_scalar(out=varow[:, :], in0=s2sum[:, :], scalar1=INV2048,
                                       scalar2=None, op0=ALU.mult))
                    vflush()
                    vm(v.tensor_tensor(out=sp0[:, :], in0=murow[:, :], in1=murow[:, :], op=ALU.mult))
                    vflush()
                    v.tensor_tensor(out=varow[:, :], in0=varow[:, :], in1=sp0[:, :],
                                    op=ALU.subtract).then_inc(g("v_mu"), 1)
                    v.wait_ge(g("a_rw"), k + 1)
                    v.tensor_tensor(out=nmrow[:, :], in0=murow[:, :], in1=rstdrow[:, :],
                                    op=ALU.mult).then_inc(g("v_nm"), 1)
                    for q in range(4):
                        pair = (2 * q, 2 * q + 1)
                        for cc in pair:
                            v.wait_ge(g("s_bcr"), 32 * (k + 1))
                            v.wait_ge(g(f"s_yw{cc % 2}"), 16 * max(0, 4 * k + cc // 2))
                            vm(v.tensor_tensor(out=lnB1[:, (cc % 2) * T:(cc % 2) * T + T],
                                               in0=y_sb[:, cc * T:(cc + 1) * T],
                                               in1=rstd_bc[:, :], op=ALU.mult))
                        vflush()
                        for cc in pair:
                            vm(v.tensor_tensor(out=cacc[:, (cc % 2) * T:(cc % 2) * T + T],
                                               in0=lnB1[:, (cc % 2) * T:(cc % 2) * T + T],
                                               in1=nm_bc[:, :], op=ALU.subtract))
                        vflush()
                        for cc in pair:
                            vm(v.tensor_scalar(out=lnB1[:, (cc % 2) * T:(cc % 2) * T + T],
                                               in0=cacc[:, (cc % 2) * T:(cc % 2) * T + T],
                                               scalar1=lngv[:, cc:cc + 1], scalar2=lnbv[:, cc:cc + 1],
                                               op0=ALU.mult, op1=ALU.add))
                        vflush()
                        for cc in pair:
                            v.wait_ge(g("a_z"), 8 * k + cc + 1)
                            v.tensor_tensor(out=ygc[:, (cc % 2) * T:(cc % 2) * T + T],
                                            in0=lnB1[:, (cc % 2) * T:(cc % 2) * T + T],
                                            in1=sz16[:, cc * T:(cc + 1) * T],
                                            op=ALU.mult).then_inc(g("v_yg"), 1)

            @blk1.scalar
            def _(a):
                CA = [0]

                def am(inst):
                    inst.then_inc(g("c_a"), 1)
                    CA[0] += 1

                def aflush():
                    a.wait_ge(g("c_a"), CA[0])

                a.wait_ge(g("o_xs"), 1)
                a.wait_ge(g("s_wp"), 16 * 9)
                a.activation(out=spt[:, 0:CB * NS], in_=Av[:, :], func=AF.Exp).then_inc(g("a_rs"), 1)

                def rstds(k):
                    for t in range(TT4):
                        tg = k * TT4 + t
                        a.wait_ge(g("v_st"), 2 * tg + 2)
                        a.wait_ge(g("v_xn"), max(0, tg - 1))
                        a.wait_ge(g("a_rs"), tg + 1)
                        a.activation(out=lnt[:, :], in_=mv[:, 2 * (tg % 2) + 1:2 * (tg % 2) + 2],
                                     func=AF.Ln, bias=cbias[:, 0:1]).then_inc(g("a_rs1"), 1)
                        a.wait_ge(g("a_rs1"), tg + 1)
                        a.activation(out=rstd1[:, (tg % 2):(tg % 2) + 1], in_=lnt[:, :], func=AF.Exp,
                                     scale=-0.5).then_inc(g("a_rs"), 1)

                rstds(0)
                a.wait_ge(g("su"), 10)
                for k in range(K1):
                    for cb in range(CB):
                        a.wait_ge(g("v_cv"), 8 * k + cb + 1)
                        if k > 0:
                            a.wait_ge(g("v_u"), 8 * (k - 1) + cb + 1)
                            a.wait_ge(g("p_xp"), k)
                            a.wait_ge(g("v_ye"), 16 * (k - 1) + cb + 1)
                        a.activation(out=xb16[:, cb * T:(cb + 1) * T],
                                     in_=cacc[:, (cb % 2) * T:(cb % 2) * T + T],
                                     func=AF.Silu, bias=cbv[:, cb:cb + 1]).then_inc(g("a_xb"), 1)
                    for zi in range(CB):
                        a.wait_ge(g("p_ip"), 16 * k + 8 + zi + 1)
                        if k > 0:
                            a.wait_ge(g("v_yg"), 8 * (k - 1) + zi + 1)
                        a.activation(out=sz16[:, zi * T:(zi + 1) * T], in_=pz[zi % 2][:, :],
                                     func=AF.Silu).then_inc(g("a_z"), 1)
                    for q in range(4):
                        for cc in (2 * q, 2 * q + 1):
                            ss = (cc % 2) * T
                            a.wait_ge(g("v_t1"), 8 * k + cc + 1)
                            a.wait_ge(g("a_sp"), max(0, 8 * k + cc - 1))
                            am(a.activation(out=spl[:, ss:ss + T], in_=spe[:, ss:ss + T],
                                            func=AF.Exp, scale=-1.0))
                        aflush()
                        for cc in (2 * q, 2 * q + 1):
                            ss = (cc % 2) * T
                            a.wait_ge(g("v_dt"), max(0, 8 * k + cc - 1))
                            a.activation(out=spe[:, ss:ss + T], in_=spl[:, ss:ss + T],
                                         func=AF.Ln, bias=cbias[:, 1:2]).then_inc(g("a_sp"), 1)
                    for gg in range(2):
                        for cb in range(CB):
                            for nn in range(8):
                                i = 128 * k + 64 * gg + 8 * cb + nn
                                n = 8 * gg + nn
                                a.wait_ge(g("v_dt"), 8 * k + cb + 1)
                                a.wait_ge(g("v_ts"), max(0, i - 1))
                                a.activation(out=dA2[:, (i % 2) * T:(i % 2) * T + T],
                                             in_=dt_sb[:, cb * T:(cb + 1) * T], func=AF.Exp,
                                             scale=Av[:, cb * NS + n:cb * NS + n + 1]).then_inc(g("a_dA"), 1)
                    for cb in range(CB):
                        a.wait_ge(g("v_ye"), 16 * k + 8 + cb + 1)
                        a.wait_ge(g("p_st"), max(0, 8 * k + cb))
                        a.activation(out=y2st16[:, :],
                                     in_=y_sb[:, cb * T:(cb + 1) * T],
                                     func=AF.Square).then_inc(g("a_y2"), 1)
                    if k + 1 < K1:
                        rstds(k + 1)
                    a.wait_ge(g("v_mu"), k + 1)
                    if k > 0:
                        a.wait_ge(g("s_sg"), SG_CNT(k - 1, 5))
                    a.activation(out=sp0[:, :], in_=varow[:, :], func=AF.Ln,
                                 bias=cbias[0:1, 0:1]).then_inc(g("a_rw1"), 1)
                    a.wait_ge(g("a_rw1"), k + 1)
                    a.activation(out=rstdrow[:, :], in_=sp0[:, :], func=AF.Exp,
                                 scale=-0.5).then_inc(g("a_rw"), 1)

            @blk1.gpsimd
            def _(p):
                p.iota(out=iotai[:, :], pattern=[[1, 128]], base=0, channel_multiplier=-1)
                p.memset(ones16c[:, :], 1.0)
                p.memset(onesbfc[:, :], 1.0)
                p.memset(onesf32c[:, :], 1.0)
                p.memset(onesf32r[:, :], 1.0)
                p.memset(cbias[:, 0:1], EPS_LN)
                p.memset(cbias[:, 1:2], 1.0)
                p.memset(cbias[:, 2:3], EPS_Q)
                p.iota(out=flagv[:, :], pattern=[[1, 64]], base=MAGIC,
                       channel_multiplier=0).then_inc(g("o_xs"), 1)
                for k in range(K1):
                    for gg in range(2):
                        for cb in range(CB):
                            for nn in range(8):
                                i = 128 * k + 64 * gg + 8 * cb + nn
                                p.wait_ge(g("v_ts"), i + 1)
                                p.wait_ge(g("s_bcc"), 16 * (2 * k + gg + 1))
                                p.wait_ge(g("p_ya"), max(0, i - 1))
                                p.tensor_tensor(out=tmp2[:, (i % 2) * T:(i % 2) * T + T],
                                                in0=h_g8[:, nn * T:nn * T + T],
                                                in1=Cp_g[:, nn * T:nn * T + T],
                                                op=ALU.mult).then_inc(g("o_xs"), 1)

            @blk1.tensor
            def _(te):
                te.wait_ge(g("c_v"), 5)
                te.wait_ge(g("o_xs"), 1)

                def transposes(k):
                    for t in range(TT4):
                        tg = k * TT4 + t
                        te.wait_ge(g("v_xn"), tg + 1)
                        te.wait_ge(g("v_te"), tg)
                        for db in range(CB):
                            mm = te.transpose(out=pbt[:, db * 128:(db + 1) * 128],
                                              in_=xn16[:, (tg % 2) * 1024 + db * 128:(tg % 2) * 1024 + (db + 1) * 128],
                                              identity=id16[:, :])
                        mm.then_inc(g("p_tp"), 1)

                def in_proj_part(k, mis):
                    te.wait_ge(g("su"), 11)
                    te.wait_ge(g("v_te"), 4 * (k + 1))
                    for mi in mis:
                        if mi < 8:
                            te.wait_ge(g("v_cv"), max(0, 8 * k + mi - 1))
                            bank = pxb[mi % 2]
                        else:
                            te.wait_ge(g("a_z"), max(0, 8 * k + (mi - 8) - 1))
                            te.wait_ge(g("v_se"), k)
                            bank = pz[mi % 2]
                        for kk in range(CB):
                            mm = te.matmul(out=bank[:, :],
                                           lhsT=wipT16[:, kk * 2048 + mi * 128:kk * 2048 + (mi + 1) * 128],
                                           rhs=xn_fm[:, kk * T:(kk + 1) * T],
                                           start=(kk == 0), stop=(kk == CB - 1))
                        mm.then_inc(g("p_ip"), 1)

                transposes(0)
                for k in range(K1):
                    in_proj_part(k, range(16) if k == 0 else range(2, 16))
                    te.wait_ge(g("v_pv"), k)
                    te.wait_ge(g("v_se"), k)
                    for cb in range(CB):
                        te.wait_ge(g("a_xb"), 8 * k + cb + 1)
                        mm = te.matmul(out=pb4[0:33, :],
                                       lhsT=xpwT16[:, cb * 33:(cb + 1) * 33],
                                       rhs=xb16[:, cb * T:(cb + 1) * T],
                                       start=(cb == 0), stop=(cb == CB - 1))
                    mm.then_inc(g("p_xp"), 1)
                    for gg in range(2):
                        for cb in range(CB):
                            for nn in range(8):
                                i = 128 * k + 64 * gg + 8 * cb + nn
                                te.wait_ge(g("o_xs"), i + 2)
                                if nn == 0:
                                    te.wait_ge(g("v_ye"), max(0, 16 * k + 8 * gg + cb - 1))
                                te.matmul(out=pby[cb % 2][:, :], lhsT=idbf[:, :],
                                          rhs=tmp2[:, (i % 2) * T:(i % 2) * T + T],
                                          start=(nn == 0), stop=(nn == 7)).then_inc(g("p_ya"), 1)
                    for cb in range(CB):
                        te.wait_ge(g("v_yf"), 8 * k + cb + 1)
                        te.wait_ge(g("a_y2"), 8 * k + cb + 1)
                        if cb == 0:
                            te.wait_ge(g("v_pv"), k + 1)
                            te.wait_ge(g("a_z"), 8 * (k + 1))
                            te.wait_ge(g("v_se"), k)
                        te.matmul(out=pb4[0:1, :], lhsT=onesbfc[:, :],
                                  rhs=yst16[:, :],
                                  start=(cb == 0), stop=(cb == CB - 1))
                        te.matmul(out=pb2[0:1, :], lhsT=onesbfc[:, :],
                                  rhs=y2st16[:, :],
                                  start=(cb == 0), stop=(cb == CB - 1)).then_inc(g("p_st"), 1)
                    if k + 1 < K1:
                        transposes(k + 1)
                        in_proj_part(k + 1, range(2))

        st2 = contextlib.ExitStack()
        with st2:
            owT16 = st2.enter_context(nc.sbuf_tensor("owT16", [128, 16 * D], F16))
            wq16 = st2.enter_context(nc.sbuf_tensor("wq16", [128, CB * D], F16))
            wqT16 = st2.enter_context(nc.sbuf_tensor("wqT16", [128, CB * D], F16))
            wld2 = st2.enter_context(nc.sbuf_tensor("wld2", [128, 2 * D], F32))
            abscols = st2.enter_context(nc.sbuf_tensor("abscols", [128, 9], F32))
            x_tm2 = st2.enter_context(nc.sbuf_tensor("x_tm2", [128, 4 * D], F32))
            bn62 = st2.enter_context(nc.sbuf_tensor("bn62", [128, 12], F32))
            mv2 = st2.enter_context(nc.sbuf_tensor("mv2", [128, 2 * 2], F32))
            rstd2 = st2.enter_context(nc.sbuf_tensor("rstd2", [128, 2], F32))
            lnt2 = st2.enter_context(nc.sbuf_tensor("lnt2", [128, 2], F32))
            xn2 = st2.enter_context(nc.sbuf_tensor("xn2", [128, 2 * D], F32))
            asc4 = st2.enter_context(nc.sbuf_tensor("asc4", [128, 4], F32))
            arec = st2.enter_context(nc.sbuf_tensor("arec", [128, 2], F32))
            zq2 = st2.enter_context(nc.sbuf_tensor("zq2", [128, 2 * D], F32))
            aqi2 = st2.enter_context(nc.sbuf_tensor("aqi2", [128, 2 * D], I32))
            aq_tm = st2.enter_context(nc.sbuf_tensor("aq_tm", [128, 4 * D], F16))
            aq_fm = st2.enter_context(nc.sbuf_tensor("aq_fm", [128, CB * T], F16))
            ascrow = st2.enter_context(nc.sbuf_tensor("ascrow", [1, T], F32))
            scrow = st2.enter_context(nc.sbuf_tensor("scrow", [1, T], F32))
            sc_bc = st2.enter_context(nc.sbuf_tensor("sc_bc", [128, T], F32))
            gbuf = st2.enter_context(nc.sbuf_tensor("gbuf", [128, 2 * T], F32))
            g_ffn = st2.enter_context(nc.sbuf_tensor("g_ffn", [128, CB * T], F32))
            yg_in = st2.enter_context(nc.sbuf_tensor("yg_in", [128, 16 * T], F16))
            sg_sb = st2.enter_context(nc.sbuf_tensor("sg_sb", [128, CB * T], F32))
            outt = st2.enter_context(nc.sbuf_tensor("outt", [128, 2 * D], F32))
            pc0 = st2.enter_context(nc.psum_tensor("pc0", [128, T], F32))
            pc1 = st2.enter_context(nc.psum_tensor("pc1", [128, T], F32))
            pc2 = st2.enter_context(nc.psum_tensor("pc2", [128, T], F32))
            pc3 = st2.enter_context(nc.psum_tensor("pc3", [128, T], F32))
            pct = st2.enter_context(nc.psum_tensor("pct", [128, 1024], F16))
            pcg = st2.enter_context(nc.psum_tensor("pcg", [128, 1024], F32))
            pcs = st2.enter_context(nc.psum_tensor("pcs", [128, 4], F32))
            blk2 = st2.enter_context(nc.Block())
            pf = [pc0, pc1]
            po = [pc2, pc3]

            @blk2.sync
            def _(sy):
                pid = sy.partition_id()
                parv = sy.scalar_reg_alu(ALU.bitwise_and, pid, 1)
                with sy.register("c2") as rc, sy.register("rr2") as rreg:
                    def poll(par, rd):
                        sy.reg_mov(rc, 1)
                        with sy.While(rc):
                            sy.load(rreg, sh_flag.ap()[1 - par:2 - par, rd:rd + 1])
                            sy.reg_alu(rc, rreg, MAGIC + rd, ALU.not_equal)

                    for par in sy.Switch(parv, 2):
                        nn = 0
                        for kk in range(CB):
                            if kk >= 2:
                                sy.wait_ge(g("wq_v"), kk - 1)
                            sy.dma_start(out=wld2[:, (kk % 2) * D:(kk % 2) * D + D],
                                         in_=ffnw_in[kk * 128:(kk + 1) * 128, :]).then_inc(g(f"s_w2{kk % 2}"), 16)
                        for kk in range(CB):
                            sy.wait_ge(g("wq_v"), 12 + max(0, kk - 1))
                            sy.dma_start(out=wld2[:, (kk % 2) * D:(kk % 2) * D + D],
                                         in_=ffnw_in[kk * 128:(kk + 1) * 128, :]).then_inc(g(f"s_w2{kk % 2}"), 16)
                        for kk in range(16):
                            sy.wait_ge(g("wq_v"), 20 if kk < 2 else 27 + kk)
                            sy.dma_start(out=wld2[:, (kk % 2) * D:(kk % 2) * D + D],
                                         in_=owT_in[kk * 128:(kk + 1) * 128, :]).then_inc(g(f"s_w2{kk % 2}"), 16)
                        sy.wait_ge(g("s_yw0"), 16 * 4 * K1)
                        sy.wait_ge(g("s_yw1"), 16 * 4 * K1)
                        for j in range(K2):
                            kk_g = 8 * par + j
                            sy.wait_ge(g("v2o"), 4 * j)
                            sy.dma_start(
                                out=x_tm2[:, :],
                                in_=x_in[par * TOK2 + j * T: par * TOK2 + (j + 1) * T, :]
                                .rearrange("(a p) d -> p a d", p=128)).then_inc(g("s_x2"), 16)
                            poll(par, R_YG(kk_g))
                            sy.wait_ge(g("p2o"), 8 * j)
                            sy.dma_start(
                                out=yg_in[:, 0:CB * T],
                                in_=sh_yg.ap().rearrange("a p (cb l) -> a p cb l", cb=CB)
                                [par, :, :, kk_g * T:(kk_g + 1) * T]).then_inc(g("s_yr"), 16)
                            sy.dma_start(
                                out=yg_in[:, CB * T:16 * T],
                                in_=sh_yg.ap().rearrange("a p (cb l) -> a p cb l", cb=CB)
                                [1 - par, :, :, kk_g * T:(kk_g + 1) * T]).then_inc(g("s_yr"), 16)
                            sy.wait_ge(g("v2x"), 4 * (j + 1))
                            if j > 0:
                                sy.wait_ge(g("s_a2"), 16 * 3 * j)
                                sy.wait_ge(g("v2q"), j)
                            sy.dma_start(out=d_asc.ap(), in_=asc4[:, :]).then_inc(g("s_a2"), 16)
                            sy.wait_ge(g("s_a2"), 16 * (3 * j + 1))
                            sy.dma_start(out=ascrow[:, :],
                                         in_=d_asc.ap().rearrange("p t -> t p").unsqueeze(0),
                                         allow_slow_non_contiguous=True).then_inc(g("s_a2"), 16)
                            sy.wait_ge(g("v2q"), j + 1)
                            if j > 0:
                                sy.wait_ge(g("s_b2"), 16 * j)
                            sy.dma_start(out=d_sc.ap(), in_=scrow[:, :]).then_inc(g("s_a2"), 16)
                            sy.wait_ge(g("s_a2"), 16 * (3 * j + 3))
                            sy.wait_ge(g("v2g"), 8 * j)
                            sy.dma_start(out=sc_bc[:, :],
                                         in_=d_sc.ap()[0:1, :].partition_broadcast(128)).then_inc(g("s_b2"), 16)
                            for t in range(TT4):
                                sy.wait_ge(g("v2o"), 4 * j + t + 1)
                                sy.dma_start(out=o_out[j * T + t * 128: j * T + (t + 1) * 128, :],
                                             in_=outt[:, (t % 2) * D:(t % 2) * D + D]).then_inc(g(f"s_o{t % 2}"), 16)
                            if dbg and j == 0:
                                sy.wait_ge(g("a2g"), 8)
                                sy.dma_start(out=dbg_o["dbg_g"], in_=g_ffn[:, :]).then_inc(g("s_fl"), 16)
                                sy.wait_ge(g("v2s"), 8)
                                sy.dma_start(out=dbg_o["dbg_sg"], in_=sg_sb[:, :]).then_inc(g("s_fl"), 16)
                                sy.wait_ge(g("v2e"), 4)
                                sy.dma_start(out=dbg_o["dbg_aq"], in_=aq_fm[:, :]).then_inc(g("s_fl"), 16)
                        sy.wait_ge(g("s_o0"), 16 * 2 * K2)
                        sy.wait_ge(g("s_o1"), 16 * 2 * K2)

            @blk2.vector
            def _(v):
                C2 = [0]

                def vm(inst):
                    inst.then_inc(g("c_v2"), 1)
                    C2[0] += 1

                def vflush():
                    v.wait_ge(g("c_v2"), C2[0])

                for kk in range(CB):
                    v.wait_ge(g(f"s_w2{kk % 2}"), 16 * (kk // 2 + 1))
                    v.tensor_reduce(out=abscols[:, kk:kk + 1],
                                    in_=wld2[:, (kk % 2) * D:(kk % 2) * D + D],
                                    axis=AX.X, op=ALU.add, apply_absolute_value=True).then_inc(g("wq_v"), 1)
                v.wait_ge(g("wq_v"), 8)
                v.tensor_reduce(out=abscols[:, 8:9], in_=abscols[:, 0:8], axis=AX.X,
                                op=ALU.add).then_inc(g("wq_v"), 1)
                v.wait_ge(g("wq_p"), 1)
                v.tensor_copy(out=wscpre[:, :], in_=pcs[0:1, 0:1]).then_inc(g("wq_v"), 1)
                v.wait_ge(g("wq_p"), 2)
                v.tensor_copy(out=wscv[:, :], in_=pcs[:, 1:2]).then_inc(g("wq_v"), 1)
                v.wait_ge(g("wq_v"), 11)
                v.reciprocal(out=wrecv[:, :], in_=wscv[:, :]).then_inc(g("wq_v"), 1)
                v.wait_ge(g("wq_v"), 12)
                for qq in range(4):
                    ka, kb = 2 * qq, 2 * qq + 1
                    for kk in (ka, kb):
                        v.wait_ge(g(f"s_w2{kk % 2}"), 16 * (4 + kk // 2 + 1))
                        vm(v.tensor_scalar(out=zq2[:, (kk % 2) * D:(kk % 2) * D + D],
                                           in0=wld2[:, (kk % 2) * D:(kk % 2) * D + D],
                                           scalar1=wrecv[:, 0:1], scalar2=None, op0=ALU.mult))
                    vflush()
                    for kk in (ka, kb):
                        sl = slice((kk % 2) * D, (kk % 2) * D + D)
                        wsl = slice(kk * D, (kk + 1) * D)
                        vm(v.tensor_copy(out=aqi2[:, sl], in_=zq2[:, sl]))
                        vflush()
                        vm(v.tensor_copy(out=wq16[:, wsl], in_=aqi2[:, sl]))
                        vflush()
                        vm(v.tensor_tensor(out=xn2[:, sl], in0=zq2[:, sl],
                                           in1=wq16[:, wsl], op=ALU.subtract))
                        vflush()
                        vm(v.tensor_scalar(out=zq2[:, sl], in0=xn2[:, sl],
                                           scalar1=0.5, scalar2=None, op0=ALU.is_gt))
                        vm(v.tensor_scalar(out=xn2[:, sl], in0=xn2[:, sl],
                                           scalar1=-0.5, scalar2=None, op0=ALU.is_lt))
                        vflush()
                        vm(v.tensor_tensor(out=wq16[:, wsl], in0=wq16[:, wsl],
                                           in1=zq2[:, sl], op=ALU.add))
                        vflush()
                        vm(v.tensor_tensor(out=wq16[:, wsl], in0=wq16[:, wsl],
                                           in1=xn2[:, sl], op=ALU.subtract))
                        vflush()
                        v.tensor_scalar(out=wq16[:, wsl], in0=wq16[:, wsl],
                                        scalar1=1.0, scalar2=-1.0,
                                        op0=ALU.min, op1=ALU.max).then_inc(g("wq_v"), 1)
                for kk in range(CB):
                    v.wait_ge(g("wq_p"), 3 + kk)
                    v.tensor_copy(
                        out=wqT16.rearrange("p (bb d) -> p bb d", bb=CB)[:, :, kk * 128:(kk + 1) * 128],
                        in_=pct.rearrange("p (bb d) -> p bb d", bb=CB)[:, :, :]).then_inc(g("wq_v"), 1)
                for kk in range(16):
                    v.wait_ge(g(f"s_w2{kk % 2}"), 16 * (8 + kk // 2 + 1))
                    v.tensor_copy(out=owT16[:, kk * D:(kk + 1) * D],
                                  in_=wld2[:, (kk % 2) * D:(kk % 2) * D + D]).then_inc(g("wq_v"), 1)

                v.wait_ge(g("wq_v"), 20)
                for j in range(K2):
                    for qq in range(2):
                        ta, tb = 2 * qq, 2 * qq + 1
                        for t in (ta, tb):
                            v.wait_ge(g("s_x2"), 16 * (j + 1))
                            vm(v.bn_stats(out=bn62[:, 0:6], in_=x_tm2[:, t * D:t * D + 512]))
                            v.bn_stats(out=bn62[:, 6:12],
                                       in_=x_tm2[:, t * D + 512:(t + 1) * D]).then_inc(g("v2a"), 1)
                            v.wait_ge(g("v2a"), 2 * (4 * j + t) + 1)
                            vflush()
                            v.wait_ge(g("v2x"), max(0, 4 * j + t - 1))
                            v.wait_ge(g("a2r"), max(0, 4 * j + t - 1))
                            v.bn_aggr(out=mv2[:, 2 * (t % 2):2 * (t % 2) + 2],
                                      in_=bn62[:, :]).then_inc(g("v2a"), 1)
                        for t in (ta, tb):
                            v.wait_ge(g("a2r"), 4 * j + t + 1)
                            v.wait_ge(g("p2t"), max(0, 4 * j + t - 1))
                            vm(v.tensor_scalar(out=xn2[:, (t % 2) * D:(t % 2) * D + D],
                                               in0=x_tm2[:, t * D:(t + 1) * D],
                                               scalar1=mv2[:, 2 * (t % 2):2 * (t % 2) + 1],
                                               scalar2=rstd2[:, (t % 2):(t % 2) + 1],
                                               op0=ALU.subtract, op1=ALU.mult))
                        vflush()
                        for t in (ta, tb):
                            vm(v.reduce_max(out=asc4[:, t:t + 1], in_=xn2[:, (t % 2) * D:(t % 2) * D + D],
                                            axis=AX.X, apply_absolute_value=True))
                        vflush()
                        for t in (ta, tb):
                            vm(v.tensor_scalar(out=asc4[:, t:t + 1], in0=asc4[:, t:t + 1],
                                               scalar1=EPS_Q, scalar2=None, op0=ALU.add))
                        vflush()
                        for t in (ta, tb):
                            vm(v.reciprocal(out=arec[:, (t % 2):(t % 2) + 1], in_=asc4[:, t:t + 1]))
                        vflush()
                        for t in (ta, tb):
                            vm(v.tensor_scalar(out=zq2[:, (t % 2) * D:(t % 2) * D + D],
                                               in0=xn2[:, (t % 2) * D:(t % 2) * D + D],
                                               scalar1=arec[:, (t % 2):(t % 2) + 1], scalar2=127.0,
                                               op0=ALU.mult, op1=ALU.mult))
                        vflush()
                        for t in (ta, tb):
                            sl = slice((t % 2) * D, (t % 2) * D + D)
                            asl = slice(t * D, (t + 1) * D)
                            vm(v.tensor_copy(out=aqi2[:, sl], in_=zq2[:, sl]))
                            vflush()
                            vm(v.tensor_copy(out=aq_tm[:, asl], in_=aqi2[:, sl]))
                            vflush()
                            vm(v.tensor_tensor(out=xn2[:, sl], in0=zq2[:, sl],
                                               in1=aq_tm[:, asl], op=ALU.subtract))
                            vflush()
                            vm(v.tensor_scalar(out=zq2[:, sl], in0=xn2[:, sl],
                                               scalar1=0.5, scalar2=None, op0=ALU.is_gt))
                            vm(v.tensor_scalar(out=xn2[:, sl], in0=xn2[:, sl],
                                               scalar1=-0.5, scalar2=None, op0=ALU.is_lt))
                            vflush()
                            vm(v.tensor_tensor(out=aq_tm[:, asl], in0=aq_tm[:, asl],
                                               in1=zq2[:, sl], op=ALU.add))
                            vflush()
                            v.tensor_tensor(out=aq_tm[:, asl], in0=aq_tm[:, asl],
                                            in1=xn2[:, sl], op=ALU.subtract).then_inc(g("v2x"), 1)
                        for t in (ta, tb):
                            v.wait_ge(g("p2t"), 4 * j + t + 1)
                            v.wait_ge(g("p2f"), 8 * j)
                            v.tensor_copy(
                                out=aq_fm.rearrange("p (db l) -> p db l", db=CB)[:, :, t * 128:(t + 1) * 128],
                                in_=pct.rearrange("p (db l) -> p db l", db=CB)[:, :, :]).then_inc(g("v2e"), 1)
                    v.wait_ge(g("s_a2"), 16 * (3 * j + 2))
                    v.wait_ge(g("wq_a"), 1)
                    v.tensor_scalar(out=scrow[:, :], in0=ascrow[:, :], scalar1=wsc[0:1, 0:1],
                                    scalar2=1.0 / 127.0, op0=ALU.mult, op1=ALU.mult).then_inc(g("v2q"), 1)
                    for m in range(CB):
                        v.wait_ge(g("p2f"), 8 * j + m + 1)
                        v.wait_ge(g("s_b2"), 16 * (j + 1))
                        v.wait_ge(g("a2g"), max(0, 8 * j + m - 1))
                        v.tensor_tensor(out=gbuf[:, (m % 2) * T:(m % 2) * T + T],
                                        in0=pf[m % 2][:, :], in1=sc_bc[:, :],
                                        op=ALU.mult).then_inc(g("v2g"), 1)
                    for m in range(CB):
                        v.wait_ge(g("p2o"), 8 * j + m + 1)
                        v.wait_ge(g("a2g"), 8 * j + m + 1)
                        v.tensor_tensor(out=sg_sb[:, m * T:(m + 1) * T],
                                        in0=po[m % 2][:, :], in1=g_ffn[:, m * T:(m + 1) * T],
                                        op=ALU.mult).then_inc(g("v2s"), 1)
                    for t in range(TT4):
                        v.wait_ge(g("p2g"), 4 * j + t + 1)
                        v.wait_ge(g(f"s_o{t % 2}"), 16 * max(0, 2 * j + t // 2))
                        v.tensor_tensor(out=outt[:, (t % 2) * D:(t % 2) * D + D],
                                        in0=x_tm2[:, t * D:(t + 1) * D], in1=pcg[:, :],
                                        op=ALU.add).then_inc(g("v2o"), 1)

            @blk2.scalar
            def _(a):
                a.wait_ge(g("wq_v"), 10)
                a.activation(out=wsc[:, :], in_=wscpre[:, :], func=AF.Copy,
                             scale=INVMEAN, bias=EPS_Q).then_inc(g("wq_a"), 1)
                for j in range(K2):
                    for t in range(TT4):
                        a.wait_ge(g("v2a"), 2 * (4 * j + t) + 2)
                        a.wait_ge(g("v2x"), max(0, 4 * j + t - 1))
                        a.activation(out=lnt2[:, (t % 2):(t % 2) + 1],
                                     in_=mv2[:, 2 * (t % 2) + 1:2 * (t % 2) + 2], func=AF.Ln,
                                     bias=cbias[:, 0:1]).then_inc(g("a_rs1"), 1)
                        a.wait_ge(g("a_rs1"), 4 * K1 + 4 * j + t + 1)
                        a.activation(out=rstd2[:, (t % 2):(t % 2) + 1],
                                     in_=lnt2[:, (t % 2):(t % 2) + 1], func=AF.Exp,
                                     scale=-0.5).then_inc(g("a2r"), 1)
                    for m in range(CB):
                        a.wait_ge(g("v2g"), 8 * j + m + 1)
                        a.activation(out=g_ffn[:, m * T:(m + 1) * T],
                                     in_=gbuf[:, (m % 2) * T:(m % 2) * T + T],
                                     func=AF.Silu).then_inc(g("a2g"), 1)

            @blk2.tensor
            def _(te):
                te.wait_ge(g("wq_v"), 9)
                te.matmul(out=pcs[0:1, 0:1], lhsT=onesf32c[:, :], rhs=abscols[:, 8:9],
                          start=True, stop=True).then_inc(g("wq_p"), 1)
                te.wait_ge(g("wq_a"), 1)
                te.matmul(out=pcs[:, 1:2], lhsT=onesf32r[:, :], rhs=wsc[:, :],
                          start=True, stop=True).then_inc(g("wq_p"), 1)
                for kk in range(CB):
                    te.wait_ge(g("wq_v"), 13 + kk)
                    te.wait_ge(g("wq_v"), 20 + max(0, kk))
                    for bb in range(CB):
                        mm = te.transpose(out=pct[:, bb * 128:(bb + 1) * 128],
                                          in_=wq16[:, kk * D + bb * 128:kk * D + (bb + 1) * 128],
                                          identity=id16[:, :])
                    mm.then_inc(g("wq_p"), 1)
                for j in range(K2):
                    for t in range(TT4):
                        te.wait_ge(g("v2x"), 4 * j + t + 1)
                        te.wait_ge(g("v2e"), 4 * j + t)
                        for db in range(CB):
                            mm = te.transpose(out=pct[:, db * 128:(db + 1) * 128],
                                              in_=aq_tm[:, t * D + db * 128:t * D + (db + 1) * 128],
                                              identity=id16[:, :])
                        mm.then_inc(g("p2t"), 1)
                    te.wait_ge(g("wq_v"), 28)
                    te.wait_ge(g("v2e"), 4 * (j + 1))
                    for m in range(CB):
                        te.wait_ge(g("v2g"), max(0, 8 * j + m - 1))
                        for kk in range(CB):
                            mm = te.matmul(out=pf[m % 2][:, :],
                                           lhsT=wqT16[:, kk * D + m * 128:kk * D + (m + 1) * 128],
                                           rhs=aq_fm[:, kk * T:(kk + 1) * T],
                                           start=(kk == 0), stop=(kk == CB - 1))
                        mm.then_inc(g("p2f"), 1)
                    te.wait_ge(g("wq_v"), 44)
                    te.wait_ge(g("s_yr"), 32 * (j + 1))
                    for m in range(CB):
                        te.wait_ge(g("v2s"), max(0, 8 * j + m - 1))
                        for kk in range(16):
                            mm = te.matmul(out=po[m % 2][:, :],
                                           lhsT=owT16[:, kk * D + m * 128:kk * D + (m + 1) * 128],
                                           rhs=yg_in[:, kk * T:(kk + 1) * T],
                                           start=(kk == 0), stop=(kk == 15))
                        mm.then_inc(g("p2o"), 1)
                    for t in range(TT4):
                        te.wait_ge(g("v2s"), 8 * (j + 1))
                        te.wait_ge(g("v2o"), 4 * j + t)
                        for m in range(CB):
                            mm = te.transpose(out=pcg[:, m * 128:(m + 1) * 128],
                                              in_=sg_sb[:, m * T + t * 128:m * T + (t + 1) * 128],
                                              identity=idf32[:, :])
                        mm.then_inc(g("p2g"), 1)

    return nc


_NC_CACHE = {}


def kernel(L0_attn_x, L0_attn_in_proj_w, L0_attn_conv_w, L0_attn_conv_b,
           L0_attn_x_proj_w, L0_attn_A_log, L0_attn_D_param, L0_attn_dt_w,
           L0_attn_dt_b, L0_attn_out_w, L0_attn_ln_g, L0_attn_ln_b, L0_ffn_weight):
    f32 = np.float32
    x = np.asarray(L0_attn_x, f32)
    ipT = np.ascontiguousarray(np.asarray(L0_attn_in_proj_w, f32).T)
    xp = np.asarray(L0_attn_x_proj_w, f32)
    xp_re = np.concatenate([xp[1:17], xp[17:33], xp[0:1]], 0)
    alog = np.asarray(L0_attn_A_log, f32)
    cw = np.asarray(L0_attn_conv_w, f32)
    cbb = np.asarray(L0_attn_conv_b, f32)
    dp = np.asarray(L0_attn_D_param, f32)
    dtw = np.asarray(L0_attn_dt_w, f32)[:, 0]
    dtb = np.asarray(L0_attn_dt_b, f32)
    lng = np.asarray(L0_attn_ln_g, f32)
    lnb = np.asarray(L0_attn_ln_b, f32)
    owT = np.ascontiguousarray(np.asarray(L0_attn_out_w, f32).T)
    ffnw = np.ascontiguousarray(np.asarray(L0_ffn_weight, f32))

    def vec_layout(vv, half):
        return np.ascontiguousarray(vv[half * CL:(half + 1) * CL].reshape(CB, 128).T)

    in_maps = []
    for r in range(8):
        b, par = r // 2, r & 1
        h0 = par * CL
        wipT = np.ascontiguousarray(
            np.concatenate([ipT[:, h0:h0 + CL], ipT[:, DI + h0:DI + h0 + CL]], 1))
        xpwT = np.ascontiguousarray(
            xp_re[:, h0:h0 + CL].T.reshape(CB, 128, 33).transpose(1, 0, 2).reshape(128, CB * 33))
        al = np.ascontiguousarray(
            alog[h0:h0 + CL].reshape(CB, 128, NS).transpose(1, 0, 2).reshape(128, CB * NS))
        cwc = np.ascontiguousarray(
            cw[h0:h0 + CL].reshape(CB, 128, 4).transpose(1, 0, 2).reshape(128, CB * 4))
        owc = np.ascontiguousarray(
            np.concatenate([owT[h0:h0 + CL], owT[CL - h0:CL - h0 + CL]], 0)) if False else \
            np.ascontiguousarray(
            np.concatenate([owT[h0:h0 + CL], owT[(1 - par) * CL:(1 - par) * CL + CL]], 0))
        in_maps.append(dict(
            x_in=np.ascontiguousarray(x[b]),
            wipT_in=wipT, xpwT_in=xpwT, alog_in=al, cw_in=cwc,
            cb_in=vec_layout(cbb, par), dp_in=vec_layout(dp, par),
            dtw_in=vec_layout(dtw, par), dtb_in=vec_layout(dtb, par),
            lng_in=vec_layout(lng, par), lnb_in=vec_layout(lnb, par),
            owT_in=owc, ffnw_in=ffnw,
        ))

    nc = build_nc(dbg=False)
    res = run_bass_kernel_spmd(nc, in_maps, core_ids=list(range(8)))
    out = np.empty((4, L, D), f32)
    for r in range(8):
        b, par = r // 2, r & 1
        out[b, par * TOK2:(par + 1) * TOK2] = res.results[r]["o_out"]
    return out
